# version 05 (final)
# baseline (speedup 1.0000x reference)
import numpy as np
import ml_dtypes
import concourse.bass as bass
import concourse.mybir as mybir
from concourse.bass_utils import run_bass_kernel_spmd

F32 = mybir.dt.float32
BF16 = mybir.dt.bfloat16
ALU = mybir.AluOpType
ACT = mybir.ActivationFunctionType
AX = mybir.AxisListType.X

T = 2048
D = 1024
NCH = D // 128
C23 = 12582912.0
C52 = 6755399441055744.0
LN_EPS = 1e-5

N_SUB = 8
ATTNB_CUT = 0
ATTNB_QUADS = 8
ENGS = ["tensor", "vector", "scalar", "sync", "gpsimd"]


class B:
    def __init__(self):
        self.nc = bass.Bass("TRN2", target_bir_lowering=False)
        self._cm = []
        self.sems = {}
        self.cnts = {}
        self.t = {}

    def new_sem(self, name):
        cm = self.nc.semaphore(name)
        h = cm.__enter__()
        self._cm.append(cm)
        self.sems[name] = h
        self.cnts[name] = 0
        return h

    def inc(self, name, n=1):
        self.cnts[name] += n
        return self.cnts[name]

    def s(self, name):
        return self.sems[name]

    def sb(self, name, shape, dtype):
        ap = self.nc.alloc_sbuf_tensor(name, shape, dtype).ap()
        self.t[name] = ap
        return ap

    def ps_(self, name, shape, dtype=F32):
        ap = self.nc.alloc_psum_tensor(name, shape, dtype).ap()
        self.t[name] = ap
        return ap

    def dram(self, name, shape, dtype=F32, kind="ExternalInput"):
        ap = self.nc.dram_tensor(name, shape, dtype, kind=kind).ap()
        self.t[name] = ap
        return ap


class E:

    def __init__(self, b):
        self.b = b
        self.qs = {k: [] for k in ENGS}

    def on(self, eng):
        def deco(fn):
            self.qs[eng].append(fn)
            return fn
        return deco

    def emit(self, block):
        for eng in ENGS:
            fns = self.qs[eng]
            if not fns:
                continue

            def body(sec, fns=fns):
                for f in fns:
                    f(sec)
            getattr(block, eng)(body)


def make_consts():
    selh = np.zeros((128, 2), dtype=ml_dtypes.bfloat16)
    selh[0:64, 0] = 1.0
    selh[64:128, 1] = 1.0
    sel32 = np.zeros((128, 4), dtype=ml_dtypes.bfloat16)
    for h in range(4):
        sel32[32 * h:32 * h + 32, h] = 1.0
    pick32 = np.zeros((4, 512), dtype=ml_dtypes.bfloat16)
    for h in range(4):
        pick32[h, 128 * h:128 * h + 128] = 1.0
    selh2 = np.zeros((2, 128), dtype=np.float32)
    selh2[0, 0:64] = 1.0
    selh2[1, 64:128] = 1.0
    return {
        "x_ident": np.eye(128, dtype=np.float32),
        "x_onesm": np.ones((128, 128), dtype=np.float32),
        "x_triu": np.triu(np.ones((64, 64), dtype=np.float32)),
        "x_selh": selh,
        "x_fconst": np.array([[1e-5, 1e-6, 0.0, 0.0, 0.0, 0.0, 0.0, 0.0]],
                             dtype=np.float32),
        "x_sel32": sel32,
        "x_pick32": pick32,
        "x_selh2": selh2,
    }


def phase_load_x(b, block):
    X, RB1, ID = b.t["X"], b.t["RB1"], b.t["IDent"]
    PS = [b.t[f"PS{i}"] for i in range(8)]
    pe0, dv0 = b.cnts["pe"], b.cnts["dv"]
    tile_dma = []

    @block.sync
    def _(sync):
        for src, dst in [("x_ident", "IDent"), ("x_onesm", "ONESM"),
                         ("x_triu", "TRIU"), ("x_selh", "SELH"),
                         ("x_fconst", "FCONST"),
                         ("x_sel32", "SEL32"), ("x_pick32", "PICK32"),
                         ("x_selh2", "SELH2")]:
            sync.dma_start(out=b.t[dst][:, :], in_=b.t[src][:, :]).then_inc(
                b.s("dq"), 16)
            b.inc("dq", 16)
        for i in range(16):
            if i >= 2:
                sync.wait_ge(b.s("pe"), pe0 + (i - 2) + 1)
            sname = "dx%d" % (i % 2)
            sync.dma_start(
                out=RB1[:, (i % 8) * 1024:(i % 8) * 1024 + 1024],
                in_=b.t["x_in"][i * 128:(i + 1) * 128, :],
            ).then_inc(b.s(sname), 16)
            tile_dma.append((sname, b.inc(sname, 16)))

    @block.vector
    def _(vector):
        for i in range(16):
            vector.wait_ge(b.s("pe"), pe0 + i + 1)
            for c in range(NCH):
                ins = vector.tensor_copy(
                    X[:, c * T + i * 128:c * T + i * 128 + 128], PS[c][:, 0:128])
            ins.then_inc(b.s("dv"), 1)
            b.inc("dv", 1)
        vector.memset(b.t["DORA"][:, :], 1.0).then_inc(b.s("dv"), 1)
        vector.wait_ge(b.s("dv"), b.inc("dv", 1))
        vector.tensor_scalar(
            b.t["DORA"][:, :], b.t["DORA"][:, :], C52, C52, ALU.add, ALU.subtract)

    @block.tensor
    def _(tensor):
        tensor.wait_ge(b.s("dq"), 128)
        for i in range(16):
            tensor.wait_ge(b.s(tile_dma[i][0]), tile_dma[i][1])
            if i >= 1:
                tensor.wait_ge(b.s("dv"), dv0 + i)
            for c in range(NCH):
                mm = tensor.transpose(
                    PS[c][:, 0:128],
                    RB1[:, (i % 8) * 1024 + c * 128:(i % 8) * 1024 + c * 128 + 128],
                    ID[:, :])
            mm.then_inc(b.s("pe"), 1)
            b.inc("pe", 1)


def phase_ln(b, block, eps=LN_EPS):
    X, RB1, RB2, ONESM = b.t["X"], b.t["RB1"], b.t["RB2"], b.t["ONESM"]
    XQH, XQL = b.t["XQH"], b.t["XQL"]
    PS4, PS5, PS6, PS7 = b.t["PS4"], b.t["PS5"], b.t["PS6"], b.t["PS7"]
    pe0, dv0, sc0 = b.cnts["pe"], b.cnts["dv"], b.cnts["sc"]

    @block.scalar
    def _(scalar):
        for q in range(4):
            if q == 0:
                scalar.wait_ge(b.s("dq"), 128)
                scalar.wait_ge(b.s("dv"), dv0)
            if q >= 2:
                scalar.wait_ge(b.s("dv"), dv0 + 3 * (q - 2) + 3)
            for c in range(NCH):
                ins = scalar.activation(
                    RB1[:, (q % 2) * 4096 + c * 512:(q % 2) * 4096 + c * 512 + 512],
                    X[:, c * T + q * 512:c * T + q * 512 + 512], ACT.Square)
            ins.then_inc(b.s("sc"), 1)
            b.inc("sc", 1)
            scalar.wait_ge(b.s("dv"), dv0 + 3 * q + 1)
            scalar.activation(
                RB2[0:1, 512:1024], RB2[0:1, 512:1024],
                ACT.Sqrt, bias=eps, scale=1.0).then_inc(b.s("sc"), 1)
            b.inc("sc", 1)

    @block.tensor
    def _(tensor):
        for q in range(4):
            if q == 0:
                tensor.wait_ge(b.s("dv"), dv0)
            if q >= 1:
                tensor.wait_ge(b.s("dv"), dv0 + 3 * (q - 1) + 1)
            for c in range(NCH):
                tensor.matmul(
                    PS6[0:1, 0:512], ONESM[:, 0:1],
                    X[:, c * T + q * 512:c * T + q * 512 + 512],
                    start=(c == 0), stop=(c == NCH - 1))
            tensor.wait_ge(b.s("sc"), sc0 + 2 * q + 1)
            for c in range(NCH):
                mm = tensor.matmul(
                    PS7[0:1, 0:512], ONESM[:, 0:1],
                    RB1[:, (q % 2) * 4096 + c * 512:(q % 2) * 4096 + c * 512 + 512],
                    start=(c == 0), stop=(c == NCH - 1))
            mm.then_inc(b.s("pe"), 1)
            b.inc("pe", 1)
            tensor.wait_ge(b.s("dv"), dv0 + 3 * q + 2)
            if q >= 1:
                tensor.wait_ge(b.s("dv"), dv0 + 3 * (q - 1) + 3)
            tensor.matmul(PS4[:, 0:512], ONESM[0:1, 0:128], RB2[0:1, 0:512],
                          start=True, stop=True)
            mm = tensor.matmul(PS5[:, 0:512], ONESM[0:1, 0:128],
                               RB2[0:1, 1024:1536], start=True, stop=True)
            mm.then_inc(b.s("pe"), 1)
            b.inc("pe", 1)

    @block.vector
    def _(vector):
        for q in range(4):
            vector.wait_ge(b.s("pe"), pe0 + 2 * q + 1)
            vector.tensor_single_scalar(
                RB2[0:1, 0:512], PS6[0:1, 0:512], 1.0 / D, ALU.mult)
            vector.drain()
            vector.tensor_tensor(
                RB2[0:1, 1536:2048], RB2[0:1, 0:512], RB2[0:1, 0:512], ALU.mult)
            vector.drain()
            vector.scalar_tensor_tensor(
                RB2[0:1, 512:1024], PS7[0:1, 0:512], 1.0 / D,
                RB2[0:1, 1536:2048], ALU.mult, ALU.subtract).then_inc(b.s("dv"), 1)
            b.inc("dv", 1)
            vector.wait_ge(b.s("sc"), sc0 + 2 * q + 2)
            vector.reciprocal(
                RB2[0:1, 1024:1536], RB2[0:1, 512:1024]).then_inc(b.s("dv"), 1)
            b.inc("dv", 1)
            vector.wait_ge(b.s("pe"), pe0 + 2 * q + 2)
            sls = []
            for c in range(NCH):
                sls.append((
                    slice(c * T + q * 512, c * T + q * 512 + 512),
                    slice((q % 2) * 4096 + c * 512,
                          (q % 2) * 4096 + c * 512 + 512)))
            for sl, xl in sls:
                vector.tensor_tensor(RB1[:, xl], X[:, sl], PS4[:, 0:512],
                                     ALU.subtract)
            vector.drain()
            for sl, xl in sls:
                vector.tensor_tensor(RB1[:, xl], RB1[:, xl], PS5[:, 0:512],
                                     ALU.mult)
            vector.drain()
            for sl, xl in sls:
                vector.tensor_copy(XQH[:, sl], RB1[:, xl])
            vector.drain()
            for sl, xl in sls:
                ins = vector.tensor_tensor(XQL[:, sl], RB1[:, xl],
                                           XQH[:, sl], ALU.subtract)
            ins.then_inc(b.s("dv"), 1)
            b.inc("dv", 1)
    return b.cnts["dv"]


def phase_store_x(b, block):
    X, RB1, ID = b.t["X"], b.t["RB1"], b.t["IDent"]
    out = b.t["out"]
    PS = [b.t[f"PS{i}"] for i in range(8)]
    pe0, dv0, st0 = b.cnts["pe"], b.cnts["dv"], b.cnts["st"]

    @block.tensor
    def _(tensor):
        for i in range(16):
            tensor.wait_ge(b.s("dv"), dv0 + i)
            for c in range(NCH):
                mm = tensor.transpose(
                    PS[c][:, 0:128], X[:, c * T + i * 128:c * T + i * 128 + 128],
                    ID[:, :])
            mm.then_inc(b.s("pe"), 1)
            b.inc("pe", 1)

    dv_tile = []
    sx0 = {0: b.cnts["sx0"], 1: b.cnts["sx1"]}

    @block.vector
    def _(vector):
        for i in range(16):
            vector.wait_ge(b.s("pe"), pe0 + i + 1)
            if i >= 8:
                sname = "sx%d" % (i % 2)
                vector.wait_ge(b.s(sname), sx0[i % 2] + ((i - 8) // 2 + 1) * 16)
            for c in range(NCH):
                ins = vector.tensor_copy(
                    RB1[:, (i % 8) * 1024 + c * 128:(i % 8) * 1024 + c * 128 + 128],
                    PS[c][:, 0:128])
            ins.then_inc(b.s("dv"), 1)
            dv_tile.append(b.inc("dv", 1))

    @block.sync
    def _(sync):
        for i in range(16):
            sync.wait_ge(b.s("dv"), dv_tile[i])
            sname = "sx%d" % (i % 2)
            if i >= 2:
                sync.wait_ge(b.s(sname), sx0[i % 2] + (i // 2) * 16)
            sync.dma_start(
                out=out[i * 128:(i + 1) * 128, :],
                in_=RB1[:, (i % 8) * 1024:(i % 8) * 1024 + 1024],
            ).then_inc(b.s(sname), 16)
            b.inc(sname, 16)
        sync.wait_ge(b.s("sx0"), sx0[0] + 8 * 16)
        sync.wait_ge(b.s("sx1"), sx0[1] + 8 * 16)


def attn64(b, block, cfg, state):
    em = E(b)
    state["prev_pair_done"] = cfg["ln_dv"]
    for hp in range(8):
        _attn64_pair(b, em, cfg, hp, state)
    em.emit(block)

    if cfg.get("internal_residual"):
        X, XQH, XQL = b.t["X"], b.t["XQH"], b.t["XQL"]
        last = state["prev_pair_done"]

        @block.vector
        def _(vector):
            vector.wait_ge(b.s("dv"), last)
            for c in range(NCH):
                vector.tensor_tensor(X[:, c * T:(c + 1) * T], X[:, c * T:(c + 1) * T],
                                     XQH[:, c * T:(c + 1) * T], ALU.add)
            vector.drain()
            for c in range(NCH):
                ins = vector.tensor_tensor(
                    X[:, c * T:(c + 1) * T], X[:, c * T:(c + 1) * T],
                    XQL[:, c * T:(c + 1) * T], ALU.add)
            ins.then_inc(b.s("dv"), 1)
            b.inc("dv", 1)
    return b.cnts["dv"]


def _attn64_pair(b, em, cfg, hp, state):
    X, RB1, RB2 = b.t["X"], b.t["RB1"], b.t["RB2"]
    XQH, XQL = b.t["XQH"], b.t["XQL"]
    RBH1, RBH2, ZBH = b.t["RBH1"], b.t["RBH2"], b.t["ZBH"]
    WSF, WSH = b.t["WSF"], b.t["WSH"]
    ONESM, ID, SM = b.t["ONESM"], b.t["IDent"], b.t["SMALL"]
    SELH, PICK32, DORA = b.t["SELH"], b.t["PICK32"], b.t["DORA"]
    PS = [b.t[f"PS{i}"] for i in range(8)]
    scale = cfg["scale"]
    VT, EB, RP = 0, 2080, 4128
    prev_done = state["prev_pair_done"]
    presplit = cfg.get("presplit", False)
    ps_fn = cfg.get("pair_setup")
    bias_waits = ps_fn(em, hp) if ps_fn else []

    split_dv = {}
    split_specs = {}
    wcol_consumed = {}
    prev_wo_pe = state.get("prev_wo_pe")
    for j in range(5):
        slot = j % 2
        if j < 3:
            pieces = [(cfg[("wq_ap", "wk_ap", "wv_ap")[j]](hp, e), e)
                      for e in (0, 1)]
            lanes = (0, 128)
        else:
            pieces = [(cfg["wo_ap"](hp, j - 3), None)]
            lanes = (0, 64)
        sf, sh = "wsf%d" % slot, "wsh%d" % slot
        if not presplit:
            dq_tgt = b.cnts[sf] + 16 * len(pieces)
            b.inc(sf, 16 * len(pieces))
            gq_tgt = b.cnts[sh] + 16 * len(pieces)
            b.inc(sh, 16 * len(pieces))

            @em.on("sync")
            def _(sync, j=j, slot=slot, pieces=pieces, sf=sf):
                if j >= 2:
                    sync.wait_ge(b.s("pe"), wcol_consumed[j - 2])
                elif prev_wo_pe is not None:
                    sync.wait_ge(b.s("pe"), prev_wo_pe)
                for src, e in pieces:
                    if e is not None:
                        dst = WSF[:, slot * 1024:slot * 1024 + 1024].rearrange(
                            "p (c two k) -> p c two k", two=2, k=64)[:, :, e, :]
                    else:
                        dst = WSF[0:64, slot * 1024:slot * 1024 + 1024]
                    sync.dma_start(out=dst, in_=src).then_inc(b.s(sf), 16)

            @em.on("gpsimd")
            def _(gp, j=j, slot=slot, pieces=pieces, sh=sh):
                if j >= 2:
                    gp.wait_ge(b.s("pe"), wcol_consumed[j - 2])
                elif prev_wo_pe is not None:
                    gp.wait_ge(b.s("pe"), prev_wo_pe)
                for src, e in pieces:
                    if e is not None:
                        dst = WSH[:, slot * 2048:slot * 2048 + 1024].rearrange(
                            "p (c two k) -> p c two k", two=2, k=64)[:, :, e, :]
                    else:
                        dst = WSH[0:64, slot * 2048:slot * 2048 + 1024]
                    gp.dma_start(out=dst, in_=src).then_inc(b.s(sh), 16)
        else:
            dq_tgt = b.cnts[sf] + 16 * 2 * len(pieces)
            b.inc(sf, 16 * 2 * len(pieces))
            gq_tgt = None

            @em.on("sync")
            def _(sync, j=j, slot=slot, pieces=pieces, sf=sf):
                if j >= 2:
                    sync.wait_ge(b.s("pe"), wcol_consumed[j - 2])
                elif prev_wo_pe is not None:
                    sync.wait_ge(b.s("pe"), prev_wo_pe)
                for ap2, e in pieces:
                    hi_ap, lo_ap = ap2
                    for h, src_ in ((0, hi_ap), (1, lo_ap)):
                        if e is not None:
                            dst = WSH[:, slot * 2048 + h * 1024:
                                      slot * 2048 + h * 1024 + 1024].rearrange(
                                "p (c two k) -> p c two k", two=2, k=64)[:, :, e, :]
                        else:
                            dst = WSH[0:64, slot * 2048 + h * 1024:
                                      slot * 2048 + h * 1024 + 1024]
                        sync.dma_start(out=dst, in_=src_).then_inc(b.s(sf), 16)

        split_specs[j] = (slot, lanes, dq_tgt, gq_tgt, sf, sh)

    def append_split(j):
        slot, lanes, dq_tgt, gq_tgt, sf, sh = split_specs[j]
        if gq_tgt is None:
            split_dv[j] = (sf, dq_tgt)
            return

        @em.on("vector")
        def _(v, j=j, slot=slot, lanes=lanes, dq_tgt=dq_tgt, gq_tgt=gq_tgt,
              sf=sf, sh=sh):
            v.wait_ge(b.s(sf), dq_tgt)
            v.wait_ge(b.s(sh), gq_tgt)
            p0, p1 = lanes
            v.tensor_tensor(
                WSH[p0:p1, slot * 2048 + 1024:slot * 2048 + 2048],
                WSF[p0:p1, slot * 1024:slot * 1024 + 1024],
                WSH[p0:p1, slot * 2048:slot * 2048 + 1024], ALU.subtract,
            ).then_inc(b.s("dv"), 1)
        split_dv[j] = ("dv", b.inc("dv", 1))

    append_split(0)
    append_split(1)

    evac_dv = {}

    def emit_proj(j, dst_hi, dst_lo, vf32):
        @em.on("tensor")
        def _(tensor, j=j):
            s_sem, s_val = split_dv[j]
            tensor.wait_ge(b.s(s_sem), s_val)
            if j == 0:
                if prev_done is not None:
                    tensor.wait_ge(b.s("dv"), prev_done)
                tensor.wait_ge(b.s("dv"), cfg["ln_dv"])
            else:
                tensor.wait_ge(b.s("dv"), evac_dv[j - 1])
            slot = j % 2
            for c in range(NCH):
                wh = WSH[:, slot * 2048 + c * 128:slot * 2048 + c * 128 + 128]
                wl = WSH[:, slot * 2048 + 1024 + c * 128:
                         slot * 2048 + 1024 + c * 128 + 128]
                for q in range(4):
                    xsl = slice(c * T + q * 512, c * T + q * 512 + 512)
                    tensor.matmul(PS[q][:, 0:512], wh, XQH[:, xsl],
                                  start=(c == 0), stop=False)
                    tensor.matmul(PS[q][:, 0:512], wh, XQL[:, xsl],
                                  start=False, stop=False)
                    mm = tensor.matmul(PS[q][:, 0:512], wl, XQH[:, xsl],
                                       start=False, stop=(c == NCH - 1))
            mm.then_inc(b.s("pe"), 1)
        pe_done = b.inc("pe", 1)
        wcol_consumed[j] = pe_done

        @em.on("vector")
        def _(v, j=j, pe_done=pe_done, dst_hi=dst_hi, dst_lo=dst_lo, vf32=vf32):
            v.wait_ge(b.s("pe"), pe_done)
            if vf32:
                mod = cfg.get("evac_mod")
                srcs = []
                if mod is not None:
                    for sm, val in bias_waits:
                        v.wait_ge(b.s(sm), val)
                for q in range(4):
                    s_ = PS[q][:, 0:512]
                    if mod is not None:
                        s_ = mod(v, hp, j, q, s_)
                    srcs.append(s_)
                if mod is not None:
                    v.drain()
                for q in range(4):
                    ins = v.tensor_copy(
                        RB1[:, EB + q * 512:EB + q * 512 + 512], srcs[q])
            else:
                mod = cfg.get("evac_mod")
                srcs = []
                if mod is not None:
                    for sm, val in bias_waits:
                        v.wait_ge(b.s(sm), val)
                for q in range(4):
                    s_ = PS[q][:, 0:512]
                    if mod is not None:
                        s_ = mod(v, hp, j, q, s_)
                    srcs.append(s_)
                if mod is not None:
                    v.drain()
                for q in range(4):
                    v.tensor_copy(dst_hi[:, q * 512:q * 512 + 512], srcs[q])
                v.drain()
                for q in range(4):
                    ins = v.tensor_tensor(
                        dst_lo[:, q * 512:q * 512 + 512], srcs[q],
                        dst_hi[:, q * 512:q * 512 + 512], ALU.subtract)
            ins.then_inc(b.s("dv"), 1)
        return b.inc("dv", 1)

    evac_dv[0] = emit_proj(0, RBH2[:, 0:2048], RBH2[:, 2048:4096], False)
    append_split(2)
    evac_dv[1] = emit_proj(1, RBH1[:, 0:2048], RBH1[:, 2048:4096], False)
    append_split(3)
    evac_dv[2] = emit_proj(2, None, None, True)
    append_split(4)

    @em.on("vector")
    def _(v):
        if prev_done is not None:
            v.wait_ge(b.s("dv"), prev_done)
        v.memset(RB1[:, VT:VT + 2080], 1.0).then_inc(b.s("dv"), 1)
    vt_ones = b.inc("dv", 1)

    tr_pe, vtr_dv = [], []
    for rnd in range(4):
        tr_pe.append(b.cnts["pe"] + 1)
        b.inc("pe", 1)
        vtr_dv.append(b.cnts["dv"] + 1)
        b.inc("dv", 1)
    for rnd in range(4):
        @em.on("tensor")
        def _(tensor, rnd=rnd):
            if rnd == 0:
                tensor.wait_ge(b.s("dv"), evac_dv[2])
            if rnd >= 2:
                tensor.wait_ge(b.s("dv"), vtr_dv[rnd - 2])
            bank = PS[4 + rnd % 2]
            for kk in range(4):
                kb = rnd * 4 + kk
                mm = tensor.transpose(
                    bank[:, kk * 128:kk * 128 + 128],
                    RB1[:, EB + kb * 128:EB + kb * 128 + 128], ID[:, :])
            mm.then_inc(b.s("pe"), 1)

        @em.on("vector")
        def _(v, rnd=rnd):
            v.wait_ge(b.s("pe"), tr_pe[rnd])
            if rnd == 0:
                v.wait_ge(b.s("dv"), vt_ones)
            bank = PS[4 + rnd % 2]
            for kk in range(4):
                kb = rnd * 4 + kk
                v.tensor_copy(RB1[:, VT + kb * 130:VT + kb * 130 + 64],
                              bank[:, kk * 128:kk * 128 + 64])
                ins = v.tensor_copy(
                    RB1[:, VT + kb * 130 + 65:VT + kb * 130 + 129],
                    bank[:, kk * 128 + 64:kk * 128 + 128])
            ins.then_inc(b.s("dv"), 1)

    ct = cfg.get("ct_rows", False)
    rrow_dv = {}
    if ct:
        ksq_sc, knrm_pe, kred_dv = {}, {}, {}
        for qt in range(4):
            ksq_sc[qt] = b.inc("sc")
        for qt in range(4):
            knrm_pe[qt] = b.inc("pe")
        for qt in range(4):
            kred_dv[qt] = b.inc("dv")

        @em.on("scalar")
        def _(scalar):
            scalar.wait_ge(b.s("dv"), evac_dv[1])
            for qt in range(4):
                if qt >= 2:
                    scalar.wait_ge(b.s("pe"), knrm_pe[qt - 2])
                scalar.activation(
                    ZBH[:, (qt % 2) * 512:(qt % 2) * 512 + 512],
                    RBH1[:, qt * 512:qt * 512 + 512], ACT.Square,
                ).then_inc(b.s("sc"), 1)

        @em.on("tensor")
        def _(tensor):
            for qt in range(4):
                tensor.wait_ge(b.s("sc"), ksq_sc[qt])
                if qt >= 1:
                    tensor.wait_ge(b.s("dv"), kred_dv[qt - 1])
                tensor.matmul(PS[6][0:1, 0:512], SELH[:, 0:1],
                              ZBH[:, (qt % 2) * 512:(qt % 2) * 512 + 512],
                              start=True, stop=True)
                tensor.matmul(PS[6][64:65, 0:512], SELH[:, 1:2],
                              ZBH[:, (qt % 2) * 512:(qt % 2) * 512 + 512],
                              start=True, stop=True).then_inc(b.s("pe"), 1)

        @em.on("vector")
        def _(v):
            for qt in range(4):
                v.wait_ge(b.s("pe"), knrm_pe[qt])
                for p in (0, 64):
                    v.reduce_max(SM[p:p + 1, 12:13], PS[6][p:p + 1, 0:512],
                                 axis=AX)
                v.drain()
                for p in (0, 64):
                    if qt == 0:
                        ins = v.tensor_copy(SM[p:p + 1, 9:10],
                                            SM[p:p + 1, 12:13])
                    else:
                        ins = v.tensor_tensor(SM[p:p + 1, 9:10],
                                              SM[p:p + 1, 9:10],
                                              SM[p:p + 1, 12:13], ALU.max)
                ins.then_inc(b.s("dv"), 1)

        nk_sc = b.inc("sc")
        nk_dv = b.inc("dv")

        @em.on("scalar")
        def _(scalar):
            scalar.wait_ge(b.s("dv"), kred_dv[3])
            scalar.activation(SM[0:1, 10:11], SM[0:1, 9:10], ACT.Sqrt,
                              bias=0.0, scale=1.0404)
            scalar.activation(SM[64:65, 10:11], SM[64:65, 9:10], ACT.Sqrt,
                              bias=0.0, scale=1.0404).then_inc(b.s("sc"), 1)

        @em.on("vector")
        def _(v):
            v.wait_ge(b.s("sc"), nk_sc)
            v.tensor_single_scalar(SM[0:1, 10:11], SM[0:1, 10:11], -1.0,
                                   ALU.mult)
            v.tensor_single_scalar(SM[64:65, 10:11], SM[64:65, 10:11], -1.0,
                                   ALU.mult).then_inc(b.s("dv"), 1)

        rsq_sc, rnrm_pe, rsqrt_sc = {}, {}, {}
        for qt in range(4):
            rsq_sc[qt] = b.inc("sc")
            rsqrt_sc[qt] = b.inc("sc")
        for qt in range(4):
            rnrm_pe[qt] = b.inc("pe")
        for qt in range(4):
            rrow_dv[qt] = b.inc("dv")

        @em.on("scalar")
        def _(scalar):
            for qt in range(4):
                if qt == 0:
                    scalar.wait_ge(b.s("dv"), evac_dv[0])
                if qt >= 2:
                    scalar.wait_ge(b.s("pe"), rnrm_pe[qt - 2])
                scalar.activation(
                    ZBH[:, (qt % 2) * 512:(qt % 2) * 512 + 512],
                    RBH2[:, qt * 512:qt * 512 + 512], ACT.Square,
                ).then_inc(b.s("sc"), 1)
                scalar.wait_ge(b.s("pe"), rnrm_pe[qt])
                scalar.activation(WSF[0:1, qt * 512:qt * 512 + 512],
                                  PS[6][0:1, 0:512], ACT.Sqrt,
                                  bias=0.0, scale=1.0)
                scalar.activation(WSF[64:65, qt * 512:qt * 512 + 512],
                                  PS[6][64:65, 0:512], ACT.Sqrt,
                                  bias=0.0, scale=1.0).then_inc(b.s("sc"), 1)

        @em.on("tensor")
        def _(tensor):
            for qt in range(4):
                tensor.wait_ge(b.s("sc"), rsq_sc[qt])
                if qt == 0:
                    tensor.wait_ge(b.s("dv"), kred_dv[3])
                else:
                    tensor.wait_ge(b.s("sc"), rsqrt_sc[qt - 1])
                tensor.matmul(PS[6][0:1, 0:512], SELH[:, 0:1],
                              ZBH[:, (qt % 2) * 512:(qt % 2) * 512 + 512],
                              start=True, stop=True)
                tensor.matmul(PS[6][64:65, 0:512], SELH[:, 1:2],
                              ZBH[:, (qt % 2) * 512:(qt % 2) * 512 + 512],
                              start=True, stop=True).then_inc(b.s("pe"), 1)

        @em.on("vector")
        def _(v):
            for qt in range(4):
                v.wait_ge(b.s("sc"), rsqrt_sc[qt])
                if qt == 0:
                    v.wait_ge(b.s("dv"), nk_dv)
                v.tensor_scalar(WSF[0:1, qt * 512:qt * 512 + 512],
                                WSF[0:1, qt * 512:qt * 512 + 512],
                                SM[0:1, 10:11], 1.0, ALU.mult, ALU.mult)
                v.tensor_scalar(WSF[64:65, qt * 512:qt * 512 + 512],
                                WSF[64:65, qt * 512:qt * 512 + 512],
                                SM[64:65, 10:11], 1.0, ALU.mult, ALU.mult,
                                ).then_inc(b.s("dv"), 1)
        bias_dv = None

    if not ct:
        sq_sc, nrm_pe, red_dv = {}, {}, {}
        for which in (0, 1):
            for qt in range(4):
                sq_sc[which, qt] = b.inc("sc")
        for which in (0, 1):
            for qt in range(4):
                nrm_pe[which, qt] = b.inc("pe")
        for which in (0, 1):
            for qt in range(4):
                red_dv[which, qt] = b.inc("dv")

        for which, SRC in ((0, RBH2), (1, RBH1)):
            @em.on("scalar")
            def _(scalar, which=which, SRC=SRC):
                scalar.wait_ge(b.s("dv"), evac_dv[which])
                for qt in range(4):
                    if (which, qt) > (0, 1):
                        prev = (which, qt - 2) if qt >= 2 else (which - 1, qt + 2)
                        scalar.wait_ge(b.s("pe"), nrm_pe[prev])
                    scalar.activation(
                        ZBH[:, (qt % 2) * 512:(qt % 2) * 512 + 512],
                        SRC[:, qt * 512:qt * 512 + 512], ACT.Square,
                    ).then_inc(b.s("sc"), 1)

            @em.on("tensor")
            def _(tensor, which=which):
                for qt in range(4):
                    tensor.wait_ge(b.s("sc"), sq_sc[which, qt])
                    if (which, qt) > (0, 0):
                        prev = (which, qt - 1) if qt >= 1 else (which - 1, 3)
                        tensor.wait_ge(b.s("dv"), red_dv[prev])
                    tensor.matmul(PS[6][0:2, 0:512], SELH[:, 0:2],
                                  ZBH[:, (qt % 2) * 512:(qt % 2) * 512 + 512],
                                  start=True, stop=True).then_inc(b.s("pe"), 1)

            @em.on("vector")
            def _(v, which=which):
                for qt in range(4):
                    v.wait_ge(b.s("pe"), nrm_pe[which, qt])
                    v.reduce_max(SM[0:2, 12:13], PS[6][0:2, 0:512], axis=AX)
                    v.drain()
                    if qt == 0:
                        ins = v.tensor_copy(SM[0:2, 8 + which:9 + which],
                                            SM[0:2, 12:13])
                    else:
                        ins = v.tensor_tensor(SM[0:2, 8 + which:9 + which],
                                              SM[0:2, 8 + which:9 + which],
                                              SM[0:2, 12:13], ALU.max)
                    ins.then_inc(b.s("dv"), 1)

        @em.on("vector")
        def _(v):
            v.drain()
            v.tensor_tensor(SM[0:2, 10:11], SM[0:2, 8:9], SM[0:2, 9:10],
                            ALU.mult).then_inc(b.s("dv"), 1)
        cc_dv = b.inc("dv", 1)

        @em.on("scalar")
        def _(scalar):
            scalar.wait_ge(b.s("dv"), cc_dv)
            scalar.activation(SM[0:2, 11:12], SM[0:2, 10:11], ACT.Sqrt,
                              bias=0.0, scale=1.0404 * scale * scale,
                              ).then_inc(b.s("sc"), 1)
        csqrt_sc = b.inc("sc", 1)

        @em.on("vector")
        def _(v):
            v.wait_ge(b.s("sc"), csqrt_sc)
            v.tensor_single_scalar(SM[0:2, 11:12], SM[0:2, 11:12], -1.0, ALU.mult)
            v.drain()
            v.tensor_copy(b.t["NEGCH"][0:2, 0:1],
                          SM[0:2, 11:12]).then_inc(b.s("dv"), 1)
        negc_dv = b.inc("dv", 1)

        @em.on("tensor")
        def _(tensor):
            tensor.wait_ge(b.s("dv"), negc_dv)
            tensor.matmul(PS[6][:, 0:1], PICK32[0:2, 0:128], b.t["NEGCH"][0:2, 0:1],
                          start=True, stop=True)
            mm = tensor.matmul(PS[7][:, 0:1], PICK32[0:2, 128:256],
                               b.t["NEGCH"][0:2, 0:1], start=True, stop=True)
            mm.then_inc(b.s("pe"), 1)
        bias_pe = b.inc("pe", 1)

        @em.on("vector")
        def _(v):
            v.wait_ge(b.s("pe"), bias_pe)
            v.tensor_copy(SM[:, 0:1], PS[6][:, 0:1])
            v.tensor_copy(SM[:, 1:2], PS[7][:, 0:1]).then_inc(b.s("dv"), 1)
        bias_dv = b.inc("dv", 1)

    sc_exp = {}
    pe_scores = {}
    pe_av = {}
    dv_z = {}

    def _avmm(tensor, qt, kb):
        tensor.wait_ge(b.s("sc"), sc_exp[qt, kb])
        if kb == 0:
            if qt == 0:
                tensor.wait_ge(b.s("dv"), vtr_dv[3])
            else:
                tensor.wait_ge(b.s("dv"), dv_z[qt - 1])
        for e in (0, 1):
            vt_ap = RB1[:, VT + kb * 130 + e * 65:VT + kb * 130 + e * 65 + 65]
            e_ap = RB1[:, EB + (kb % 2) * 1024 + e * 512:
                       EB + (kb % 2) * 1024 + e * 512 + 512]
            mm = tensor.matmul(PS[e][0:65, 0:512], vt_ap, e_ap,
                               start=(kb == 0), stop=(kb == 15))
        mm.then_inc(b.s("pe"), 1)

    for qt in range(4):
        for kb in range(16):
            pe_scores[qt, kb] = b.inc("pe")
            if kb >= 1:
                pe_av[qt, kb - 1] = b.inc("pe")
        pe_av[qt, 15] = b.inc("pe")
        for kb in range(16):
            sc_exp[qt, kb] = b.inc("sc")

        @em.on("tensor")
        def _(tensor, qt=qt):
            for kb in range(16):
                sb_, ob_ = (2, 3) if kb % 2 == 0 else (4, 5)
                if qt == 0 and kb == 0:
                    tensor.wait_ge(b.s("dv"), evac_dv[0])
                    tensor.wait_ge(b.s("dv"), evac_dv[1])
                if kb >= 2:
                    tensor.wait_ge(b.s("sc"), sc_exp[qt, kb - 2])
                elif qt >= 1:
                    tensor.wait_ge(b.s("sc"), sc_exp[qt - 1, 14 + kb])
                if ct and kb == 0:
                    tensor.wait_ge(b.s("dv"), rrow_dv[qt])
                for e in (0, 1):
                    bank = PS[sb_] if e == 0 else PS[ob_]
                    r0 = 64 * e
                    kh = RBH1[r0:r0 + 64, kb * 128:kb * 128 + 128]
                    kl = RBH1[r0:r0 + 64, 2048 + kb * 128:2048 + kb * 128 + 128]
                    qh = RBH2[r0:r0 + 64, qt * 512:qt * 512 + 512]
                    ql = RBH2[r0:r0 + 64, 2048 + qt * 512:2048 + qt * 512 + 512]
                    tp = (r0, 0)
                    tensor.matmul(bank[:, 0:512], kh, qh, start=True, stop=False,
                                  tile_position=tp)
                    tensor.matmul(bank[:, 0:512], kh, ql, start=False, stop=False,
                                  tile_position=tp)
                    mm = tensor.matmul(bank[:, 0:512], kl, qh, start=False,
                                       stop=not ct, tile_position=tp)
                    if ct:
                        mm = tensor.matmul(
                            bank[:, 0:512], ONESM[r0:r0 + 1, 0:128],
                            WSF[r0:r0 + 1, qt * 512:qt * 512 + 512],
                            start=False, stop=True)
                mm.then_inc(b.s("pe"), 1)
                if kb >= 1:
                    _avmm(tensor, qt, kb - 1)
            _avmm(tensor, qt, 15)

        @em.on("scalar")
        def _(scalar, qt=qt):
            for kb in range(16):
                sb_, ob_ = (2, 3) if kb % 2 == 0 else (4, 5)
                scalar.wait_ge(b.s("pe"), pe_scores[qt, kb])
                if qt == 0 and kb == 0 and not ct:
                    scalar.wait_ge(b.s("dv"), bias_dv)
                if kb >= 2:
                    scalar.wait_ge(b.s("pe"), pe_av[qt, kb - 2])
                eb0 = 0.0 if ct else SM[:, 0:1]
                eb1 = 0.0 if ct else SM[:, 1:2]
                scalar.activation(
                    RB1[:, EB + (kb % 2) * 1024:EB + (kb % 2) * 1024 + 512],
                    PS[sb_][:, 0:512], ACT.Exp, bias=eb0, scale=scale)
                scalar.activation(
                    RB1[:, EB + (kb % 2) * 1024 + 512:EB + (kb % 2) * 1024 + 1024],
                    PS[ob_][:, 0:512], ACT.Exp, bias=eb1, scale=scale,
                ).then_inc(b.s("sc"), 1)

        rc_dv = b.inc("dv")
        rp_pe = b.inc("pe")
        z_dv = b.inc("dv")
        dv_z[qt] = z_dv
        wo_pe = [b.inc("pe") for _ in range(16)]
        xacc = [b.inc("dv") for _ in range(16)]
        if qt == 3:
            wcol_consumed[3] = wo_pe[15]
            wcol_consumed[4] = wo_pe[15]
            state["prev_wo_pe"] = wo_pe[15]

        @em.on("vector")
        def _(v, qt=qt):
            v.wait_ge(b.s("pe"), pe_av[qt, 15])
            v.reciprocal(RB2[64:65, 0:512], PS[0][64:65, 0:512])
            v.reciprocal(RB2[64:65, 512:1024], PS[1][64:65, 0:512],
                         ).then_inc(b.s("dv"), 1)

        @em.on("tensor")
        def _(tensor, qt=qt, rc_dv=rc_dv):
            tensor.wait_ge(b.s("dv"), rc_dv)
            tensor.matmul(PS[6][:, 0:512], ONESM[64:65, 0:128],
                          RB2[64:65, 0:512], start=True, stop=True)
            mm = tensor.matmul(PS[7][:, 0:512], ONESM[64:65, 0:128],
                               RB2[64:65, 512:1024], start=True, stop=True)
            mm.then_inc(b.s("pe"), 1)

        @em.on("vector")
        def _(v, qt=qt, rp_pe=rp_pe):
            v.wait_ge(b.s("pe"), rp_pe)
            v.tensor_copy(RB1[:, RP:RP + 512], PS[6][:, 0:512])
            v.tensor_copy(RB1[:, RP + 512:RP + 1024], PS[7][:, 0:512])
            v.drain()
            for e in (0, 1):
                v.tensor_tensor(ZBH[0:64, 1024 * e:1024 * e + 512],
                                PS[e][0:64, 0:512],
                                RB1[0:64, RP + 512 * e:RP + 512 * e + 512],
                                ALU.mult)
            for e in (0, 1):
                v.tensor_tensor(RB2[0:64, 1024 + 512 * e:1536 + 512 * e],
                                PS[e][0:64, 0:512],
                                RB1[0:64, RP + 512 * e:RP + 512 * e + 512],
                                ALU.mult)
            v.drain()
            for e in (0, 1):
                ins = v.tensor_tensor(
                    ZBH[0:64, 1024 * e + 512:1024 * e + 1024],
                    RB2[0:64, 1024 + 512 * e:1536 + 512 * e],
                    ZBH[0:64, 1024 * e:1024 * e + 512], ALU.subtract)
            ins.then_inc(b.s("dv"), 1)

        @em.on("tensor")
        def _(tensor, qt=qt, z_dv=z_dv, xacc=xacc):
            tensor.wait_ge(b.s("dv"), z_dv)
            if qt == 0:
                for jj in (3, 4):
                    w_sem, w_val = split_dv[jj]
                    tensor.wait_ge(b.s(w_sem), w_val)
            idx = 0
            for e in (0, 1):
                slot = 1 if e == 0 else 0
                for do in range(8):
                    bank = PS[6 + (idx % 2)]
                    if idx >= 2:
                        tensor.wait_ge(b.s("dv"), xacc[idx - 2])
                    wh = WSH[0:64,
                             slot * 2048 + do * 128:slot * 2048 + do * 128 + 128]
                    wl = WSH[0:64, slot * 2048 + 1024 + do * 128:
                             slot * 2048 + 1024 + do * 128 + 128]
                    zh = ZBH[0:64, 1024 * e:1024 * e + 512]
                    zl = ZBH[0:64, 1024 * e + 512:1024 * e + 1024]
                    tensor.matmul(bank[:, 0:512], wh, zh, start=True, stop=False)
                    tensor.matmul(bank[:, 0:512], wh, zl, start=False, stop=False)
                    mm = tensor.matmul(bank[:, 0:512], wl, zh, start=False,
                                       stop=True)
                    mm.then_inc(b.s("pe"), 1)
                    idx += 1

        @em.on("vector")
        def _(v, qt=qt, wo_pe=wo_pe):
            idx = 0
            xb = cfg.get("xacc_bias")
            for e in (0, 1):
                for do in range(8):
                    bank = PS[6 + (idx % 2)]
                    v.wait_ge(b.s("pe"), wo_pe[idx])
                    sl = slice(do * T + qt * 512, do * T + qt * 512 + 512)
                    if xb is not None and e == 0 and hp == 0:
                        ins = v.scalar_tensor_tensor(
                            X[:, sl], bank[:, 0:512], xb(do), X[:, sl],
                            ALU.add, ALU.add)
                    else:
                        ins = v.tensor_tensor(X[:, sl], X[:, sl],
                                              bank[:, 0:512], ALU.add)
                    ins.then_inc(b.s("dv"), 1)
                    idx += 1

    state["prev_pair_done"] = b.cnts["dv"]


def prep_T(b, block, state, blocks, scr):
    WSF, WSH, ID = b.t["WSF"], b.t["WSH"], b.t["IDent"]
    PS = [b.t[f"PS{i}"] for i in range(8)]
    em = E(b)
    n = len(blocks)
    ld_tgt, tr_pe, mid_dv, sp_dv, st_tgt = {}, {}, {}, {}, {}
    for i in range(n):
        slot = i % 2
        ld_tgt[i] = b.inc("wsf%d" % slot, 16)
    for i in range(n):
        tr_pe[i] = b.inc("pe")
        tr_pe[i, 1] = b.inc("pe")
    for i in range(n):
        mid_dv[i] = b.inc("dv")
        sp_dv[i] = b.inc("dv")
    for i in range(n):
        slot = i % 2
        st_tgt[i] = b.inc("wsh%d" % slot, 32)

    for i in range(n):
        src_ap, col0 = blocks[i]
        slot = i % 2

        @em.on("sync")
        def _(sync, i=i, slot=slot, src_ap=src_ap):
            if i >= 2:
                sync.wait_ge(b.s("wsh%d" % slot), st_tgt[i - 2])
            elif state.get("prev_wo_pe") is not None:
                sync.wait_ge(b.s("pe"), state["prev_wo_pe"])
            sync.dma_start(
                out=WSF[:, slot * 1024:slot * 1024 + 1024],
                in_=src_ap,
            ).then_inc(b.s("wsf%d" % slot), 16)

        @em.on("tensor")
        def _(tensor, i=i, slot=slot):
            tensor.wait_ge(b.s("wsf%d" % slot), ld_tgt[i])
            if i >= 2:
                tensor.wait_ge(b.s("dv"), sp_dv[i - 2])
            bank = PS[0 + slot]
            for half in range(2):
                if half == 1:
                    tensor.wait_ge(b.s("dv"), mid_dv[i])
                for cc in range(4 * half, 4 * half + 4):
                    mm = tensor.transpose(
                        bank[:, (cc % 4) * 128:(cc % 4) * 128 + 128],
                        WSF[:, slot * 1024 + cc * 128:slot * 1024 + cc * 128 + 128],
                        ID[:, :])
                mm.then_inc(b.s("pe"), 1)

        @em.on("vector")
        def _(v, i=i, slot=slot):
            bank = PS[0 + slot]
            for half in range(2):
                v.wait_ge(b.s("pe"), tr_pe[i] if half == 0 else tr_pe[i, 1])
                for cc in range(4 * half, 4 * half + 4):
                    v.tensor_copy(
                        WSH[:, slot * 2048 + cc * 128:slot * 2048 + cc * 128 + 128],
                        bank[:, (cc % 4) * 128:(cc % 4) * 128 + 128])
                v.drain()
                for cc in range(4 * half, 4 * half + 4):
                    ins = v.tensor_tensor(
                        WSH[:, slot * 2048 + 1024 + cc * 128:
                            slot * 2048 + 1024 + cc * 128 + 128],
                        bank[:, (cc % 4) * 128:(cc % 4) * 128 + 128],
                        WSH[:, slot * 2048 + cc * 128:slot * 2048 + cc * 128 + 128],
                        ALU.subtract)
                ins.then_inc(b.s("dv"), 1)

        @em.on("gpsimd")
        def _(gp, i=i, slot=slot, col0=col0):
            gp.wait_ge(b.s("dv"), sp_dv[i])
            for h in range(2):
                src_s = WSH[:, slot * 2048 + h * 1024:
                            slot * 2048 + h * 1024 + 1024].rearrange(
                    "q (c d) -> q c d", c=8)
                dst = scr[h, :, col0:col0 + 128].rearrange(
                    "(c q) d -> q c d", q=128)
                gp.dma_start(out=dst, in_=src_s).then_inc(
                    b.s("wsh%d" % slot), 16)
    em.emit(block)
    return (b.cnts["wsh0"], b.cnts["wsh1"])


def ffn_prep(b, block, state):
    if state.get("ffn_prep_done"):
        return
    names = ["L0_ffn_W_r", "L0_ffn_W_k", "L0_ffn_W_v"]
    scr = b.t["scr_ffn"].rearrange("(h c d) -> h c d", h=2, c=1024)
    blocks = []
    for p in range(3):
        for dc in range(8):
            blocks.append((b.t[names[p]][dc * 128:dc * 128 + 128, :],
                           p * 1024 + dc * 128))
    state["ffn_prep_done"] = prep_T(b, block, state, blocks, scr)


def sub_l0_ffn(b, block, ln_dv, state):
    state["ffn_attn_wo_pe"] = state.get("prev_wo_pe")
    state.pop("ffn_prev_out_dv", None)
    ffn_prep(b, block, state)
    prep_done = state["ffn_prep_done"]
    X, RB1, RB2 = b.t["X"], b.t["RB1"], b.t["RB2"]
    XQH, XQL = b.t["XQH"], b.t["XQL"]
    RBH1, RBH2 = b.t["RBH1"], b.t["RBH2"]
    WSF, WSH, SM = b.t["WSF"], b.t["WSH"], b.t["SMALL"]
    PS = [b.t[f"PS{i}"] for i in range(8)]
    em = E(b)
    XL, T1, T2, S4 = 0, 2048, 4096, 6144

    mu_tgt = b.inc("dq", 16)
    muk_tgt = b.inc("dq", 16)
    mu_dv = b.inc("dv")

    @em.on("sync")
    def _(sync):
        sync.wait_ge(b.s("wsh0"), prep_done[0])
        sync.wait_ge(b.s("wsh1"), prep_done[1])
        sync.dma_start(out=SM[:, 32:40],
                       in_=b.t["L0_ffn_mu_r"].rearrange("(c p) -> p c", p=128),
                       allow_slow_non_contiguous=True).then_inc(b.s("dq"), 16)
        sync.dma_start(out=SM[:, 40:48],
                       in_=b.t["L0_ffn_mu_k"].rearrange("(c p) -> p c", p=128),
                       allow_slow_non_contiguous=True).then_inc(b.s("dq"), 16)

    @em.on("vector")
    def _(v):
        v.wait_ge(b.s("dq"), muk_tgt)
        v.tensor_scalar(SM[:, 48:56], SM[:, 32:40], -1.0, 1.0, ALU.mult, ALU.add)
        v.tensor_scalar(SM[:, 56:64], SM[:, 40:48], -1.0, 1.0,
                        ALU.mult, ALU.add).then_inc(b.s("dv"), 1)

    for tt in range(8):
        _ffn_tile(b, em, tt, ln_dv, mu_dv, state)
    em.emit(block)
    state["ffn_call"] = state.get("ffn_call", 0) + 1
    return b.cnts["dv"]


def _ffn_tile(b, em, tt, ln_dv, mu_dv, state):
    X, RB1, RB2 = b.t["X"], b.t["RB1"], b.t["RB2"]
    XQH, XQL = b.t["XQH"], b.t["XQL"]
    RBH1, RBH2 = b.t["RBH1"], b.t["RBH2"]
    WSF, WSH, SM = b.t["WSF"], b.t["WSH"], b.t["SMALL"]
    PS = [b.t[f"PS{i}"] for i in range(8)]
    scr = b.t["scr_ffn"].rearrange("(h c d) -> h c d", h=2, c=1024)
    XL, T1, T2, S4 = 0, 2048, 4096, 6144
    t0 = tt * 256
    bnd_w = 16 + (tt % 2) * 8
    bnd_r = 16 + ((tt + 1) % 2) * 8

    def sap(coloff, dims):
        base = RB1[:, 0:1]
        return bass.AP(base.tensor, base.offset + coloff,
                       [list(base.ap[0])] + dims)

    def smap(coloff, dims):
        base = SM[:, 0:1]
        return bass.AP(base.tensor, base.offset + coloff,
                       [list(base.ap[0])] + dims)

    mix_dv = b.inc("dv")
    prev_out = state.get("ffn_prev_out_dv")

    @em.on("vector")
    def _(v, tt=tt):
        if tt == 0:
            v.wait_ge(b.s("dv"), ln_dv)
            v.wait_ge(b.s("dv"), mu_dv)
        for c in range(NCH):
            v.tensor_tensor(RB1[:, XL + c * 256:XL + c * 256 + 256],
                            XQH[:, c * T + t0:c * T + t0 + 256],
                            XQL[:, c * T + t0:c * T + t0 + 256], ALU.add)
        v.drain()
        v.tensor_copy(smap(bnd_w, [[1, 8], [1, 1]]),
                      sap(XL + 255, [[256, 8], [1, 1]]))
        v.tensor_tensor(sap(T1 + 1, [[256, 8], [1, 255]]),
                        sap(XL + 1, [[256, 8], [1, 255]]),
                        sap(XL, [[256, 8], [1, 255]]), ALU.subtract)
        if tt == 0:
            v.tensor_copy(sap(T1, [[256, 8], [1, 1]]),
                          sap(XL, [[256, 8], [1, 1]]))
        else:
            v.tensor_tensor(sap(T1, [[256, 8], [1, 1]]),
                            sap(XL, [[256, 8], [1, 1]]),
                            smap(bnd_r, [[1, 8], [1, 1]]), ALU.subtract)
        v.drain()
        for which, mucol in ((0, 48), (1, 56)):
            mub = smap(mucol, [[1, 8], [0, 256]])
            v.tensor_tensor(sap(T2, [[256, 8], [1, 256]]),
                            sap(T1, [[256, 8], [1, 256]]), mub, ALU.mult)
            v.drain()
            v.tensor_tensor(RB1[:, S4:S4 + 2048], RB1[:, XL:XL + 2048],
                            RB1[:, T2:T2 + 2048], ALU.subtract)
            v.drain()
            H = RBH1 if which == 0 else RBH2
            v.tensor_copy(H[:, 0:2048], RB1[:, S4:S4 + 2048])
            v.drain()
            ins = v.tensor_tensor(H[:, 2048:4096], RB1[:, S4:S4 + 2048],
                                  H[:, 0:2048], ALU.subtract)
            v.drain()
        ins.then_inc(b.s("dv"), 1)

    wcons = state.setdefault("ffn_wcons", {})
    callid = state.get("ffn_call", 0)
    col_ld = {}
    for idx in range(24):
        slot = idx % 2
        col_ld[idx] = b.inc("wsf%d" % slot, 32)
    mm_pe = {}
    for p in range(3):
        mm_pe[p] = [b.inc("pe") for _ in range(8)]
    for idx in range(24):
        p, do = divmod(idx, 8)
        wcons[(callid, tt * 24 + idx)] = mm_pe[p][do]
    r_sc = b.inc("sc")
    k_sc = b.inc("sc")
    ksp_dv = b.inc("dv")
    out_dv = b.inc("dv")
    state["ffn_prev_out_dv"] = out_dv
    state["prev_wo_pe"] = mm_pe[2][7]

    @em.on("sync")
    def _(sync, tt=tt):
        for idx in range(24):
            p, do = divmod(idx, 8)
            slot = idx % 2
            gidx = tt * 24 + idx - 2
            if gidx >= 0 and (tt > 0 or idx >= 2):
                sync.wait_ge(b.s("pe"), wcons[(callid, gidx)])
            elif tt == 0 and idx < 2 and state.get("ffn_attn_wo_pe") is not None:
                sync.wait_ge(b.s("pe"), state["ffn_attn_wo_pe"])
            for h in range(2):
                sync.dma_start(
                    out=WSH[:, slot * 2048 + h * 1024:slot * 2048 + h * 1024 + 1024],
                    in_=scr[h, :, p * 1024 + do * 128:p * 1024 + do * 128 + 128
                            ].rearrange("(c q) d -> q c d", q=128),
                ).then_inc(b.s("wsf%d" % slot), 16)

    @em.on("tensor")
    def _(tensor, tt=tt, prev_out=prev_out):
        for p in range(3):
            for do in range(8):
                idx = p * 8 + do
                slot = idx % 2
                tensor.wait_ge(b.s("wsf%d" % slot), col_ld[idx])
                if idx == 0:
                    tensor.wait_ge(b.s("dv"), mix_dv)
                    if prev_out is not None:
                        tensor.wait_ge(b.s("dv"), prev_out)
                if p == 1 and do == 0:
                    tensor.wait_ge(b.s("sc"), r_sc)
                if p == 2 and do == 0:
                    tensor.wait_ge(b.s("dv"), ksp_dv)
                H = RBH1 if p == 0 else (RBH2 if p == 1 else RBH1)
                bank = PS[do]
                for c in range(NCH):
                    wh = WSH[:, slot * 2048 + c * 128:slot * 2048 + c * 128 + 128]
                    wl = WSH[:, slot * 2048 + 1024 + c * 128:
                             slot * 2048 + 1024 + c * 128 + 128]
                    tensor.matmul(bank[:, 0:256], wh, H[:, c * 256:c * 256 + 256],
                                  start=(c == 0), stop=False)
                    tensor.matmul(bank[:, 0:256], wh,
                                  H[:, 2048 + c * 256:2048 + c * 256 + 256],
                                  start=False, stop=False)
                    mm = tensor.matmul(bank[:, 0:256], wl,
                                       H[:, c * 256:c * 256 + 256],
                                       start=False, stop=(c == NCH - 1))
                mm.then_inc(b.s("pe"), 1)

    @em.on("scalar")
    def _(scalar, tt=tt):
        for do in range(8):
            scalar.wait_ge(b.s("pe"), mm_pe[0][do])
            if do == 0:
                scalar.wait_ge(b.s("dv"), mix_dv)
            ins = scalar.activation(RB1[:, XL + do * 256:XL + do * 256 + 256],
                                    PS[do][:, 0:256], ACT.Sigmoid)
        ins.then_inc(b.s("sc"), 1)
        for do in range(8):
            scalar.wait_ge(b.s("pe"), mm_pe[1][do])
            scalar.activation(RB1[:, T1 + do * 256:T1 + do * 256 + 256],
                              PS[do][:, 0:256], ACT.Relu)
        scalar.drain()
        for do in range(8):
            ins = scalar.activation(RB1[:, T2 + do * 256:T2 + do * 256 + 256],
                                    RB1[:, T1 + do * 256:T1 + do * 256 + 256],
                                    ACT.Square)
        ins.then_inc(b.s("sc"), 1)

    @em.on("vector")
    def _(v, tt=tt):
        v.wait_ge(b.s("sc"), k_sc)
        v.wait_ge(b.s("pe"), mm_pe[0][7])
        v.tensor_copy(RBH1[:, 0:2048], RB1[:, T2:T2 + 2048])
        v.drain()
        v.tensor_tensor(RBH1[:, 2048:4096], RB1[:, T2:T2 + 2048],
                        RBH1[:, 0:2048], ALU.subtract).then_inc(b.s("dv"), 1)
        for do in range(8):
            v.wait_ge(b.s("pe"), mm_pe[2][do])
            v.tensor_tensor(RB1[:, S4 + do * 256:S4 + do * 256 + 256],
                            PS[do][:, 0:256],
                            RB1[:, XL + do * 256:XL + do * 256 + 256], ALU.mult)
        v.drain()
        for do in range(8):
            sl = slice(do * T + t0, do * T + t0 + 256)
            ins = v.tensor_tensor(X[:, sl], X[:, sl],
                                  RB1[:, S4 + do * 256:S4 + do * 256 + 256],
                                  ALU.add)
        ins.then_inc(b.s("dv"), 1)


def sub_l0_attn_b(b, block, ln_dv, state):
    X, RB1, RB2 = b.t["X"], b.t["RB1"], b.t["RB2"]
    XQH, XQL = b.t["XQH"], b.t["XQL"]
    RBH1, RBH2, ZBH = b.t["RBH1"], b.t["RBH2"], b.t["ZBH"]
    WSF, WSH, SM = b.t["WSF"], b.t["WSH"], b.t["SMALL"]
    ONESM, ID = b.t["ONESM"], b.t["IDent"]
    SEL32, PICK32 = b.t["SEL32"], b.t["PICK32"]
    PS = [b.t[f"PS{i}"] for i in range(8)]
    wqkv = b.t["L0_attn_b_w_qkv"].rearrange("(c p) e -> p c e", p=128)
    wout = b.t["L0_attn_b_w_out"]
    scale = 0.17677669529663687
    VT, EB, RP, WS_ = 0, 2112, 4160, 6208
    em = E(b)
    prev_done = ln_dv
    prev_wo_pe = state.get("prev_wo_pe")
    state["ffn_attn_wo_pe"] = None

    par_tgt = b.cnts["dq"] + 48
    b.inc("dq", 48)

    @block.sync
    def _(sync):
        for i, nm in enumerate(["L0_attn_b_ln_gamma", "L0_attn_b_ln_beta",
                                "L0_attn_b_b_out"]):
            sync.dma_start(out=SM[:, 64 + 8 * i:72 + 8 * i],
                           in_=b.t[nm].rearrange("(c p) -> p c", p=128),
                           allow_slow_non_contiguous=True).then_inc(b.s("dq"), 16)

    @block.vector
    def _(v):
        v.memset(RB2[32:64, 0:1024], 0.0).then_inc(b.s("dv"), 1)
        v.memset(RB2[96:128, 0:1024], 0.0).then_inc(b.s("dv"), 1)
    b.inc("dv", 1)
    zb0_dv = b.inc("dv", 1)
    for g in range(ATTNB_QUADS):
        prev_done, prev_wo_pe = _attnb_quad(
            b, em, g, prev_done, prev_wo_pe, zb0_dv, scale,
            wqkv, wout, VT, EB, RP, WS_, state)
    em.emit(block)
    state["prev_wo_pe"] = prev_wo_pe
    if ATTNB_CUT >= 1:
        return prev_done

    scro = b.t["scr_out"]
    pe0, dv0, sc0 = b.cnts["pe"], b.cnts["dv"], b.cnts["sc"]
    acc_done0, acc_done1 = b.cnts["ac0"], b.cnts["ac1"]
    PS4, PS5, PS6, PS7 = PS[4], PS[5], PS[6], PS[7]
    ld_tgt = {}
    for tt in range(8):
        ld_tgt[tt] = b.inc("dx%d" % (tt % 2), 16)
    b.inc("dv", 32)
    b.inc("pe", 16)
    b.inc("sc", 16)

    @block.sync
    def _(sync):
        sync.wait_ge(b.s("ac0"), acc_done0)
        sync.wait_ge(b.s("ac1"), acc_done1)
        for tt in range(8):
            if tt >= 2:
                sync.wait_ge(b.s("dv"), dv0 + 4 * (tt - 2) + 4)
            sync.dma_start(
                out=RB1[:, (tt % 2) * 4096:(tt % 2) * 4096 + 2048].rearrange(
                    "p (c t) -> p c t", c=8),
                in_=scro.rearrange("(c p) t -> p c t", p=128)[
                    :, :, tt * 256:tt * 256 + 256],
            ).then_inc(b.s("dx%d" % (tt % 2)), 16)

    @block.vector
    def _(vector):
        for tt in range(8):
            a0 = (tt % 2) * 4096
            vector.wait_ge(b.s("dx%d" % (tt % 2)), ld_tgt[tt])
            if tt == 0:
                vector.wait_ge(b.s("dq"), par_tgt)
            for c in range(NCH):
                ins = vector.tensor_scalar_add(
                    RB1[:, a0 + c * 256:a0 + c * 256 + 256],
                    RB1[:, a0 + c * 256:a0 + c * 256 + 256],
                    SM[:, 80 + c:81 + c])
            ins.then_inc(b.s("dv"), 1)
            vector.wait_ge(b.s("pe"), pe0 + 2 * tt + 1)
            vector.tensor_single_scalar(
                RB2[0:1, 0:256], PS6[0:1, 0:256], 1.0 / D, ALU.mult)
            vector.drain()
            vector.tensor_tensor(
                RB2[0:1, 1536:1792], RB2[0:1, 0:256], RB2[0:1, 0:256], ALU.mult)
            vector.drain()
            vector.scalar_tensor_tensor(
                RB2[0:1, 512:768], PS7[0:1, 0:256], 1.0 / D,
                RB2[0:1, 1536:1792], ALU.mult, ALU.subtract).then_inc(b.s("dv"), 1)
            vector.wait_ge(b.s("sc"), sc0 + 2 * tt + 2)
            vector.reciprocal(
                RB2[0:1, 1024:1280], RB2[0:1, 512:768]).then_inc(b.s("dv"), 1)
            vector.wait_ge(b.s("pe"), pe0 + 2 * tt + 2)
            sls = []
            for c in range(NCH):
                sls.append((slice(c * T + tt * 256, c * T + tt * 256 + 256),
                            slice(a0 + c * 256, a0 + c * 256 + 256)))
            for sl, xl in sls:
                vector.tensor_tensor(RB1[:, xl], RB1[:, xl], PS4[:, 0:256],
                                     ALU.subtract)
            vector.drain()
            for sl, xl in sls:
                vector.tensor_tensor(RB1[:, xl], RB1[:, xl], PS5[:, 0:256],
                                     ALU.mult)
            vector.drain()
            for c in range(NCH):
                sl, xl = sls[c]
                vector.tensor_scalar(RB1[:, xl], RB1[:, xl],
                                     SM[:, 64 + c:65 + c], SM[:, 72 + c:73 + c],
                                     ALU.mult, ALU.add)
            vector.drain()
            for sl, xl in sls:
                ins = vector.tensor_tensor(X[:, sl], X[:, sl], RB1[:, xl],
                                           ALU.add)
            ins.then_inc(b.s("dv"), 1)

    @block.scalar
    def _(scalar):
        for tt in range(8):
            a0 = (tt % 2) * 4096
            scalar.wait_ge(b.s("dv"), dv0 + 4 * tt + 1)
            for c in range(NCH):
                ins = scalar.activation(
                    RB1[:, a0 + 2048 + c * 256:a0 + 2048 + c * 256 + 256],
                    RB1[:, a0 + c * 256:a0 + c * 256 + 256], ACT.Square)
            ins.then_inc(b.s("sc"), 1)
            scalar.wait_ge(b.s("dv"), dv0 + 4 * tt + 2)
            scalar.activation(
                RB2[0:1, 512:768], RB2[0:1, 512:768],
                ACT.Sqrt, bias=1e-6, scale=1.0).then_inc(b.s("sc"), 1)

    @block.tensor
    def _(tensor):
        for tt in range(8):
            a0 = (tt % 2) * 4096
            tensor.wait_ge(b.s("dv"), dv0 + 4 * tt + 1)
            if tt >= 1:
                tensor.wait_ge(b.s("dv"), dv0 + 4 * (tt - 1) + 2)
            for c in range(NCH):
                tensor.matmul(
                    PS6[0:1, 0:256], ONESM[:, 0:1],
                    RB1[:, a0 + c * 256:a0 + c * 256 + 256],
                    start=(c == 0), stop=(c == NCH - 1))
            tensor.wait_ge(b.s("sc"), sc0 + 2 * tt + 1)
            for c in range(NCH):
                mm = tensor.matmul(
                    PS7[0:1, 0:256], ONESM[:, 0:1],
                    RB1[:, a0 + 2048 + c * 256:a0 + 2048 + c * 256 + 256],
                    start=(c == 0), stop=(c == NCH - 1))
            mm.then_inc(b.s("pe"), 1)
            tensor.wait_ge(b.s("dv"), dv0 + 4 * tt + 3)
            if tt >= 1:
                tensor.wait_ge(b.s("dv"), dv0 + 4 * (tt - 1) + 4)
            tensor.matmul(PS4[:, 0:256], ONESM[0:1, 0:128], RB2[0:1, 0:256],
                          start=True, stop=True)
            mm = tensor.matmul(PS5[:, 0:256], ONESM[0:1, 0:128],
                               RB2[0:1, 1024:1280], start=True, stop=True)
            mm.then_inc(b.s("pe"), 1)
    return b.cnts["dv"]


def _attnb_quad(b, em, g, prev_done, prev_wo_pe, zb0_dv, scale, wqkv, wout,
                VT, EB, RP, WS_, state):
    X, RB1, RB2 = b.t["X"], b.t["RB1"], b.t["RB2"]
    XQH, XQL = b.t["XQH"], b.t["XQL"]
    RBH1, RBH2, ZBH = b.t["RBH1"], b.t["RBH2"], b.t["ZBH"]
    WSF, WSH, SM = b.t["WSF"], b.t["WSH"], b.t["SMALL"]
    ONESM, ID = b.t["ONESM"], b.t["IDent"]
    SEL32, PICK32 = b.t["SEL32"], b.t["PICK32"]
    PS = [b.t[f"PS{i}"] for i in range(8)]
    AVB = [0, 1, 6, 7]

    split_dv = {}
    wcol_consumed = {}
    specs = {}
    wo_ld = {}
    for j in range(5):
        slot = j % 2
        sf, sh = "wsf%d" % slot, "wsh%d" % slot
        if j < 3:
            srcs = [(wqkv[:, :, j * 1024 + 128 * g:j * 1024 + 128 * g + 128],
                     (0, 128))]
        else:
            e0 = 128 * g + (j - 3) * 64
            srcs = [(wout[e0:e0 + 32, :], (0, 32)),
                    (wout[e0 + 32:e0 + 64, :], (64, 96))]
        dq_tgt = b.inc(sf, 16 * len(srcs))
        if j < 3:
            gq_tgt = b.inc(sh, 16)
        else:
            gq_tgt = None
            wo_ld[j] = dq_tgt
        specs[j] = (slot, srcs, dq_tgt, gq_tgt, sf, sh)

        @em.on("sync")
        def _(sync, j=j, slot=slot, srcs=srcs, sf=sf):
            if j >= 2:
                sync.wait_ge(b.s("pe"), wcol_consumed[j - 2])
            elif prev_wo_pe is not None:
                sync.wait_ge(b.s("pe"), prev_wo_pe)
            for src, lanes in srcs:
                if j < 3:
                    dst = WSF[:, slot * 1024:slot * 1024 + 1024].rearrange(
                        "p (c d) -> p c d", c=8)
                else:
                    dst = WSF[lanes[0]:lanes[1], slot * 1024:slot * 1024 + 1024]
                sync.dma_start(out=dst, in_=src).then_inc(b.s(sf), 16)

        if j < 3:
            @em.on("gpsimd")
            def _(gp, j=j, slot=slot, srcs=srcs, sh=sh):
                if j >= 2:
                    gp.wait_ge(b.s("pe"), wcol_consumed[j - 2])
                elif prev_wo_pe is not None:
                    gp.wait_ge(b.s("pe"), prev_wo_pe)
                for src, lanes in srcs:
                    dst = WSH[:, slot * 2048:slot * 2048 + 1024].rearrange(
                        "p (c d) -> p c d", c=8)
                    gp.dma_start(out=dst, in_=src).then_inc(b.s(sh), 16)

    def append_split(j):
        slot, srcs, dq_tgt, gq_tgt, sf, sh = specs[j]

        @em.on("vector")
        def _(v, j=j, slot=slot, dq_tgt=dq_tgt, gq_tgt=gq_tgt, sf=sf, sh=sh):
            v.wait_ge(b.s(sf), dq_tgt)
            v.wait_ge(b.s(sh), gq_tgt)
            v.tensor_tensor(
                WSH[:, slot * 2048 + 1024:slot * 2048 + 2048],
                WSF[:, slot * 1024:slot * 1024 + 1024],
                WSH[:, slot * 2048:slot * 2048 + 1024],
                ALU.subtract).then_inc(b.s("dv"), 1)
        split_dv[j] = b.inc("dv", 1)

    append_split(0)
    append_split(1)

    evac_dv = {}

    def emit_proj(j, dst, vf32):
        @em.on("tensor")
        def _(tensor, j=j):
            tensor.wait_ge(b.s("dv"), split_dv[j])
            if j == 0:
                tensor.wait_ge(b.s("dv"), prev_done)
            else:
                tensor.wait_ge(b.s("dv"), evac_dv[j - 1])
            slot = j % 2
            for c in range(NCH):
                wh = WSH[:, slot * 2048 + c * 128:slot * 2048 + c * 128 + 128]
                wl = WSH[:, slot * 2048 + 1024 + c * 128:
                         slot * 2048 + 1024 + c * 128 + 128]
                for q in range(4):
                    xsl = slice(c * T + q * 512, c * T + q * 512 + 512)
                    tensor.matmul(PS[q][:, 0:512], wh, XQH[:, xsl],
                                  start=(c == 0), stop=False)
                    tensor.matmul(PS[q][:, 0:512], wh, XQL[:, xsl],
                                  start=False, stop=False)
                    mm = tensor.matmul(PS[q][:, 0:512], wl, XQH[:, xsl],
                                       start=False, stop=(c == NCH - 1))
            mm.then_inc(b.s("pe"), 1)
        pe_done = b.inc("pe", 1)
        wcol_consumed[j] = pe_done

        @em.on("vector")
        def _(v, j=j, pe_done=pe_done, dst=dst, vf32=vf32):
            v.wait_ge(b.s("pe"), pe_done)
            if vf32:
                for q in range(4):
                    ins = v.tensor_copy(
                        RB1[:, EB + q * 512:EB + q * 512 + 512], PS[q][:, 0:512])
            else:
                for q in range(4):
                    v.tensor_copy(dst[:, q * 512:q * 512 + 512], PS[q][:, 0:512])
                v.drain()
                for q in range(4):
                    ins = v.tensor_tensor(
                        dst[:, 2048 + q * 512:2048 + q * 512 + 512],
                        PS[q][:, 0:512], dst[:, q * 512:q * 512 + 512],
                        ALU.subtract)
            ins.then_inc(b.s("dv"), 1)
        return b.inc("dv", 1)

    evac_dv[0] = emit_proj(0, RBH2, False)
    append_split(2)
    evac_dv[1] = emit_proj(1, RBH1, False)
    evac_dv[2] = emit_proj(2, None, True)

    @em.on("vector")
    def _(v):
        v.wait_ge(b.s("dv"), prev_done)
        v.memset(RB1[:, VT:VT + 2112], 1.0).then_inc(b.s("dv"), 1)
    vt_ones = b.inc("dv", 1)

    tr_pe = [b.inc("pe") for _ in range(4)]
    vtr_dv = [b.inc("dv") for _ in range(4)]
    for rnd in range(4):
        @em.on("tensor")
        def _(tensor, rnd=rnd):
            if rnd == 0:
                tensor.wait_ge(b.s("dv"), evac_dv[2])
            if rnd >= 2:
                tensor.wait_ge(b.s("dv"), vtr_dv[rnd - 2])
            bank = PS[4 + rnd % 2]
            for kk in range(4):
                kb = rnd * 4 + kk
                mm = tensor.transpose(
                    bank[:, kk * 128:kk * 128 + 128],
                    RB1[:, EB + kb * 128:EB + kb * 128 + 128], ID[:, :])
            mm.then_inc(b.s("pe"), 1)

        @em.on("vector")
        def _(v, rnd=rnd):
            v.wait_ge(b.s("pe"), tr_pe[rnd])
            if rnd == 0:
                v.wait_ge(b.s("dv"), vt_ones)
            bank = PS[4 + rnd % 2]
            for kk in range(4):
                kb = rnd * 4 + kk
                for h in range(4):
                    off = VT + kb * 132 + h * 33
                    ins = v.tensor_copy(
                        RB1[:, off:off + 32],
                        bank[:, kk * 128 + h * 32:kk * 128 + h * 32 + 32])
            ins.then_inc(b.s("dv"), 1)

    sq_sc, nrm_pe, red_dv = {}, {}, {}
    for which in (0, 1):
        for qt in range(4):
            sq_sc[which, qt] = b.inc("sc")
    for which in (0, 1):
        for qt in range(4):
            nrm_pe[which, qt] = b.inc("pe")
    for which in (0, 1):
        for qt in range(4):
            red_dv[which, qt] = b.inc("dv")

    for which, SRC in ((0, RBH2), (1, RBH1)):
        @em.on("scalar")
        def _(scalar, which=which, SRC=SRC):
            scalar.wait_ge(b.s("dv"), evac_dv[which])
            for qt in range(4):
                if (which, qt) > (0, 1):
                    prev = (which, qt - 2) if qt >= 2 else (which - 1, qt + 2)
                    scalar.wait_ge(b.s("pe"), nrm_pe[prev])
                scalar.activation(
                    ZBH[:, (qt % 2) * 512:(qt % 2) * 512 + 512],
                    SRC[:, qt * 512:qt * 512 + 512], ACT.Square,
                ).then_inc(b.s("sc"), 1)

        @em.on("tensor")
        def _(tensor, which=which):
            for qt in range(4):
                tensor.wait_ge(b.s("sc"), sq_sc[which, qt])
                if (which, qt) > (0, 0):
                    prev = (which, qt - 1) if qt >= 1 else (which - 1, 3)
                    tensor.wait_ge(b.s("dv"), red_dv[prev])
                tensor.matmul(PS[6][0:4, 0:512], SEL32[:, 0:4],
                              ZBH[:, (qt % 2) * 512:(qt % 2) * 512 + 512],
                              start=True, stop=True).then_inc(b.s("pe"), 1)

        @em.on("vector")
        def _(v, which=which):
            for qt in range(4):
                v.wait_ge(b.s("pe"), nrm_pe[which, qt])
                v.reduce_max(SM[0:4, 12:13], PS[6][0:4, 0:512], axis=AX)
                v.drain()
                if qt == 0:
                    ins = v.tensor_copy(SM[0:4, 8 + which:9 + which],
                                        SM[0:4, 12:13])
                else:
                    ins = v.tensor_tensor(SM[0:4, 8 + which:9 + which],
                                          SM[0:4, 8 + which:9 + which],
                                          SM[0:4, 12:13], ALU.max)
                ins.then_inc(b.s("dv"), 1)

    @em.on("vector")
    def _(v):
        v.drain()
        v.tensor_tensor(SM[0:4, 10:11], SM[0:4, 8:9], SM[0:4, 9:10],
                        ALU.mult).then_inc(b.s("dv"), 1)
    cc_dv = b.inc("dv", 1)

    @em.on("scalar")
    def _(scalar):
        scalar.wait_ge(b.s("dv"), cc_dv)
        scalar.activation(SM[0:4, 11:12], SM[0:4, 10:11], ACT.Sqrt,
                          bias=0.0, scale=1.0404 * scale * scale,
                          ).then_inc(b.s("sc"), 1)
    csqrt_sc = b.inc("sc", 1)

    @em.on("vector")
    def _(v):
        v.wait_ge(b.s("sc"), csqrt_sc)
        v.tensor_single_scalar(SM[0:4, 11:12], SM[0:4, 11:12], -1.0, ALU.mult)
        v.drain()
        v.tensor_copy(b.t["NEGCH"][0:4, 0:1],
                      SM[0:4, 11:12]).then_inc(b.s("dv"), 1)
    negc_dv = b.inc("dv", 1)

    bias_pe = {}
    bias_dv = {}
    for hh in range(2):
        bias_pe[hh] = b.inc("pe")
        bias_dv[hh] = b.inc("dv")
    for hh in range(2):
        @em.on("tensor")
        def _(tensor, hh=hh):
            if hh == 0:
                tensor.wait_ge(b.s("dv"), negc_dv)
            else:
                tensor.wait_ge(b.s("dv"), bias_dv[0])
            tensor.matmul(PS[6][:, 0:1], PICK32[:, 256 * hh:256 * hh + 128],
                          b.t["NEGCH"][0:4, 0:1], start=True, stop=True)
            mm = tensor.matmul(PS[7][:, 0:1],
                               PICK32[:, 256 * hh + 128:256 * hh + 256],
                               b.t["NEGCH"][0:4, 0:1], start=True, stop=True)
            mm.then_inc(b.s("pe"), 1)

        @em.on("vector")
        def _(v, hh=hh):
            v.wait_ge(b.s("pe"), bias_pe[hh])
            v.tensor_copy(SM[:, 2 * hh:2 * hh + 1], PS[6][:, 0:1])
            v.tensor_copy(SM[:, 2 * hh + 1:2 * hh + 2],
                          PS[7][:, 0:1]).then_inc(b.s("dv"), 1)

    sc_exp, pe_scores, pe_av, dv_z = {}, {}, {}, {}
    last_wo_pe = None
    stg_last = {}
    for qt in range(4):
        for kb in range(16):
            pe_scores[qt, kb] = b.inc("pe")
            if kb >= 1:
                pe_av[qt, kb - 1] = b.inc("pe")
        pe_av[qt, 15] = b.inc("pe")
        for kb in range(16):
            sc_exp[qt, kb] = b.inc("sc")

        def _avmm_b(tensor, qt, kb):
            tensor.wait_ge(b.s("sc"), sc_exp[qt, kb])
            if kb == 0:
                if qt == 0:
                    tensor.wait_ge(b.s("dv"), vtr_dv[3])
                else:
                    tensor.wait_ge(b.s("dv"), dv_z[qt - 1])
            for h in range(4):
                vt_ap = RB1[:, VT + kb * 132 + h * 33:VT + kb * 132 + h * 33 + 33]
                e_ap = RB1[:, EB + h * 512:EB + h * 512 + 512]
                r0 = 64 * (h % 2)
                mm = tensor.matmul(PS[AVB[h]][r0:r0 + 33, 0:512],
                                   vt_ap, e_ap,
                                   start=(kb == 0), stop=(kb == 15),
                                   tile_position=(0, r0))
            mm.then_inc(b.s("pe"), 1)

        @em.on("tensor")
        def _(tensor, qt=qt, _avmm_b=_avmm_b):
            for kb in range(16):
                if qt == 0 and kb == 0:
                    tensor.wait_ge(b.s("dv"), evac_dv[0])
                    tensor.wait_ge(b.s("dv"), evac_dv[1])
                if kb >= 1:
                    tensor.wait_ge(b.s("sc"), sc_exp[qt, kb - 1])
                elif qt >= 1:
                    tensor.wait_ge(b.s("sc"), sc_exp[qt - 1, 15])
                    tensor.wait_ge(b.s("dv"), stg_last[qt - 1])
                for h in range(4):
                    bank = PS[2 + h]
                    r0 = 32 * h
                    kh = RBH1[r0:r0 + 32, kb * 128:kb * 128 + 128]
                    kl = RBH1[r0:r0 + 32, 2048 + kb * 128:2048 + kb * 128 + 128]
                    qh = RBH2[r0:r0 + 32, qt * 512:qt * 512 + 512]
                    ql = RBH2[r0:r0 + 32, 2048 + qt * 512:2048 + qt * 512 + 512]
                    tp = (r0, 0)
                    tensor.matmul(bank[:, 0:512], kh, qh, start=True, stop=False,
                                  tile_position=tp)
                    tensor.matmul(bank[:, 0:512], kh, ql, start=False,
                                  stop=False, tile_position=tp)
                    mm = tensor.matmul(bank[:, 0:512], kl, qh, start=False,
                                       stop=True, tile_position=tp)
                mm.then_inc(b.s("pe"), 1)
                if kb >= 1:
                    _avmm_b(tensor, qt, kb - 1)
            _avmm_b(tensor, qt, 15)

        @em.on("scalar")
        def _(scalar, qt=qt):
            for kb in range(16):
                scalar.wait_ge(b.s("pe"), pe_scores[qt, kb])
                if qt == 0 and kb == 0:
                    scalar.wait_ge(b.s("dv"), bias_dv[1])
                if kb >= 1:
                    scalar.wait_ge(b.s("pe"), pe_av[qt, kb - 1])
                for h in range(4):
                    ins = scalar.activation(
                        RB1[:, EB + h * 512:EB + h * 512 + 512],
                        PS[2 + h][:, 0:512], ACT.Exp, bias=SM[:, h:h + 1],
                        scale=scale)
                ins.then_inc(b.s("sc"), 1)

        rc_dv = b.inc("dv")
        mv_tgt = b.inc("sx0", 16)
        rp_pe = [b.inc("pe"), b.inc("pe")]
        rp_dv = [b.inc("dv"), b.inc("dv")]
        z_dv = b.inc("dv")
        dv_z[qt] = z_dv
        wo_pe = [b.inc("pe") for _ in range(8)]
        st_dv = [b.inc("dv") for _ in range(8)]
        stg_last[qt] = st_dv[7]
        last_wo_pe = wo_pe[7]

        @em.on("vector")
        def _(v, qt=qt):
            v.wait_ge(b.s("pe"), pe_av[qt, 15])
            v.reciprocal(RB2[32:33, 1024:1536], PS[AVB[0]][32:33, 0:512])
            v.reciprocal(RB2[96:97, 1024:1536], PS[AVB[1]][96:97, 0:512])
            v.reciprocal(RB2[32:33, 1536:2048], PS[AVB[2]][32:33, 0:512])
            v.reciprocal(RB2[96:97, 1536:2048],
                         PS[AVB[3]][96:97, 0:512]).then_inc(b.s("dv"), 1)

        @em.on("sync")
        def _(sync, qt=qt, mv_tgt=mv_tgt, rc_dv=rc_dv):
            sync.wait_ge(b.s("dv"), rc_dv)
            sync.dma_start(out=RB2[64:65, 1024:2048],
                           in_=RB2[96:97, 1024:2048]).then_inc(b.s("sx0"), 16)

        for hh in range(2):
            @em.on("tensor")
            def _(tensor, qt=qt, hh=hh, mv_tgt=mv_tgt, rc_dv=rc_dv,
                  rp_dv=rp_dv):
                if hh == 0:
                    tensor.wait_ge(b.s("dv"), rc_dv)
                    tensor.wait_ge(b.s("sx0"), mv_tgt)
                    tensor.wait_ge(b.s("sc"), sc_exp[qt, 15])
                else:
                    tensor.wait_ge(b.s("dv"), rp_dv[0])
                tensor.matmul(PS[2][:, 0:512], ONESM[32:33, 0:128],
                              RB2[32:33, 1024 + 512 * hh:1536 + 512 * hh],
                              start=True, stop=True)
                mm = tensor.matmul(PS[3][:, 0:512], ONESM[64:65, 0:128],
                                   RB2[64:65, 1024 + 512 * hh:1536 + 512 * hh],
                                   start=True, stop=True)
                mm.then_inc(b.s("pe"), 1)

            @em.on("vector")
            def _(v, qt=qt, hh=hh, rp_pe=rp_pe):
                v.wait_ge(b.s("pe"), rp_pe[hh])
                v.tensor_copy(RB1[:, RP + 1024 * hh:RP + 1024 * hh + 512],
                              PS[2][:, 0:512])
                ins = v.tensor_copy(
                    RB1[:, RP + 1024 * hh + 512:RP + 1024 * hh + 1024],
                    PS[3][:, 0:512])
                ins.then_inc(b.s("dv"), 1)

        @em.on("vector")
        def _(v, qt=qt, rp_dv=rp_dv):
            v.wait_ge(b.s("dv"), rp_dv[1])
            for h in range(4):
                hh = h // 2
                r0 = 64 * (h % 2)
                rep = RB1[r0:r0 + 32,
                          RP + 1024 * hh + 512 * (h % 2):
                          RP + 1024 * hh + 512 * (h % 2) + 512]
                ins = v.tensor_tensor(
                    RB2[r0:r0 + 32, 512 * hh:512 * hh + 512],
                    PS[AVB[h]][r0:r0 + 32, 0:512], rep, ALU.mult)
            ins.then_inc(b.s("dv"), 1)

        @em.on("tensor")
        def _(tensor, qt=qt, z_dv=z_dv, wo_pe=wo_pe, st_dv=st_dv):
            tensor.wait_ge(b.s("dv"), z_dv)
            if qt == 0:
                tensor.wait_ge(b.s("wsf1"), wo_ld[3])
                tensor.wait_ge(b.s("wsf0"), wo_ld[4])
                if g == 0:
                    tensor.wait_ge(b.s("dv"), zb0_dv)
            for do in range(8):
                bank = PS[4 + do % 2]
                if do >= 2:
                    tensor.wait_ge(b.s("dv"), st_dv[do - 2])
                for hh in range(2):
                    slot = 1 - hh
                    wo_t = WSF[:, slot * 1024 + do * 128:
                               slot * 1024 + do * 128 + 128]
                    z_t = RB2[:, 512 * hh:512 * hh + 512]
                    mm = tensor.matmul(bank[:, 0:512], wo_t, z_t,
                                       start=(hh == 0), stop=(hh == 1))
                mm.then_inc(b.s("pe"), 1)

        @em.on("vector")
        def _(v, qt=qt, wo_pe=wo_pe, g=g):
            for do in range(8):
                bank = PS[4 + do % 2]
                v.wait_ge(b.s("pe"), wo_pe[do])
                slot = do % 2
                v.wait_ge(b.s("ac%d" % slot),
                          state.get(("acst", g, qt, do), 0))
                ins = v.tensor_copy(
                    RB1[:, WS_ + slot * 512:WS_ + slot * 512 + 512],
                    bank[:, 0:512])
                ins.then_inc(b.s("dv"), 1)

        @em.on("gpsimd")
        def _(gp, qt=qt, g=g, st_dv=st_dv):
            scro = b.t["scr_out"]
            for do in range(8):
                slot = do % 2
                gp.wait_ge(b.s("dv"), st_dv[do])
                gp.dma_start(
                    out=scro[do * 128:do * 128 + 128, qt * 512:qt * 512 + 512],
                    in_=RB1[:, WS_ + slot * 512:WS_ + slot * 512 + 512],
                    accum_op=(ALU.add if g > 0 else ALU.bypass),
                ).then_inc(b.s("ac%d" % slot), 16)
        for do in range(8):
            slot = do % 2
            state[("acst_next", g, qt, do)] = b.inc("ac%d" % slot, 16)

    wcol_consumed[3] = last_wo_pe
    wcol_consumed[4] = last_wo_pe
    for qt in range(4):
        for do in range(8):
            slot = do % 2
            if do >= 2:
                pq, pd = qt, do - 2
            elif qt >= 1:
                pq, pd = qt - 1, 6 + do
            else:
                pq = pd = None
            if pq is not None:
                state[("acst", g, qt, do)] = state[("acst_next", g, pq, pd)]
            else:
                state[("acst", g, qt, do)] = state.get(
                    ("acst_next", g - 1, 3, 6 + do), 0)
    return b.cnts["dv"], last_wo_pe


def sub_l1_attn_a(b, block, ln_dv, state):
    SM = b.t["SMALL"]
    if not state.get("l1a_prep_done"):
        blocks = []
        for dc in range(24):
            blocks.append((b.t["L1_attn_a_qkv_w"][dc * 128:dc * 128 + 128, :],
                           dc * 128))
        for dc in range(8):
            blocks.append((b.t["L1_attn_a_fc_w"][dc * 128:dc * 128 + 128, :],
                           3072 + dc * 128))
        state["l1a_prep_done"] = prep_T(
            b, block, state, blocks,
            b.t["scr_l1a"].rearrange("(h c d) -> h c d", h=2, c=1024))
    prep_done = state["l1a_prep_done"]
    scr = b.t["scr_l1a"].rearrange("(h c d) -> h c d", h=2, c=1024)

    fcb_tgt = b.inc("dq", 16)

    @block.sync
    def _(sync):
        sync.wait_ge(b.s("wsh0"), prep_done[0])
        sync.wait_ge(b.s("wsh1"), prep_done[1])
        sync.dma_start(out=SM[:, 96:104],
                       in_=b.t["L1_attn_a_fc_b"].rearrange("(c p) -> p c", p=128),
                       allow_slow_non_contiguous=True).then_inc(b.s("dq"), 16)

    def col_ap(e_off, width):
        return (scr[0, :, e_off:e_off + width].rearrange("(c q) d -> q c d", q=128),
                scr[1, :, e_off:e_off + width].rearrange("(c q) d -> q c d", q=128))

    def wo_ap2(hp, e):
        hd0 = (2 * hp + e) * 64
        return (scr[0, hd0:hd0 + 64, 3072:4096],
                scr[1, hd0:hd0 + 64, 3072:4096])

    def pair_setup(em, hp):
        qb_tgt = b.inc("dq", 16 * 7)
        hm_pe = b.inc("pe")
        hm_dv = b.inc("dv")

        pd = state.get("prev_pair_done")

        @em.on("sync")
        def _(sync, hp=hp, pd=pd):
            if hp == 0:
                sync.wait_ge(b.s("dq"), fcb_tgt)
            if pd is not None and hp > 0:
                sync.wait_ge(b.s("dv"), pd)
            for j in range(3):
                for e in (0, 1):
                    off = (2 * hp + e) * 192 + j * 64
                    sync.dma_start(
                        out=SM[e * 64:e * 64 + 64, 89 + j:90 + j],
                        in_=b.t["L1_attn_a_qkv_b"][off:off + 64].rearrange(
                            "(k o) -> k o", o=1),
                    ).then_inc(b.s("dq"), 16)
            sync.dma_start(
                out=SM[0:2, 92:93],
                in_=b.t["L1_attn_a_head_mask"][2 * hp:2 * hp + 2].rearrange(
                    "(k o) -> k o", o=1),
            ).then_inc(b.s("dq"), 16)

        @em.on("tensor")
        def _(tensor, hp=hp, prev_hm=state.get("l1a_prev_hm")):
            tensor.wait_ge(b.s("dq"), qb_tgt)
            if prev_hm is not None:
                tensor.wait_ge(b.s("dv"), prev_hm)
            tensor.matmul(b.t["PS6"][:, 0:1], b.t["SELH2"][:, 0:128],
                          SM[0:2, 92:93], start=True,
                          stop=True).then_inc(b.s("pe"), 1)

        @em.on("vector")
        def _(v, hp=hp):
            v.wait_ge(b.s("pe"), hm_pe)
            v.tensor_copy(SM[:, 93:94], b.t["PS6"][:, 0:1]).then_inc(b.s("dv"), 1)
        state["l1a_prev_hm"] = hm_dv
        return [("dq", qb_tgt), ("dv", hm_dv)]

    RB2 = b.t["RB2"]

    def evac_mod(v, hp, j, q, src):
        dstm = RB2[:, q * 512:q * 512 + 512]
        v.tensor_scalar(dstm, src, SM[:, 89 + j:90 + j], SM[:, 93:94],
                        ALU.add, ALU.mult)
        return dstm

    cfg = {
        "wq_ap": lambda hp, e: col_ap((2 * hp + e) * 192 + 0, 64),
        "wk_ap": lambda hp, e: col_ap((2 * hp + e) * 192 + 64, 64),
        "wv_ap": lambda hp, e: col_ap((2 * hp + e) * 192 + 128, 64),
        "wo_ap": wo_ap2,
        "scale": 0.125,
        "internal_residual": False,
        "ln_dv": ln_dv,
        "presplit": True,
        "ct_rows": True,
        "pair_setup": pair_setup,
        "evac_mod": evac_mod,
        "xacc_bias": lambda do: SM[:, 96 + do:97 + do],
    }
    return attn64(b, block, cfg, state)


def _round5(v, w, scr, DORA):
    v.tensor_scalar(scr, w, C23, C23, ALU.add, ALU.subtract)
    v.tensor_scalar(w, w, C52, C52, ALU.add, ALU.subtract)
    v.drain()
    v.tensor_tensor(w, w, scr, ALU.subtract)
    v.drain()
    v.scalar_tensor_tensor(w, w, DORA[:, 0:1], scr, ALU.mult, ALU.add)
    v.drain()


def prep_bitnet(b, block, state, ln_dv):
    WSF, WSH, ID, SM = b.t["WSF"], b.t["WSH"], b.t["IDent"], b.t["SMALL"]
    ONESM, DORA, RB2, ZBH = (b.t["ONESM"], b.t["DORA"], b.t["RB2"],
                             b.t["ZBH"])
    PS = [b.t[f"PS{i}"] for i in range(8)]
    gT = b.t["scr_gT"].rearrange("(r c) -> r c", c=4096)
    uT = b.t["scr_uT"].rearrange("(r c) -> r c", c=4096)
    dT = b.t["scr_dT"].rearrange("(r c) -> r c", c=1024)
    em = E(b)
    blocks = []
    for dc in range(32):
        blocks.append((b.t["L1_ffn_w_gate"][dc * 128:dc * 128 + 128, :],
                       gT[:, 128 * dc:128 * dc + 128], 0))
    for dc in range(32):
        blocks.append((b.t["L1_ffn_w_up"][dc * 128:dc * 128 + 128, :],
                       uT[:, 128 * dc:128 * dc + 128], 1))
    for ob in range(8):
        for isub in range(4):
            blocks.append((
                b.t["L1_ffn_w_down"][ob * 128:ob * 128 + 128,
                                     isub * 1024:isub * 1024 + 1024],
                dT[1024 * isub:1024 * isub + 1024, 128 * ob:128 * ob + 128],
                2))
    n = len(blocks)
    MATN = {0: 4096.0 * 1024, 1: 4096.0 * 1024, 2: 1024.0 * 4096}
    prev_done = state.get("prev_pair_done")
    pw = state.get("prev_wo_pe")

    ld1, abs_dv, sum_pe = {}, {}, {}
    for i in range(n):
        ld1[i] = b.inc("wsf%d" % (i % 2), 16)
    for i in range(n):
        abs_dv[i] = b.inc("dv")
    for i in range(n):
        sum_pe[i] = b.inc("pe")
    mstat_sc, mstat_dv, mrep_pe, mrep_dv = {}, {}, {}, {}
    for m in range(3):
        mstat_sc[m] = b.inc("sc")
    for m in range(3):
        mstat_dv[m] = b.inc("dv")
    for m in range(3):
        mrep_pe[m] = b.inc("pe")
    for m in range(3):
        mrep_dv[m] = b.inc("dv")

    @em.on("sync")
    def _(sync):
        for i in range(n):
            slot = i % 2
            if i >= 2:
                sync.wait_ge(b.s("dv"), abs_dv[i - 2])
            else:
                if pw is not None:
                    sync.wait_ge(b.s("pe"), pw)
                if prev_done is not None:
                    sync.wait_ge(b.s("dv"), prev_done)
            sync.dma_start(out=WSF[:, slot * 1024:slot * 1024 + 1024],
                           in_=blocks[i][0]).then_inc(b.s("wsf%d" % slot), 16)

    @em.on("vector")
    def _(v):
        v.wait_ge(b.s("dv"), ln_dv)
        for i in range(n):
            slot = i % 2
            v.wait_ge(b.s("wsf%d" % slot), ld1[i])
            if i >= 1:
                v.wait_ge(b.s("pe"), sum_pe[i - 1])
            v.scalar_tensor_tensor(
                RB2[:, 0:1024], WSF[:, slot * 1024:slot * 1024 + 1024], -1.0,
                WSF[:, slot * 1024:slot * 1024 + 1024],
                ALU.mult, ALU.max).then_inc(b.s("dv"), 1)
        for m in range(3):
            v.wait_ge(b.s("sc"), mstat_sc[m])
            v.tensor_scalar(SM[0:1, 104 + m:105 + m],
                            SM[0:1, 104 + m:105 + m],
                            1.0 / MATN[m], 1e-5, ALU.mult, ALU.max)
            v.drain()
            v.reciprocal(SM[0:1, 108 + m:109 + m],
                         SM[0:1, 104 + m:105 + m]).then_inc(b.s("dv"), 1)
        for m in range(3):
            v.wait_ge(b.s("pe"), mrep_pe[m])
            v.tensor_copy(SM[:, 108 + m:109 + m], PS[7][:, 0:1])
            v.tensor_copy(SM[:, 111 + m:112 + m],
                          PS[7][:, 1:2]).then_inc(b.s("dv"), 1)

    @em.on("tensor")
    def _(tensor):
        for i in range(n):
            m = blocks[i][2]
            first = (i == 0 or blocks[i - 1][2] != m)
            last = (i == n - 1 or blocks[i + 1][2] != m)
            tensor.wait_ge(b.s("dv"), abs_dv[i])
            if first and m > 0:
                tensor.wait_ge(b.s("sc"), mstat_sc[m - 1])
            for cg in range(2):
                mm = tensor.matmul(
                    PS[6][0:1, 0:512], ONESM[:, 0:1],
                    RB2[:, cg * 512:cg * 512 + 512],
                    start=(first and cg == 0), stop=(last and cg == 1))
            mm.then_inc(b.s("pe"), 1)
        for m in range(3):
            tensor.wait_ge(b.s("dv"), mstat_dv[m])
            if m >= 1:
                tensor.wait_ge(b.s("dv"), mrep_dv[m - 1])
            tensor.matmul(PS[7][:, 0:1], ONESM[0:1, 0:128],
                          SM[0:1, 108 + m:109 + m], start=True, stop=True)
            tensor.matmul(PS[7][:, 1:2], ONESM[0:1, 0:128],
                          SM[0:1, 104 + m:105 + m],
                          start=True, stop=True).then_inc(b.s("pe"), 1)

    @em.on("scalar")
    def _(scalar):
        for m in range(3):
            scalar.wait_ge(b.s("pe"), sum_pe[{0: 31, 1: 63, 2: 95}[m]])
            scalar.activation(ZBH[0:1, 0:512], PS[6][0:1, 0:512],
                              ACT.Identity,
                              accum_out=SM[0:1, 104 + m:105 + m],
                              ).then_inc(b.s("sc"), 1)

    ld2, q_dv, tr_pe, cp_dv, st_tgt = {}, {}, {}, {}, {}
    for i in range(n):
        ld2[i] = b.inc("wsf%d" % (i % 2), 16)
    for i in range(n):
        q_dv[i] = b.inc("dv")
        cp_dv[i] = b.inc("dv")
        cp_dv[i, 1] = b.inc("dv")
    for i in range(n):
        tr_pe[i] = b.inc("pe")
        tr_pe[i, 1] = b.inc("pe")
    for i in range(n):
        st_tgt[i] = b.inc("wsh%d" % (i % 2), 16)

    @em.on("sync")
    def _(sync):
        for i in range(n):
            slot = i % 2
            if i >= 2:
                sync.wait_ge(b.s("pe"), tr_pe[i - 2, 1])
            else:
                sync.wait_ge(b.s("dv"), mrep_dv[2])
            sync.dma_start(out=WSF[:, slot * 1024:slot * 1024 + 1024],
                           in_=blocks[i][0]).then_inc(b.s("wsf%d" % slot), 16)


    @em.on("tensor")
    def _(tensor):
        for i in range(n):
            slot = i % 2
            bank = PS[slot]
            tensor.wait_ge(b.s("dv"), q_dv[i])
            if i >= 2:
                tensor.wait_ge(b.s("dv"), cp_dv[i - 2, 1])
            for half in range(2):
                if half == 1:
                    tensor.wait_ge(b.s("dv"), cp_dv[i])
                for cc in range(4 * half, 4 * half + 4):
                    mm = tensor.transpose(
                        bank[:, (cc % 4) * 128:(cc % 4) * 128 + 128],
                        WSF[:, slot * 1024 + cc * 128:
                            slot * 1024 + cc * 128 + 128],
                        ID[:, :])
                mm.then_inc(b.s("pe"), 1)

    @em.on("vector")
    def _(v):
        for i in range(n):
            slot = i % 2
            m = blocks[i][2]
            bank = PS[slot]
            w = WSF[:, slot * 1024:slot * 1024 + 1024]
            v.wait_ge(b.s("wsf%d" % slot), ld2[i])
            if i == 0:
                v.wait_ge(b.s("pe"), sum_pe[n - 1])
            if i >= 2:
                v.wait_ge(b.s("wsh%d" % slot), st_tgt[i - 2])
            v.tensor_scalar(w, w, SM[:, 108 + m:109 + m], 1.0,
                            ALU.mult, ALU.mult)
            v.drain()
            _round5(v, w, RB2[:, 0:1024], DORA)
            v.tensor_scalar(w, w, 1.0, -1.0, ALU.min,
                            ALU.max).then_inc(b.s("dv"), 1)
            for half in range(2):
                v.wait_ge(b.s("pe"), tr_pe[i] if half == 0 else tr_pe[i, 1])
                for cc in range(4 * half, 4 * half + 4):
                    ins = v.tensor_copy(
                        WSH[:, slot * 2048 + cc * 128:
                            slot * 2048 + cc * 128 + 128],
                        bank[:, (cc % 4) * 128:(cc % 4) * 128 + 128])
                ins.then_inc(b.s("dv"), 1)

    @em.on("gpsimd")
    def _(gp):
        for i in range(n):
            slot = i % 2
            gp.wait_ge(b.s("dv"), cp_dv[i, 1])
            src_s = WSH[:, slot * 2048:slot * 2048 + 1024].rearrange(
                "q (c d) -> q c d", c=8)
            dst = blocks[i][1].rearrange("(c q) d -> q c d", q=128)
            gp.dma_start(out=dst, in_=src_s).then_inc(b.s("wsh%d" % slot), 16)
    em.emit(block)

    fin_dv = b.inc("dv")

    @block.vector
    def _(v):
        v.wait_ge(b.s("dv"), mrep_dv[2])
        v.tensor_tensor(SM[:, 114:115], SM[:, 111:112], SM[:, 112:113],
                        ALU.mult).then_inc(b.s("dv"), 1)
    state["bit_prep"] = (st_tgt[n - 2], st_tgt[n - 1], fin_dv,
                         tr_pe[n - 1, 1])


def sub_l1_ffn(b, block, ln_dv, state):
    X, RB1, RB2, ZBH = b.t["X"], b.t["RB1"], b.t["RB2"], b.t["ZBH"]
    XQH, XQL = b.t["XQH"], b.t["XQL"]
    WSH, SM, ONESM, DORA = b.t["WSH"], b.t["SMALL"], b.t["ONESM"], b.t["DORA"]
    PICK32 = b.t["PICK32"]
    PS = [b.t[f"PS{i}"] for i in range(8)]
    gT = b.t["scr_gT"].rearrange("(r c) -> r c", c=4096)
    uT = b.t["scr_uT"].rearrange("(r c) -> r c", c=4096)
    dT = b.t["scr_dT"].rearrange("(r c) -> r c", c=1024)
    if not state.get("bit_prep"):
        prep_bitnet(b, block, state, ln_dv)
    prep0, prep1, fin_dv, prep_tr = state["bit_prep"]
    em = E(b)
    AX_C = mybir.AxisListType.C
    ABSC = 7168

    q1_add_dv, q1_abs_dv, q1_red_gp, q1_row_dv, q1_rep_pe, q1_u_dv = \
        {}, {}, {}, {}, {}, {}
    q1_mg_dv = {}
    for tc in range(4):
        for c in range(8):
            q1_add_dv[tc, c] = b.inc("dv")
            q1_abs_dv[tc, c] = b.inc("dv")
            if c >= 1:
                q1_mg_dv[tc, c] = b.inc("dv")
        q1_row_dv[tc] = b.inc("dv")
        for c in range(8):
            q1_u_dv[tc, c] = b.inc("dv")
    for tc in range(4):
        for c in range(8):
            q1_red_gp[tc, c] = b.inc("gp")
    for tc in range(4):
        q1_rep_pe[tc] = b.inc("pe")

    @em.on("vector")
    def _(v):
        for tc in range(4):
            if tc == 0:
                v.wait_ge(b.s("dv"), ln_dv)
                v.wait_ge(b.s("pe"), prep_tr)
            for c in range(8):
                xsl = slice(c * T + tc * 512, c * T + tc * 512 + 512)
                v.tensor_tensor(RB1[:, c * 512:c * 512 + 512],
                                XQH[:, xsl], XQL[:, xsl],
                                ALU.add).then_inc(b.s("dv"), 1)
                v.drain()
                v.scalar_tensor_tensor(
                    RB1[:, ABSC:ABSC + 512], RB1[:, c * 512:c * 512 + 512],
                    -1.0, RB1[:, c * 512:c * 512 + 512],
                    ALU.mult, ALU.max).then_inc(b.s("dv"), 1)
                if c >= 1:
                    v.wait_ge(b.s("gp"), q1_red_gp[tc, c])
                    v.tensor_tensor(RB1[0:1, 4096:4608],
                                    RB1[0:1, 4096:4608],
                                    RB1[0:1, 4608:5120],
                                    ALU.max).then_inc(b.s("dv"), 1)
                    v.drain()
            v.wait_ge(b.s("gp"), q1_red_gp[tc, 7])
            v.wait_ge(b.s("dv"), q1_mg_dv[tc, 7])
            v.tensor_single_scalar(RB1[0:1, 4096:4608], RB1[0:1, 4096:4608],
                                   1e-5, ALU.max)
            v.drain()
            v.tensor_single_scalar(RB2[0:1, tc * 512:tc * 512 + 512],
                                   RB1[0:1, 4096:4608], 1.0 / 127.0, ALU.mult)
            v.reciprocal(RB1[0:1, 4608:5120], RB1[0:1, 4096:4608])
            v.drain()
            v.tensor_single_scalar(RB1[0:1, 4608:5120], RB1[0:1, 4608:5120],
                                   127.0, ALU.mult).then_inc(b.s("dv"), 1)
            v.wait_ge(b.s("pe"), q1_rep_pe[tc])
            for c in range(8):
                xf = RB1[:, c * 512:c * 512 + 512]
                xsl = slice(c * T + tc * 512, c * T + tc * 512 + 512)
                v.tensor_tensor(xf, xf, PS[5][:, 0:512], ALU.mult)
                v.drain()
                _round5(v, xf, RB1[:, ABSC:ABSC + 512], DORA)
                v.tensor_scalar(xf, xf, 127.0, -128.0, ALU.min, ALU.max)
                v.drain()
                v.tensor_copy(XQH[:, xsl], xf).then_inc(b.s("dv"), 1)

    @em.on("gpsimd")
    def _(gp):
        for tc in range(4):
            for c in range(8):
                gp.wait_ge(b.s("dv"), q1_abs_dv[tc, c])
                if c == 0:
                    gp.reduce_max(RB1[0:1, 4096:4608],
                                  RB1[:, ABSC:ABSC + 512],
                                  axis=AX_C).then_inc(b.s("gp"), 1)
                else:
                    if c >= 2:
                        gp.wait_ge(b.s("dv"), q1_mg_dv[tc, c - 1])
                    gp.reduce_max(RB1[0:1, 4608:5120],
                                  RB1[:, ABSC:ABSC + 512],
                                  axis=AX_C).then_inc(b.s("gp"), 1)

    @em.on("tensor")
    def _(tensor):
        for tc in range(4):
            tensor.wait_ge(b.s("dv"), q1_row_dv[tc])
            if tc >= 1:
                tensor.wait_ge(b.s("dv"), q1_u_dv[tc - 1, 7])
            tensor.matmul(PS[5][:, 0:512], ONESM[0:1, 0:128],
                          RB1[0:1, 4608:5120], start=True,
                          stop=True).then_inc(b.s("pe"), 1)

    rep1_pe, guld, gu_pe = {}, {}, {}
    gd_dv, gu_dv, q2_abs_dv, q2_red_gp, q2_row_dv, q2_rep_pe = \
        {}, {}, {}, {}, {}, {}
    u2_dv, dld, d_pe, d_dv, sg_sc = {}, {}, {}, {}, {}
    dq2_rep_pe, q2_mg_dv = {}, {}
    for tt in range(4):
        rep1_pe[tt] = b.inc("pe")
        for oc in range(32):
            guld[tt, oc] = b.inc("wsf%d" % (oc % 2), 32)
        for oc in range(32):
            gu_pe[tt, oc] = b.inc("pe")
        q2_rep_pe[tt] = b.inc("pe")
        dq2_rep_pe[tt] = b.inc("pe")
        for do in range(8):
            d_pe[tt, do] = b.inc("pe")
        for oc in range(32):
            gd_dv[tt, oc] = b.inc("dv")
            gu_dv[tt, oc] = b.inc("dv")
            q2_abs_dv[tt, oc] = b.inc("dv")
            if oc >= 1:
                q2_mg_dv[tt, oc] = b.inc("dv")
        q2_row_dv[tt] = b.inc("dv")
        for oc in range(32):
            u2_dv[tt, oc] = b.inc("dv")
        for do in range(8):
            d_dv[tt, do] = b.inc("dv")
        for oc in range(32):
            q2_red_gp[tt, oc] = b.inc("gp")
        for oc in range(32):
            sg_sc[tt, oc] = b.inc("sc")
        for do in range(8):
            dld[tt, do, 0] = b.inc("wsf0", 16)
            dld[tt, do, 1] = b.inc("wsf1", 16)

    @em.on("sync")
    def _(sync):
        for tt in range(4):
            for oc in range(32):
                slot = oc % 2
                if tt == 0 and oc < 2:
                    sync.wait_ge(b.s("wsh0"), prep0)
                    sync.wait_ge(b.s("wsh1"), prep1)
                    if state.get("prev_wo_pe") is not None:
                        sync.wait_ge(b.s("pe"), state["prev_wo_pe"])
                elif oc < 2:
                    sync.wait_ge(b.s("pe"), d_pe[tt - 1, 7])
                else:
                    sync.wait_ge(b.s("pe"), gu_pe[tt, oc - 2])
                sync.dma_start(
                    out=WSH[:, slot * 2048:slot * 2048 + 1024].rearrange(
                        "p (c d) -> p c d", c=8),
                    in_=gT[:, oc * 128:oc * 128 + 128].rearrange(
                        "(c p) d -> p c d", p=128),
                ).then_inc(b.s("wsf%d" % slot), 16)
                sync.dma_start(
                    out=WSH[:, slot * 2048 + 1024:slot * 2048 + 2048
                            ].rearrange("p (c d) -> p c d", c=8),
                    in_=uT[:, oc * 128:oc * 128 + 128].rearrange(
                        "(c p) d -> p c d", p=128),
                ).then_inc(b.s("wsf%d" % slot), 16)
            for do in range(8):
                if do == 0:
                    sync.wait_ge(b.s("pe"), gu_pe[tt, 31])
                else:
                    sync.wait_ge(b.s("pe"), d_pe[tt, do - 1])
                for half in range(2):
                    sync.dma_start(
                        out=WSH[:, half * 2048:half * 2048 + 2048].rearrange(
                            "p (c d) -> p c d", c=16),
                        in_=dT[half * 2048:half * 2048 + 2048,
                               do * 128:do * 128 + 128].rearrange(
                            "(c p) d -> p c d", p=128),
                    ).then_inc(b.s("wsf%d" % half), 16)

    @em.on("tensor")
    def _(tensor):
        for tt in range(4):
            tensor.wait_ge(b.s("dv"), q1_row_dv[tt])
            if tt >= 1:
                tensor.wait_ge(b.s("dv"), d_dv[tt - 1, 7])
            tensor.matmul(PS[2][:, 0:512], ONESM[0:1, 0:128],
                          RB2[0:1, tt * 512:tt * 512 + 512], start=True,
                          stop=True).then_inc(b.s("pe"), 1)
            for oc in range(32):
                slot = oc % 2
                tensor.wait_ge(b.s("wsf%d" % slot), guld[tt, oc])
                if oc == 0:
                    tensor.wait_ge(b.s("dv"), q1_u_dv[tt, 7])
                    tensor.wait_ge(b.s("dv"), fin_dv)
                prev = (tt, oc - 1) if oc >= 1 else (tt - 1, 31)
                if prev[0] >= 0:
                    tensor.wait_ge(b.s("dv"), gu_dv[prev])
                for kc in range(8):
                    u_ap = XQH[:, kc * T + tt * 512:kc * T + tt * 512 + 512]
                    tensor.matmul(
                        PS[0][:, 0:512],
                        WSH[:, slot * 2048 + kc * 128:
                            slot * 2048 + kc * 128 + 128],
                        u_ap, start=(kc == 0), stop=(kc == 7))
                for kc in range(8):
                    u_ap = XQH[:, kc * T + tt * 512:kc * T + tt * 512 + 512]
                    mm = tensor.matmul(
                        PS[1][:, 0:512],
                        WSH[:, slot * 2048 + 1024 + kc * 128:
                            slot * 2048 + 1024 + kc * 128 + 128],
                        u_ap, start=(kc == 0), stop=(kc == 7))
                mm.then_inc(b.s("pe"), 1)
            tensor.wait_ge(b.s("dv"), q2_row_dv[tt])
            tensor.matmul(PS[2][:, 0:512], ONESM[0:1, 0:128],
                          RB1[0:1, 4608:5120], start=True,
                          stop=True).then_inc(b.s("pe"), 1)
            tensor.wait_ge(b.s("dv"), u2_dv[tt, 31])
            tensor.matmul(PS[2][:, 0:512], ONESM[0:1, 0:128],
                          RB1[0:1, 5120 + tt * 512:5632 + tt * 512],
                          start=True, stop=True).then_inc(b.s("pe"), 1)
            for do in range(8):
                bank = PS[6 + do % 2]
                tensor.wait_ge(b.s("wsf0"), dld[tt, do, 0])
                tensor.wait_ge(b.s("wsf1"), dld[tt, do, 1])
                if do == 0:
                    tensor.wait_ge(b.s("dv"), u2_dv[tt, 31])
                if do >= 2:
                    tensor.wait_ge(b.s("dv"), d_dv[tt, do - 2])
                for kc in range(32):
                    mm = tensor.matmul(
                        bank[:, 0:512],
                        WSH[:, kc * 128:kc * 128 + 128],
                        XQL[:, kc * 512:kc * 512 + 512],
                        start=(kc == 0), stop=(kc == 31))
                mm.then_inc(b.s("pe"), 1)

    @em.on("scalar")
    def _(scalar):
        for tt in range(4):
            for oc in range(32):
                scalar.wait_ge(b.s("dv"), gd_dv[tt, oc])
                scalar.activation(RB1[:, 1024:1536], RB1[:, 2048:2560],
                                  ACT.Sigmoid).then_inc(b.s("sc"), 1)

    @em.on("gpsimd")
    def _(gp):
        for tt in range(4):
            for oc in range(32):
                gp.wait_ge(b.s("dv"), q2_abs_dv[tt, oc])
                if oc == 0:
                    gp.reduce_max(ZBH[0:1, 0:512], RB1[:, 7168:7680],
                                  axis=AX_C).then_inc(b.s("gp"), 1)
                else:
                    if oc >= 2:
                        gp.wait_ge(b.s("dv"), q2_mg_dv[tt, oc - 1])
                    gp.reduce_max(ZBH[0:1, 512:1024], RB1[:, 7168:7680],
                                  axis=AX_C).then_inc(b.s("gp"), 1)

    @em.on("vector")
    def _(v):
        for tt in range(4):
            for oc in range(32):
                v.wait_ge(b.s("pe"), gu_pe[tt, oc])
                if oc == 0:
                    v.wait_ge(b.s("pe"), rep1_pe[tt])
                    v.drain()
                    v.tensor_copy(RB1[:, 1536:2048], PS[2][:, 0:512])
                    v.drain()
                gucol = XQL[:, oc * 512:oc * 512 + 512]
                rep_sb = RB1[:, 1536:2048]
                ga, ua = RB1[:, 0:512], RB1[:, 512:1024]
                gd, pp = RB1[:, 2048:2560], RB1[:, 2560:3072]
                v.tensor_copy(ga, PS[0][:, 0:512])
                v.tensor_copy(ua, PS[1][:, 0:512])
                v.drain()
                v.tensor_tensor(gd, ga, rep_sb, ALU.mult)
                v.drain()
                v.tensor_scalar(gd, gd, SM[:, 111:112], 1.0,
                                ALU.mult, ALU.mult).then_inc(b.s("dv"), 1)
                v.tensor_tensor(pp, ga, ua, ALU.mult)
                v.drain()
                v.tensor_tensor(pp, pp, rep_sb, ALU.mult)
                v.drain()
                v.tensor_tensor(pp, pp, rep_sb, ALU.mult)
                v.drain()
                v.tensor_scalar(pp, pp, SM[:, 114:115], 1.0,
                                ALU.mult, ALU.mult)
                v.wait_ge(b.s("sc"), sg_sc[tt, oc])
                v.drain()
                v.tensor_tensor(gucol, pp, RB1[:, 1024:1536],
                                ALU.mult).then_inc(b.s("dv"), 1)
                v.drain()
                v.scalar_tensor_tensor(
                    RB1[:, 7168:7680], gucol, -1.0, gucol,
                    ALU.mult, ALU.max).then_inc(b.s("dv"), 1)
                if oc >= 1:
                    v.wait_ge(b.s("gp"), q2_red_gp[tt, oc])
                    v.tensor_tensor(ZBH[0:1, 0:512], ZBH[0:1, 0:512],
                                    ZBH[0:1, 512:1024],
                                    ALU.max).then_inc(b.s("dv"), 1)
                    v.drain()
            v.wait_ge(b.s("gp"), q2_red_gp[tt, 31])
            v.wait_ge(b.s("dv"), q2_mg_dv[tt, 31])
            v.tensor_single_scalar(RB1[0:1, 4096:4608], ZBH[0:1, 0:512],
                                   1e-5, ALU.max)
            v.drain()
            v.tensor_single_scalar(
                RB1[0:1, 5120 + tt * 512:5632 + tt * 512],
                RB1[0:1, 4096:4608], 1.0 / 127.0, ALU.mult)
            v.reciprocal(RB1[0:1, 4608:5120], RB1[0:1, 4096:4608])
            v.drain()
            v.tensor_single_scalar(RB1[0:1, 4608:5120], RB1[0:1, 4608:5120],
                                   127.0, ALU.mult).then_inc(b.s("dv"), 1)
            v.wait_ge(b.s("pe"), q2_rep_pe[tt])
            for oc in range(32):
                gucol = XQL[:, oc * 512:oc * 512 + 512]
                v.tensor_tensor(gucol, gucol, PS[2][:, 0:512], ALU.mult)
                v.drain()
                _round5(v, gucol, RB1[:, ABSC:ABSC + 512], DORA)
                v.tensor_scalar(gucol, gucol, 127.0, -128.0, ALU.min,
                                ALU.max).then_inc(b.s("dv"), 1)
            for do in range(8):
                bank = PS[6 + do % 2]
                v.wait_ge(b.s("pe"), d_pe[tt, do])
                if do == 0:
                    v.drain()
                    v.tensor_copy(RB1[:, 1536:2048], PS[2][:, 0:512])
                v.drain()
                v.tensor_tensor(RB1[:, 2048:2560], bank[:, 0:512],
                                RB1[:, 1536:2048], ALU.mult)
                v.drain()
                v.tensor_scalar(RB1[:, 2048:2560], RB1[:, 2048:2560],
                                SM[:, 113:114], 1.0, ALU.mult, ALU.mult)
                v.drain()
                xsl = slice(do * T + tt * 512, do * T + tt * 512 + 512)
                v.tensor_tensor(X[:, xsl], X[:, xsl], RB1[:, 2048:2560],
                                ALU.add).then_inc(b.s("dv"), 1)
    em.emit(block)
    state["prev_wo_pe"] = d_pe[3, 7]
    state["prev_pair_done"] = b.cnts["dv"]
    return b.cnts["dv"]


def sub_l1_attn_b(b, block, ln_dv, state):
    X, RB1, RB2, ZBH = b.t["X"], b.t["RB1"], b.t["RB2"], b.t["ZBH"]
    XQH, XQL = b.t["XQH"], b.t["XQL"]
    WSH, SM = b.t["WSH"], b.t["SMALL"]
    ONESM, ID, TRIU = b.t["ONESM"], b.t["IDent"], b.t["TRIU"]
    PS = [b.t[f"PS{i}"] for i in range(8)]
    scr = b.t["scr_l1a"].rearrange("(h c d) -> h c d", h=2, c=1024)
    scro = b.t["scr_out"]
    em = E(b)
    blocks = []
    for j, nm in enumerate(["L1_attn_b_Wq", "L1_attn_b_Wk", "L1_attn_b_Wv",
                            "L1_attn_b_Wo"]):
        for dc in range(8):
            blocks.append((b.t[nm][dc * 128:dc * 128 + 128, :],
                           j * 1024 + dc * 128))
    prep_done = prep_T(b, block, state, blocks, scr)

    nw_tgt = b.inc("dq", 16)

    @block.sync
    def _(sync):
        sync.wait_ge(b.s("wsh0"), prep_done[0])
        sync.wait_ge(b.s("wsh1"), prep_done[1])
        sync.dma_start(out=SM[:, 118:119],
                       in_=b.t["L1_attn_b_norm_w"].rearrange(
                           "(c p) -> p c", p=128),
                       allow_slow_non_contiguous=True).then_inc(b.s("dq"), 16)

    QO, KO, VO = 0, 2048, 4096
    ac_base = b.cnts["sx0"]
    wld, proj_pe, evac_dv = {}, {}, {}
    tr_pe, trc_dv, sc_pe, msk_dv, io_pe = {}, {}, {}, {}, {}
    cs_pe, zc_dv, dn_pe, dnr_dv, rp_pe = {}, {}, {}, {}, {}
    o2_dv, sq_sc, var_pe, vs_sc, vr_dv, rs_pe = {}, {}, {}, {}, {}, {}
    o3_dv, sadd_pe, supd_dv, sz_dv, ost = {}, {}, {}, {}, {}
    for h in range(8):
        for j in range(3):
            wld[h, j] = b.inc("wsf%d" % (j % 2), 32)
    for h in range(8):
        for j in range(3):
            proj_pe[h, j] = b.inc("pe")
        for c in range(32):
            tr_pe[h, c] = b.inc("pe")
            sc_pe[h, c] = b.inc("pe")
            io_pe[h, c] = b.inc("pe")
            cs_pe[h, c] = b.inc("pe")
            dn_pe[h, c] = b.inc("pe")
            rp_pe[h, c] = b.inc("pe")
            var_pe[h, c] = b.inc("pe")
            rs_pe[h, c] = b.inc("pe")
            sadd_pe[h, c] = b.inc("pe")
    for h in range(8):
        for j in range(3):
            evac_dv[h, j] = b.inc("dv")
        sz_dv[h] = b.inc("dv", 2)
        for c in range(32):
            trc_dv[h, c] = b.inc("dv")
            msk_dv[h, c] = b.inc("dv")
            zc_dv[h, c] = b.inc("dv")
            dnr_dv[h, c] = b.inc("dv")
            o2_dv[h, c] = b.inc("dv")
            vr_dv[h, c] = b.inc("dv")
            o3_dv[h, c] = b.inc("dv")
            supd_dv[h, c] = b.inc("dv")
    for h in range(8):
        for c in range(32):
            sq_sc[h, c] = b.inc("sc")
            vs_sc[h, c] = b.inc("sc")
    for h in range(8):
        for c in range(32):
            ost[h, c] = b.inc("sx0", 16)

    @em.on("sync")
    def _(sync):
        for h in range(8):
            for j in range(3):
                slot = j % 2
                if h == 0 and j < 2:
                    sync.wait_ge(b.s("wsh0"), prep_done[0])
                    sync.wait_ge(b.s("wsh1"), prep_done[1])
                    if state.get("prev_wo_pe") is not None:
                        sync.wait_ge(b.s("pe"), state["prev_wo_pe"])
                else:
                    pj = (h, j - 2) if j >= 2 else (h - 1, j + 1)
                    sync.wait_ge(b.s("pe"), proj_pe[pj])
                for e in (0, 1):
                    sync.dma_start(
                        out=WSH[:, slot * 2048 + e * 1024:
                                slot * 2048 + e * 1024 + 1024].rearrange(
                            "p (c d) -> p c d", c=8),
                        in_=scr[e, :, j * 1024 + h * 128:
                                j * 1024 + h * 128 + 128].rearrange(
                            "(c p) d -> p c d", p=128),
                    ).then_inc(b.s("wsf%d" % slot), 16)
            for c in range(32):
                sync.wait_ge(b.s("dv"), o3_dv[h, c])
                if (h, c) > (0, 0):
                    ph, pc = (h, c - 1) if c >= 1 else (h - 1, 31)
                    sync.wait_ge(b.s("sx0"), ost[ph, pc])
                sync.dma_start(
                    out=scro[h * 128:h * 128 + 128, c * 64:c * 64 + 64],
                    in_=RB2[:, 640:704]).then_inc(b.s("sx0"), 16)

    @em.on("tensor")
    def _(tensor):
        for h in range(8):
            for j in range(3):
                slot = j % 2
                tensor.wait_ge(b.s("wsf%d" % slot), wld[h, j])
                if h == 0 and j == 0:
                    tensor.wait_ge(b.s("dv"), ln_dv)
                if (h, j) > (0, 0):
                    pj = (h, j - 1) if j >= 1 else (h - 1, 2)
                    tensor.wait_ge(b.s("dv"), evac_dv[pj])
                for q in range(4):
                    for c in range(NCH):
                        wh = WSH[:, slot * 2048 + c * 128:
                                 slot * 2048 + c * 128 + 128]
                        wl = WSH[:, slot * 2048 + 1024 + c * 128:
                                 slot * 2048 + 1024 + c * 128 + 128]
                        xsl = slice(c * T + q * 512, c * T + q * 512 + 512)
                        tensor.matmul(PS[q][:, 0:512], wh, XQH[:, xsl],
                                      start=(c == 0), stop=False)
                        tensor.matmul(PS[q][:, 0:512], wh, XQL[:, xsl],
                                      start=False, stop=False)
                        mm = tensor.matmul(PS[q][:, 0:512], wl, XQH[:, xsl],
                                           start=False, stop=(c == NCH - 1))
                mm.then_inc(b.s("pe"), 1)
            for c in range(32):
                if c == 0:
                    tensor.wait_ge(b.s("dv"), evac_dv[h, 2])
                    tensor.wait_ge(b.s("dv"), sz_dv[h])
                if c >= 1:
                    tensor.wait_ge(b.s("dv"), trc_dv[h, c - 1])
                    tensor.wait_ge(b.s("dv"), dnr_dv[h, c - 1])
                tensor.transpose(PS[4][0:64, 0:128],
                                 RB1[:, KO + c * 64:KO + c * 64 + 64],
                                 ID[:, :])
                tensor.transpose(PS[4][0:64, 128:256],
                                 RB1[:, VO + c * 64:VO + c * 64 + 64],
                                 ID[:, :]).then_inc(b.s("pe"), 1)
                if c >= 1:
                    tensor.wait_ge(b.s("dv"), msk_dv[h, c - 1])
                tensor.matmul(PS[0][0:64, 0:64],
                              RB1[:, KO + c * 64:KO + c * 64 + 64],
                              RB1[:, QO + c * 64:QO + c * 64 + 64],
                              start=True, stop=True).then_inc(b.s("pe"), 1)
                tensor.wait_ge(b.s("dv"), msk_dv[h, c])
                tensor.wait_ge(b.s("dv"), trc_dv[h, c])
                if c >= 1:
                    tensor.wait_ge(b.s("dv"), o2_dv[h, c - 1])
                    tensor.wait_ge(b.s("dv"), supd_dv[h, c - 1])
                tensor.matmul(PS[1][:, 0:64], RB2[0:64, 128:256],
                              RB2[0:64, 384:448], start=True, stop=True)
                tensor.matmul(PS[2][:, 0:64], RB2[:, 256:384],
                              RB1[:, QO + c * 64:QO + c * 64 + 64],
                              start=True, stop=True).then_inc(b.s("pe"), 1)
                if c >= 1:
                    tensor.wait_ge(b.s("dv"), zc_dv[h, c - 1])
                tensor.matmul(PS[3][:, 0:64], RB2[0:64, 0:128],
                              TRIU[:, :], start=True,
                              stop=True).then_inc(b.s("pe"), 1)
                tensor.wait_ge(b.s("dv"), zc_dv[h, c])
                tensor.matmul(PS[7][0:1, 0:64], ONESM[:, 0:1],
                              RB2[:, 448:512], start=True,
                              stop=True).then_inc(b.s("pe"), 1)
                tensor.wait_ge(b.s("dv"), dnr_dv[h, c])
                tensor.matmul(PS[5][:, 0:64], ONESM[0:1, 0:128],
                              RB2[0:1, 704:768], start=True,
                              stop=True).then_inc(b.s("pe"), 1)
                tensor.wait_ge(b.s("sc"), sq_sc[h, c])
                if c >= 1:
                    tensor.wait_ge(b.s("sc"), vs_sc[h, c - 1])
                tensor.matmul(PS[7][0:1, 64:128], ONESM[:, 0:1],
                              RB2[:, 832:896], start=True,
                              stop=True).then_inc(b.s("pe"), 1)
                tensor.wait_ge(b.s("dv"), vr_dv[h, c])
                tensor.matmul(PS[5][:, 64:128], ONESM[0:1, 0:128],
                              RB2[0:1, 768:832], start=True,
                              stop=True).then_inc(b.s("pe"), 1)
                tensor.wait_ge(b.s("dv"), o3_dv[h, c])
                tensor.matmul(PS[6][:, 0:128], RB2[0:64, 0:128],
                              RB2[0:64, 128:256], start=True,
                              stop=True).then_inc(b.s("pe"), 1)


    @em.on("vector")
    def _(v):
        for h in range(8):
            for j in range(3):
                v.wait_ge(b.s("pe"), proj_pe[h, j])
                base = (QO, KO, VO)[j]
                for q in range(4):
                    ins = v.tensor_copy(
                        RB1[:, base + q * 512:base + q * 512 + 512],
                        PS[q][:, 0:512])
                ins.then_inc(b.s("dv"), 1)
            if h == 0:
                v.wait_ge(b.s("dq"), nw_tgt)
            v.memset(RB2[:, 256:384], 0.0).then_inc(b.s("dv"), 1)
            v.memset(RB2[:, 512:513], 0.0).then_inc(b.s("dv"), 1)
            for c in range(32):
                v.wait_ge(b.s("pe"), tr_pe[h, c])
                v.tensor_copy(RB2[0:64, 0:128], PS[4][0:64, 0:128])
                ins = v.tensor_copy(RB2[0:64, 128:256], PS[4][0:64, 128:256])
                ins.then_inc(b.s("dv"), 1)
                v.wait_ge(b.s("pe"), sc_pe[h, c])
                v.tensor_tensor(RB2[0:64, 384:448], PS[0][0:64, 0:64],
                                TRIU[:, :], ALU.mult).then_inc(b.s("dv"), 1)
                v.wait_ge(b.s("pe"), cs_pe[h, c])
                v.tensor_scalar(RB2[:, 448:512], PS[3][:, 0:64],
                                RB2[:, 512:513], 0.0, ALU.add, ALU.add)
                v.drain()
                v.tensor_copy(RB2[:, 512:513], RB2[:, 511:512])
                v.drain()
                v.tensor_tensor(RB2[:, 448:512], RB2[:, 448:512],
                                RB1[:, QO + c * 64:QO + c * 64 + 64],
                                ALU.mult).then_inc(b.s("dv"), 1)
                v.wait_ge(b.s("pe"), dn_pe[h, c])
                v.tensor_copy(RB2[0:1, 960:1024], PS[7][0:1, 0:64])
                v.drain()
                v.scalar_tensor_tensor(RB2[0:1, 704:768],
                                       RB2[0:1, 960:1024], -1.0,
                                       RB2[0:1, 960:1024],
                                       ALU.mult, ALU.max)
                v.drain()
                v.tensor_single_scalar(RB2[0:1, 704:768], RB2[0:1, 704:768],
                                       1.0, ALU.max)
                v.drain()
                v.reciprocal(RB2[0:1, 704:768],
                             RB2[0:1, 704:768]).then_inc(b.s("dv"), 1)
                v.wait_ge(b.s("pe"), io_pe[h, c])
                v.wait_ge(b.s("pe"), rp_pe[h, c])
                v.tensor_copy(RB2[:, 896:960], PS[2][:, 0:64])
                v.drain()
                v.tensor_tensor(RB2[:, 576:640], PS[1][:, 0:64],
                                RB2[:, 896:960], ALU.add)
                v.drain()
                v.tensor_tensor(RB2[:, 576:640], RB2[:, 576:640],
                                PS[5][:, 0:64], ALU.mult,
                                ).then_inc(b.s("dv"), 1)
                v.wait_ge(b.s("sc"), vs_sc[h, c])
                v.reciprocal(RB2[0:1, 768:832],
                             RB2[0:1, 768:832]).then_inc(b.s("dv"), 1)
                v.wait_ge(b.s("pe"), rs_pe[h, c])
                if c >= 1 or h >= 1:
                    ph, pc = (h, c - 1) if c >= 1 else (h - 1, 31)
                    v.wait_ge(b.s("sx0"), ost[ph, pc])
                v.tensor_tensor(RB2[:, 640:704], RB2[:, 576:640],
                                PS[5][:, 64:128], ALU.mult)
                v.drain()
                v.tensor_scalar(RB2[:, 640:704], RB2[:, 640:704],
                                SM[:, 118:119], 1.0, ALU.mult, ALU.mult,
                                ).then_inc(b.s("dv"), 1)
                v.wait_ge(b.s("pe"), sadd_pe[h, c])
                v.tensor_tensor(RB2[:, 256:384], RB2[:, 256:384],
                                PS[6][:, 0:128],
                                ALU.add).then_inc(b.s("dv"), 1)

    @em.on("scalar")
    def _(scalar):
        for h in range(8):
            for c in range(32):
                scalar.wait_ge(b.s("dv"), o2_dv[h, c])
                scalar.activation(RB2[:, 832:896], RB2[:, 576:640],
                                  ACT.Square).then_inc(b.s("sc"), 1)
                scalar.wait_ge(b.s("pe"), var_pe[h, c])
                scalar.activation(RB2[0:1, 768:832], PS[7][0:1, 64:128],
                                  ACT.Sqrt, bias=1e-5,
                                  scale=1.0 / 128.0).then_inc(b.s("sc"), 1)
    em.emit(block)

    em2 = E(b)
    ofin = b.cnts["sx0"]
    old_, osp_dv, wo_ld, woec_pe, wo_dv = {}, {}, {}, {}, {}
    for tcol in range(4):
        for ec in range(8):
            old_[tcol, ec] = b.inc("dx%d" % (ec % 2), 16)
    for tcol in range(4):
        for ec in range(8):
            wo_ld[tcol, ec] = b.inc("wsf%d" % (ec % 2), 32)
    for tcol in range(4):
        for ec in range(8):
            woec_pe[tcol, ec] = b.inc("pe")
    for tcol in range(4):
        for ec in range(8):
            osp_dv[tcol, ec] = b.inc("dv")
        for do in range(8):
            wo_dv[tcol, do] = b.inc("dv")

    @em2.on("sync")
    def _(sync):
        for tcol in range(4):
            for ec in range(8):
                if tcol == 0 and ec == 0:
                    sync.wait_ge(b.s("sx0"), ofin)
                if (tcol, ec) > (0, 0):
                    pt = (tcol, ec - 1) if ec else (tcol - 1, 7)
                    sync.wait_ge(b.s("dv"), osp_dv[pt])
                sync.dma_start(
                    out=RB1[:, 0:512],
                    in_=scro[ec * 128:ec * 128 + 128,
                             tcol * 512:tcol * 512 + 512],
                ).then_inc(b.s("dx%d" % (ec % 2)), 16)
                slot = ec % 2
                pw = (tcol, ec - 2) if ec >= 2 else (tcol - 1, 6 + ec)
                if pw[0] >= 0:
                    sync.wait_ge(b.s("pe"), woec_pe[pw])
                sync.dma_start(
                    out=WSH[:, slot * 2048:slot * 2048 + 1024],
                    in_=scr[0, ec * 128:ec * 128 + 128, 3072:4096],
                ).then_inc(b.s("wsf%d" % slot), 16)
                sync.dma_start(
                    out=WSH[:, slot * 2048 + 1024:slot * 2048 + 2048],
                    in_=scr[1, ec * 128:ec * 128 + 128, 3072:4096],
                ).then_inc(b.s("wsf%d" % slot), 16)

    @em2.on("vector")
    def _(v):
        for tcol in range(4):
            for ec in range(8):
                v.wait_ge(b.s("dx%d" % (ec % 2)), old_[tcol, ec])
                if (tcol, ec) > (0, 0):
                    pt = (tcol, ec - 1) if ec else (tcol - 1, 7)
                    v.wait_ge(b.s("pe"), woec_pe[pt])
                v.tensor_copy(ZBH[:, 0:512], RB1[:, 0:512])
                v.drain()
                ins = v.tensor_tensor(ZBH[:, 512:1024], RB1[:, 0:512],
                                      ZBH[:, 0:512], ALU.subtract)
                ins.then_inc(b.s("dv"), 1)
            for do in range(8):
                v.wait_ge(b.s("pe"), woec_pe[tcol, 7])
                v.drain()
                v.tensor_copy(RB2[:, 0:512], PS[do][:, 0:512])
                v.drain()
                xsl = slice(do * T + tcol * 512, do * T + tcol * 512 + 512)
                v.tensor_tensor(X[:, xsl], X[:, xsl], RB2[:, 0:512],
                                ALU.add).then_inc(b.s("dv"), 1)

    @em2.on("tensor")
    def _(tensor):
        for tcol in range(4):
            for ec in range(8):
                slot = ec % 2
                tensor.wait_ge(b.s("dv"), osp_dv[tcol, ec])
                tensor.wait_ge(b.s("wsf%d" % slot), wo_ld[tcol, ec])
                if ec == 0 and tcol >= 1:
                    tensor.wait_ge(b.s("dv"), wo_dv[tcol - 1, 7])
                for do in range(8):
                    woh = WSH[:, slot * 2048 + do * 128:
                              slot * 2048 + do * 128 + 128]
                    wol = WSH[:, slot * 2048 + 1024 + do * 128:
                              slot * 2048 + 1024 + do * 128 + 128]
                    tensor.matmul(PS[do][:, 0:512], woh, ZBH[:, 0:512],
                                  start=(ec == 0), stop=False)
                    tensor.matmul(PS[do][:, 0:512], woh, ZBH[:, 512:1024],
                                  start=False, stop=False)
                    mm = tensor.matmul(PS[do][:, 0:512], wol, ZBH[:, 0:512],
                                       start=False, stop=(ec == 7))
                mm.then_inc(b.s("pe"), 1)
    em2.emit(block)
    state["prev_wo_pe"] = woec_pe[3, 7]
    state["prev_pair_done"] = b.cnts["dv"]
    return b.cnts["dv"]


WSHAPES = {
    "L0_attn_a_w_q": [16, 1024, 64], "L0_attn_a_w_k": [16, 1024, 64],
    "L0_attn_a_w_v": [16, 1024, 64], "L0_attn_a_w_o": [1024, 1024],
    "L0_attn_b_w_qkv": [1024, 3072], "L0_attn_b_w_out": [1024, 1024],
    "L0_attn_b_b_out": [1024], "L0_attn_b_ln_gamma": [1024],
    "L0_attn_b_ln_beta": [1024],
    "L0_ffn_W_r": [1024, 1024], "L0_ffn_W_k": [1024, 1024],
    "L0_ffn_W_v": [1024, 1024], "L0_ffn_mu_r": [1024], "L0_ffn_mu_k": [1024],
    "L1_attn_a_head_mask": [16], "L1_attn_a_qkv_w": [3072, 1024],
    "L1_attn_a_qkv_b": [3072], "L1_attn_a_fc_w": [1024, 1024],
    "L1_attn_a_fc_b": [1024],
    "L1_attn_b_Wq": [1024, 1024], "L1_attn_b_Wk": [1024, 1024],
    "L1_attn_b_Wv": [1024, 1024], "L1_attn_b_Wo": [1024, 1024],
    "L1_attn_b_norm_w": [128],
    "L1_ffn_w_gate": [4096, 1024], "L1_ffn_w_up": [4096, 1024],
    "L1_ffn_w_down": [1024, 4096],
}


def sub_l0_attn_a(b, block, ln_dv, state):
    wq = b.t["L0_attn_a_w_q"].rearrange("h (c p) k -> h p c k", p=128)
    wk = b.t["L0_attn_a_w_k"].rearrange("h (c p) k -> h p c k", p=128)
    wv = b.t["L0_attn_a_w_v"].rearrange("h (c p) k -> h p c k", p=128)
    wo = b.t["L0_attn_a_w_o"]
    cfg = {
        "wq_ap": lambda hp, e: wq[2 * hp + e],
        "wk_ap": lambda hp, e: wk[2 * hp + e],
        "wv_ap": lambda hp, e: wv[2 * hp + e],
        "wo_ap": lambda hp, e: wo[(2 * hp + e) * 64:(2 * hp + e) * 64 + 64, :],
        "scale": 0.125,
        "internal_residual": True,
        "ln_dv": ln_dv,
    }
    return attn64(b, block, cfg, state)


def build_program(n_sub=None):
    if n_sub is None:
        n_sub = N_SUB
    b = B()
    b.dram("x_in", [T, D])
    b.dram("out", [T, D], kind="ExternalOutput")
    b.dram("x_ident", [128, 128])
    b.dram("x_onesm", [128, 128])
    b.dram("x_triu", [64, 64])
    b.dram("x_selh", [128, 2], BF16)
    b.dram("x_fconst", [1, 8])
    b.dram("x_sel32", [128, 4], BF16)
    b.dram("x_pick32", [4, 512], BF16)
    b.dram("x_selh2", [2, 128])
    for k, v in WSHAPES.items():
        b.dram(k, v)
    b.dram("scr_gT", [1024 * 4096], BF16, kind="Internal")
    b.dram("scr_uT", [1024 * 4096], BF16, kind="Internal")
    b.dram("scr_dT", [4096 * 1024], BF16, kind="Internal")
    b.dram("scr_kv", [1 << 21], F32, kind="Internal")
    b.dram("scr_ffn", [3 * 2 * 1024 * 1024], BF16, kind="Internal")
    b.dram("scr_out", [1024, 2048], F32, kind="Internal")
    b.dram("scr_l1a", [2 * 1024 * 4096], BF16, kind="Internal")

    b.sb("X", [128, NCH * T], F32)
    b.sb("XQH", [128, NCH * T], BF16)
    b.sb("XQL", [128, NCH * T], BF16)
    b.sb("RB1", [128, 8192], F32)
    b.sb("RB2", [128, 2048], F32)
    b.sb("RBH1", [128, 4096], BF16)
    b.sb("RBH2", [128, 4096], BF16)
    b.sb("WSF", [128, 2048], F32)
    b.sb("WSH", [128, 4096], BF16)
    b.sb("ZBH", [128, 2048], BF16)
    b.sb("SMALL", [128, 120], F32)
    b.sb("IDent", [128, 128], F32)
    b.sb("ONESM", [128, 128], F32)
    b.sb("TRIU", [64, 64], F32)
    b.sb("SELH", [128, 2], BF16)
    b.sb("SEL32", [128, 4], BF16)
    b.sb("PICK32", [4, 512], BF16)
    b.sb("NEGCH", [4, 1], BF16)
    b.sb("SELH2", [2, 128], F32)
    b.sb("DORA", [128, 1], F32)
    b.sb("FCONST", [1, 8], F32)
    b.nc.const_aps.register(F32, 1e-5, b.t["FCONST"][0:1, 0:1])
    b.nc.const_aps.register(F32, 1e-6, b.t["FCONST"][0:1, 1:2])

    for i in range(8):
        b.ps_(f"PS{i}", [128, 512])

    for s in ["dq", "gq", "pe", "dv", "sc", "gp", "st", "wf",
              "dx0", "dx1", "wsf0", "wsf1", "wsh0", "wsh1", "sx0", "sx1",
              "ac0", "ac1"]:
        b.new_sem(s)

    with b.nc.Block() as block:
        gstate = {}
        phase_load_x(b, block)
        if n_sub >= 1:
            ln_dv = phase_ln(b, block)
            sub_l0_attn_a(b, block, ln_dv, gstate)
        if n_sub >= 2:
            ln_dv = phase_ln(b, block)
            sub_l0_ffn(b, block, ln_dv, gstate)
        if n_sub >= 3:
            ln_dv = phase_ln(b, block)
            sub_l0_attn_b(b, block, ln_dv, gstate)
        if n_sub >= 4:
            ln_dv = phase_ln(b, block)
            sub_l0_ffn(b, block, ln_dv, gstate)
        if n_sub >= 5:
            ln_dv = phase_ln(b, block)
            sub_l1_attn_a(b, block, ln_dv, gstate)
        if n_sub >= 6:
            ln_dv = phase_ln(b, block)
            sub_l1_ffn(b, block, ln_dv, gstate)
        if n_sub >= 7:
            ln_dv = phase_ln(b, block)
            sub_l1_attn_b(b, block, ln_dv, gstate)
        if n_sub >= 8:
            ln_dv = phase_ln(b, block)
            sub_l1_ffn(b, block, ln_dv, gstate)
        phase_store_x(b, block)
    return b.nc


_CONSTS = None


def kernel(**inputs):
    global _CONSTS
    if _CONSTS is None:
        _CONSTS = make_consts()
    x = np.ascontiguousarray(np.asarray(inputs["L0_attn_a_x"], dtype=np.float32))
    nc = build_program()
    base = dict(_CONSTS)
    for k in WSHAPES:
        base[k] = np.ascontiguousarray(np.asarray(inputs[k], dtype=np.float32))
    in_maps = []
    for i in range(8):
        m = dict(base)
        m["x_in"] = x[i]
        in_maps.append(m)
    res = run_bass_kernel_spmd(nc, in_maps, core_ids=list(range(8)))
    return np.stack([r["out"] for r in res.results], axis=0)


# version 06
# speedup vs baseline: 1.1280x; 1.1280x over previous
import numpy as np
import ml_dtypes
import concourse.bass as bass
import concourse.mybir as mybir
from concourse.bass_utils import run_bass_kernel_spmd

F32 = mybir.dt.float32
BF16 = mybir.dt.bfloat16
ALU = mybir.AluOpType
ACT = mybir.ActivationFunctionType
AX = mybir.AxisListType.X

T = 2048
D = 1024
NCH = D // 128
C23 = 12582912.0
C52 = 6755399441055744.0
LN_EPS = 1e-5

N_SUB = 8
ATTNB_CUT = 0
ATTNB_QUADS = 8
ENGS = ["tensor", "vector", "scalar", "sync", "gpsimd"]


class B:
    def __init__(self):
        self.nc = bass.Bass("TRN2", target_bir_lowering=False)
        self._cm = []
        self.sems = {}
        self.cnts = {}
        self.t = {}

    def new_sem(self, name):
        cm = self.nc.semaphore(name)
        h = cm.__enter__()
        self._cm.append(cm)
        self.sems[name] = h
        self.cnts[name] = 0
        return h

    def inc(self, name, n=1):
        self.cnts[name] += n
        return self.cnts[name]

    def s(self, name):
        return self.sems[name]

    def sb(self, name, shape, dtype):
        ap = self.nc.alloc_sbuf_tensor(name, shape, dtype).ap()
        self.t[name] = ap
        return ap

    def ps_(self, name, shape, dtype=F32):
        ap = self.nc.alloc_psum_tensor(name, shape, dtype).ap()
        self.t[name] = ap
        return ap

    def dram(self, name, shape, dtype=F32, kind="ExternalInput"):
        ap = self.nc.dram_tensor(name, shape, dtype, kind=kind).ap()
        self.t[name] = ap
        return ap


class E:

    def __init__(self, b):
        self.b = b
        self.qs = {k: [] for k in ENGS}

    def on(self, eng):
        def deco(fn):
            self.qs[eng].append(fn)
            return fn
        return deco

    def emit(self, block):
        for eng in ENGS:
            fns = self.qs[eng]
            if not fns:
                continue

            def body(sec, fns=fns):
                for f in fns:
                    f(sec)
            getattr(block, eng)(body)


def make_consts():
    selh = np.zeros((128, 2), dtype=ml_dtypes.bfloat16)
    selh[0:64, 0] = 1.0
    selh[64:128, 1] = 1.0
    sel32 = np.zeros((128, 4), dtype=ml_dtypes.bfloat16)
    for h in range(4):
        sel32[32 * h:32 * h + 32, h] = 1.0
    pick32 = np.zeros((4, 512), dtype=ml_dtypes.bfloat16)
    for h in range(4):
        pick32[h, 128 * h:128 * h + 128] = 1.0
    selh2 = np.zeros((2, 128), dtype=np.float32)
    selh2[0, 0:64] = 1.0
    selh2[1, 64:128] = 1.0
    return {
        "x_ident": np.eye(128, dtype=np.float32),
        "x_onesm": np.ones((128, 128), dtype=np.float32),
        "x_triu": np.triu(np.ones((64, 64), dtype=np.float32)),
        "x_selh": selh,
        "x_fconst": np.array([[1e-5, 1e-6, 0.0, 0.0, 0.0, 0.0, 0.0, 0.0]],
                             dtype=np.float32),
        "x_sel32": sel32,
        "x_pick32": pick32,
        "x_selh2": selh2,
    }


def phase_load_x(b, block):
    X, RB1, ID = b.t["X"], b.t["RB1"], b.t["IDent"]
    PS = [b.t[f"PS{i}"] for i in range(8)]
    pe0, dv0 = b.cnts["pe"], b.cnts["dv"]
    tile_dma = []

    @block.sync
    def _(sync):
        for src, dst in [("x_ident", "IDent"), ("x_onesm", "ONESM"),
                         ("x_triu", "TRIU"), ("x_selh", "SELH"),
                         ("x_fconst", "FCONST"),
                         ("x_sel32", "SEL32"), ("x_pick32", "PICK32"),
                         ("x_selh2", "SELH2")]:
            sync.dma_start(out=b.t[dst][:, :], in_=b.t[src][:, :]).then_inc(
                b.s("dq"), 16)
            b.inc("dq", 16)
        for i in range(16):
            if i >= 2:
                sync.wait_ge(b.s("pe"), pe0 + (i - 2) + 1)
            sname = "dx%d" % (i % 2)
            sync.dma_start(
                out=RB1[:, (i % 8) * 1024:(i % 8) * 1024 + 1024],
                in_=b.t["x_in"][i * 128:(i + 1) * 128, :],
            ).then_inc(b.s(sname), 16)
            tile_dma.append((sname, b.inc(sname, 16)))

    @block.vector
    def _(vector):
        for i in range(16):
            vector.wait_ge(b.s("pe"), pe0 + i + 1)
            for c in range(NCH):
                ins = vector.tensor_copy(
                    X[:, c * T + i * 128:c * T + i * 128 + 128], PS[c][:, 0:128])
            ins.then_inc(b.s("dv"), 1)
            b.inc("dv", 1)
        vector.memset(b.t["DORA"][:, :], 1.0).then_inc(b.s("dv"), 1)
        vector.wait_ge(b.s("dv"), b.inc("dv", 1))
        vector.tensor_scalar(
            b.t["DORA"][:, :], b.t["DORA"][:, :], C52, C52, ALU.add, ALU.subtract)

    @block.tensor
    def _(tensor):
        tensor.wait_ge(b.s("dq"), 128)
        for i in range(16):
            tensor.wait_ge(b.s(tile_dma[i][0]), tile_dma[i][1])
            if i >= 1:
                tensor.wait_ge(b.s("dv"), dv0 + i)
            for c in range(NCH):
                mm = tensor.transpose(
                    PS[c][:, 0:128],
                    RB1[:, (i % 8) * 1024 + c * 128:(i % 8) * 1024 + c * 128 + 128],
                    ID[:, :])
            mm.then_inc(b.s("pe"), 1)
            b.inc("pe", 1)


def phase_ln(b, block, eps=LN_EPS):
    X, RB1, RB2, ONESM = b.t["X"], b.t["RB1"], b.t["RB2"], b.t["ONESM"]
    XQH, XQL = b.t["XQH"], b.t["XQL"]
    PS4, PS5, PS6, PS7 = b.t["PS4"], b.t["PS5"], b.t["PS6"], b.t["PS7"]
    pe0, dv0, sc0 = b.cnts["pe"], b.cnts["dv"], b.cnts["sc"]

    @block.scalar
    def _(scalar):
        for q in range(4):
            if q == 0:
                scalar.wait_ge(b.s("dq"), 128)
                scalar.wait_ge(b.s("dv"), dv0)
            if q >= 2:
                scalar.wait_ge(b.s("dv"), dv0 + 3 * (q - 2) + 3)
            for c in range(NCH):
                ins = scalar.activation(
                    RB1[:, (q % 2) * 4096 + c * 512:(q % 2) * 4096 + c * 512 + 512],
                    X[:, c * T + q * 512:c * T + q * 512 + 512], ACT.Square)
            ins.then_inc(b.s("sc"), 1)
            b.inc("sc", 1)
            scalar.wait_ge(b.s("dv"), dv0 + 3 * q + 1)
            scalar.activation(
                RB2[0:1, 512:1024], RB2[0:1, 512:1024],
                ACT.Sqrt, bias=eps, scale=1.0).then_inc(b.s("sc"), 1)
            b.inc("sc", 1)

    @block.tensor
    def _(tensor):
        for q in range(4):
            if q == 0:
                tensor.wait_ge(b.s("dv"), dv0)
            if q >= 1:
                tensor.wait_ge(b.s("dv"), dv0 + 3 * (q - 1) + 1)
            for c in range(NCH):
                tensor.matmul(
                    PS6[0:1, 0:512], ONESM[:, 0:1],
                    X[:, c * T + q * 512:c * T + q * 512 + 512],
                    start=(c == 0), stop=(c == NCH - 1))
            tensor.wait_ge(b.s("sc"), sc0 + 2 * q + 1)
            for c in range(NCH):
                mm = tensor.matmul(
                    PS7[0:1, 0:512], ONESM[:, 0:1],
                    RB1[:, (q % 2) * 4096 + c * 512:(q % 2) * 4096 + c * 512 + 512],
                    start=(c == 0), stop=(c == NCH - 1))
            mm.then_inc(b.s("pe"), 1)
            b.inc("pe", 1)
            tensor.wait_ge(b.s("dv"), dv0 + 3 * q + 2)
            if q >= 1:
                tensor.wait_ge(b.s("dv"), dv0 + 3 * (q - 1) + 3)
            tensor.matmul(PS4[:, 0:512], ONESM[0:1, 0:128], RB2[0:1, 0:512],
                          start=True, stop=True)
            mm = tensor.matmul(PS5[:, 0:512], ONESM[0:1, 0:128],
                               RB2[0:1, 1024:1536], start=True, stop=True)
            mm.then_inc(b.s("pe"), 1)
            b.inc("pe", 1)

    @block.vector
    def _(vector):
        for q in range(4):
            vector.wait_ge(b.s("pe"), pe0 + 2 * q + 1)
            vector.tensor_single_scalar(
                RB2[0:1, 0:512], PS6[0:1, 0:512], 1.0 / D, ALU.mult)
            vector.drain()
            vector.tensor_tensor(
                RB2[0:1, 1536:2048], RB2[0:1, 0:512], RB2[0:1, 0:512], ALU.mult)
            vector.drain()
            vector.scalar_tensor_tensor(
                RB2[0:1, 512:1024], PS7[0:1, 0:512], 1.0 / D,
                RB2[0:1, 1536:2048], ALU.mult, ALU.subtract).then_inc(b.s("dv"), 1)
            b.inc("dv", 1)
            vector.wait_ge(b.s("sc"), sc0 + 2 * q + 2)
            vector.reciprocal(
                RB2[0:1, 1024:1536], RB2[0:1, 512:1024]).then_inc(b.s("dv"), 1)
            b.inc("dv", 1)
            vector.wait_ge(b.s("pe"), pe0 + 2 * q + 2)
            sls = []
            for c in range(NCH):
                sls.append((
                    slice(c * T + q * 512, c * T + q * 512 + 512),
                    slice((q % 2) * 4096 + c * 512,
                          (q % 2) * 4096 + c * 512 + 512)))
            for sl, xl in sls:
                vector.tensor_tensor(RB1[:, xl], X[:, sl], PS4[:, 0:512],
                                     ALU.subtract)
            vector.drain()
            for sl, xl in sls:
                vector.tensor_tensor(RB1[:, xl], RB1[:, xl], PS5[:, 0:512],
                                     ALU.mult)
            vector.drain()
            for sl, xl in sls:
                vector.tensor_copy(XQH[:, sl], RB1[:, xl])
            vector.drain()
            for sl, xl in sls:
                ins = vector.tensor_tensor(XQL[:, sl], RB1[:, xl],
                                           XQH[:, sl], ALU.subtract)
            ins.then_inc(b.s("dv"), 1)
            b.inc("dv", 1)
    return b.cnts["dv"]


def phase_store_x(b, block):
    X, RB1, ID = b.t["X"], b.t["RB1"], b.t["IDent"]
    out = b.t["out"]
    PS = [b.t[f"PS{i}"] for i in range(8)]
    pe0, dv0, st0 = b.cnts["pe"], b.cnts["dv"], b.cnts["st"]

    @block.tensor
    def _(tensor):
        for i in range(16):
            tensor.wait_ge(b.s("dv"), dv0 + i)
            for c in range(NCH):
                mm = tensor.transpose(
                    PS[c][:, 0:128], X[:, c * T + i * 128:c * T + i * 128 + 128],
                    ID[:, :])
            mm.then_inc(b.s("pe"), 1)
            b.inc("pe", 1)

    dv_tile = []
    sx0 = {0: b.cnts["sx0"], 1: b.cnts["sx1"]}

    @block.vector
    def _(vector):
        for i in range(16):
            vector.wait_ge(b.s("pe"), pe0 + i + 1)
            if i >= 8:
                sname = "sx%d" % (i % 2)
                vector.wait_ge(b.s(sname), sx0[i % 2] + ((i - 8) // 2 + 1) * 16)
            for c in range(NCH):
                ins = vector.tensor_copy(
                    RB1[:, (i % 8) * 1024 + c * 128:(i % 8) * 1024 + c * 128 + 128],
                    PS[c][:, 0:128])
            ins.then_inc(b.s("dv"), 1)
            dv_tile.append(b.inc("dv", 1))

    @block.sync
    def _(sync):
        for i in range(16):
            sync.wait_ge(b.s("dv"), dv_tile[i])
            sname = "sx%d" % (i % 2)
            if i >= 2:
                sync.wait_ge(b.s(sname), sx0[i % 2] + (i // 2) * 16)
            sync.dma_start(
                out=out[i * 128:(i + 1) * 128, :],
                in_=RB1[:, (i % 8) * 1024:(i % 8) * 1024 + 1024],
            ).then_inc(b.s(sname), 16)
            b.inc(sname, 16)
        sync.wait_ge(b.s("sx0"), sx0[0] + 8 * 16)
        sync.wait_ge(b.s("sx1"), sx0[1] + 8 * 16)


def attn64(b, block, cfg, state):
    em = E(b)
    state["prev_pair_done"] = cfg["ln_dv"]
    for hp in range(8):
        _attn64_pair(b, em, cfg, hp, state)
    em.emit(block)

    if cfg.get("internal_residual"):
        X, XQH, XQL = b.t["X"], b.t["XQH"], b.t["XQL"]
        last = state["prev_pair_done"]

        @block.vector
        def _(vector):
            vector.wait_ge(b.s("dv"), last)
            for c in range(NCH):
                vector.tensor_tensor(X[:, c * T:(c + 1) * T], X[:, c * T:(c + 1) * T],
                                     XQH[:, c * T:(c + 1) * T], ALU.add)
            vector.drain()
            for c in range(NCH):
                ins = vector.tensor_tensor(
                    X[:, c * T:(c + 1) * T], X[:, c * T:(c + 1) * T],
                    XQL[:, c * T:(c + 1) * T], ALU.add)
            ins.then_inc(b.s("dv"), 1)
            b.inc("dv", 1)
    return b.cnts["dv"]


def _attn64_pair(b, em, cfg, hp, state):
    X, RB1, RB2 = b.t["X"], b.t["RB1"], b.t["RB2"]
    XQH, XQL = b.t["XQH"], b.t["XQL"]
    RBH1, RBH2, ZBH = b.t["RBH1"], b.t["RBH2"], b.t["ZBH"]
    WSF, WSH = b.t["WSF"], b.t["WSH"]
    ONESM, ID, SM = b.t["ONESM"], b.t["IDent"], b.t["SMALL"]
    SELH, PICK32, DORA = b.t["SELH"], b.t["PICK32"], b.t["DORA"]
    PS = [b.t[f"PS{i}"] for i in range(8)]
    scale = cfg["scale"]
    VT, EB, RP = 0, 2080, 4128
    prev_done = state["prev_pair_done"]
    presplit = cfg.get("presplit", False)
    ps_fn = cfg.get("pair_setup")
    bias_waits = ps_fn(em, hp) if ps_fn else []

    split_dv = {}
    split_specs = {}
    wcol_consumed = {}
    prev_wo_pe = state.get("prev_wo_pe")
    for j in range(5):
        slot = j % 2
        if j < 3:
            pieces = [(cfg[("wq_ap", "wk_ap", "wv_ap")[j]](hp, e), e)
                      for e in (0, 1)]
            lanes = (0, 128)
        else:
            pieces = [(cfg["wo_ap"](hp, j - 3), None)]
            lanes = (0, 64)
        sf, sh = "wsf%d" % slot, "wsh%d" % slot
        if not presplit:
            dq_tgt = b.cnts[sf] + 16 * len(pieces)
            b.inc(sf, 16 * len(pieces))
            gq_tgt = b.cnts[sh] + 16 * len(pieces)
            b.inc(sh, 16 * len(pieces))

            @em.on("sync")
            def _(sync, j=j, slot=slot, pieces=pieces, sf=sf):
                if j >= 2:
                    sync.wait_ge(b.s("pe"), wcol_consumed[j - 2])
                elif prev_wo_pe is not None:
                    sync.wait_ge(b.s("pe"), prev_wo_pe)
                for src, e in pieces:
                    if e is not None:
                        dst = WSF[:, slot * 1024:slot * 1024 + 1024].rearrange(
                            "p (c two k) -> p c two k", two=2, k=64)[:, :, e, :]
                    else:
                        dst = WSF[0:64, slot * 1024:slot * 1024 + 1024]
                    sync.dma_start(out=dst, in_=src).then_inc(b.s(sf), 16)

            @em.on("gpsimd")
            def _(gp, j=j, slot=slot, pieces=pieces, sh=sh):
                if j >= 2:
                    gp.wait_ge(b.s("pe"), wcol_consumed[j - 2])
                elif prev_wo_pe is not None:
                    gp.wait_ge(b.s("pe"), prev_wo_pe)
                for src, e in pieces:
                    if e is not None:
                        dst = WSH[:, slot * 2048:slot * 2048 + 1024].rearrange(
                            "p (c two k) -> p c two k", two=2, k=64)[:, :, e, :]
                    else:
                        dst = WSH[0:64, slot * 2048:slot * 2048 + 1024]
                    gp.dma_start(out=dst, in_=src).then_inc(b.s(sh), 16)
        else:
            dq_tgt = b.cnts[sf] + 16 * 2 * len(pieces)
            b.inc(sf, 16 * 2 * len(pieces))
            gq_tgt = None

            @em.on("sync")
            def _(sync, j=j, slot=slot, pieces=pieces, sf=sf):
                if j >= 2:
                    sync.wait_ge(b.s("pe"), wcol_consumed[j - 2])
                elif prev_wo_pe is not None:
                    sync.wait_ge(b.s("pe"), prev_wo_pe)
                for ap2, e in pieces:
                    hi_ap, lo_ap = ap2
                    for h, src_ in ((0, hi_ap), (1, lo_ap)):
                        if e is not None:
                            dst = WSH[:, slot * 2048 + h * 1024:
                                      slot * 2048 + h * 1024 + 1024].rearrange(
                                "p (c two k) -> p c two k", two=2, k=64)[:, :, e, :]
                        else:
                            dst = WSH[0:64, slot * 2048 + h * 1024:
                                      slot * 2048 + h * 1024 + 1024]
                        sync.dma_start(out=dst, in_=src_).then_inc(b.s(sf), 16)

        split_specs[j] = (slot, lanes, dq_tgt, gq_tgt, sf, sh)

    def append_split(j):
        slot, lanes, dq_tgt, gq_tgt, sf, sh = split_specs[j]
        if gq_tgt is None:
            split_dv[j] = (sf, dq_tgt)
            return

        @em.on("vector")
        def _(v, j=j, slot=slot, lanes=lanes, dq_tgt=dq_tgt, gq_tgt=gq_tgt,
              sf=sf, sh=sh):
            v.wait_ge(b.s(sf), dq_tgt)
            v.wait_ge(b.s(sh), gq_tgt)
            p0, p1 = lanes
            v.tensor_tensor(
                WSH[p0:p1, slot * 2048 + 1024:slot * 2048 + 2048],
                WSF[p0:p1, slot * 1024:slot * 1024 + 1024],
                WSH[p0:p1, slot * 2048:slot * 2048 + 1024], ALU.subtract,
            ).then_inc(b.s("dv"), 1)
        split_dv[j] = ("dv", b.inc("dv", 1))

    append_split(0)
    append_split(1)

    evac_dv = {}

    def emit_proj(j, dst_hi, dst_lo, vf32):
        @em.on("tensor")
        def _(tensor, j=j):
            s_sem, s_val = split_dv[j]
            tensor.wait_ge(b.s(s_sem), s_val)
            if j == 0:
                if prev_done is not None:
                    tensor.wait_ge(b.s("dv"), prev_done)
                tensor.wait_ge(b.s("dv"), cfg["ln_dv"])
            else:
                tensor.wait_ge(b.s("dv"), evac_dv[j - 1])
            slot = j % 2
            for c in range(NCH):
                wh = WSH[:, slot * 2048 + c * 128:slot * 2048 + c * 128 + 128]
                wl = WSH[:, slot * 2048 + 1024 + c * 128:
                         slot * 2048 + 1024 + c * 128 + 128]
                for q in range(4):
                    xsl = slice(c * T + q * 512, c * T + q * 512 + 512)
                    tensor.matmul(PS[q][:, 0:512], wh, XQH[:, xsl],
                                  start=(c == 0), stop=False)
                    tensor.matmul(PS[q][:, 0:512], wh, XQL[:, xsl],
                                  start=False, stop=False)
                    mm = tensor.matmul(PS[q][:, 0:512], wl, XQH[:, xsl],
                                       start=False, stop=(c == NCH - 1))
            mm.then_inc(b.s("pe"), 1)
        pe_done = b.inc("pe", 1)
        wcol_consumed[j] = pe_done

        @em.on("vector")
        def _(v, j=j, pe_done=pe_done, dst_hi=dst_hi, dst_lo=dst_lo, vf32=vf32):
            v.wait_ge(b.s("pe"), pe_done)
            if vf32:
                mod = cfg.get("evac_mod")
                srcs = []
                if mod is not None:
                    for sm, val in bias_waits:
                        v.wait_ge(b.s(sm), val)
                for q in range(4):
                    s_ = PS[q][:, 0:512]
                    if mod is not None:
                        s_ = mod(v, hp, j, q, s_)
                    srcs.append(s_)
                if mod is not None:
                    v.drain()
                for q in range(4):
                    ins = v.tensor_copy(
                        RB1[:, EB + q * 512:EB + q * 512 + 512], srcs[q])
            else:
                mod = cfg.get("evac_mod")
                srcs = []
                if mod is not None:
                    for sm, val in bias_waits:
                        v.wait_ge(b.s(sm), val)
                for q in range(4):
                    s_ = PS[q][:, 0:512]
                    if mod is not None:
                        s_ = mod(v, hp, j, q, s_)
                    srcs.append(s_)
                if mod is not None:
                    v.drain()
                for q in range(4):
                    v.tensor_copy(dst_hi[:, q * 512:q * 512 + 512], srcs[q])
                v.drain()
                for q in range(4):
                    ins = v.tensor_tensor(
                        dst_lo[:, q * 512:q * 512 + 512], srcs[q],
                        dst_hi[:, q * 512:q * 512 + 512], ALU.subtract)
            ins.then_inc(b.s("dv"), 1)
        return b.inc("dv", 1)

    evac_dv[0] = emit_proj(0, RBH2[:, 0:2048], RBH2[:, 2048:4096], False)
    append_split(2)
    evac_dv[1] = emit_proj(1, RBH1[:, 0:2048], RBH1[:, 2048:4096], False)
    append_split(3)
    evac_dv[2] = emit_proj(2, None, None, True)
    append_split(4)

    @em.on("vector")
    def _(v):
        if prev_done is not None:
            v.wait_ge(b.s("dv"), prev_done)
        v.memset(RB1[:, VT:VT + 2080], 1.0).then_inc(b.s("dv"), 1)
    vt_ones = b.inc("dv", 1)

    tr_pe, vtr_dv = [], []
    for rnd in range(4):
        tr_pe.append(b.cnts["pe"] + 1)
        b.inc("pe", 1)
        vtr_dv.append(b.cnts["dv"] + 1)
        b.inc("dv", 1)
    for rnd in range(4):
        @em.on("tensor")
        def _(tensor, rnd=rnd):
            if rnd == 0:
                tensor.wait_ge(b.s("dv"), evac_dv[2])
            if rnd >= 2:
                tensor.wait_ge(b.s("dv"), vtr_dv[rnd - 2])
            bank = PS[4 + rnd % 2]
            for kk in range(4):
                kb = rnd * 4 + kk
                mm = tensor.transpose(
                    bank[:, kk * 128:kk * 128 + 128],
                    RB1[:, EB + kb * 128:EB + kb * 128 + 128], ID[:, :])
            mm.then_inc(b.s("pe"), 1)

        @em.on("vector")
        def _(v, rnd=rnd):
            v.wait_ge(b.s("pe"), tr_pe[rnd])
            if rnd == 0:
                v.wait_ge(b.s("dv"), vt_ones)
            bank = PS[4 + rnd % 2]
            for kk in range(4):
                kb = rnd * 4 + kk
                v.tensor_copy(RB1[:, VT + kb * 130:VT + kb * 130 + 64],
                              bank[:, kk * 128:kk * 128 + 64])
                ins = v.tensor_copy(
                    RB1[:, VT + kb * 130 + 65:VT + kb * 130 + 129],
                    bank[:, kk * 128 + 64:kk * 128 + 128])
            ins.then_inc(b.s("dv"), 1)

    ct = cfg.get("ct_rows", False)
    rrow_dv = {}
    if ct:
        ksq_sc, knrm_pe, kred_dv = {}, {}, {}
        for qt in range(4):
            ksq_sc[qt] = b.inc("sc")
        for qt in range(4):
            knrm_pe[qt] = b.inc("pe")
        for qt in range(4):
            kred_dv[qt] = b.inc("dv")

        @em.on("scalar")
        def _(scalar):
            scalar.wait_ge(b.s("dv"), evac_dv[1])
            for qt in range(4):
                if qt >= 2:
                    scalar.wait_ge(b.s("pe"), knrm_pe[qt - 2])
                scalar.activation(
                    ZBH[:, (qt % 2) * 512:(qt % 2) * 512 + 512],
                    RBH1[:, qt * 512:qt * 512 + 512], ACT.Square,
                ).then_inc(b.s("sc"), 1)

        @em.on("tensor")
        def _(tensor):
            for qt in range(4):
                tensor.wait_ge(b.s("sc"), ksq_sc[qt])
                if qt >= 1:
                    tensor.wait_ge(b.s("dv"), kred_dv[qt - 1])
                tensor.matmul(PS[6][0:1, 0:512], SELH[:, 0:1],
                              ZBH[:, (qt % 2) * 512:(qt % 2) * 512 + 512],
                              start=True, stop=True)
                tensor.matmul(PS[6][64:65, 0:512], SELH[:, 1:2],
                              ZBH[:, (qt % 2) * 512:(qt % 2) * 512 + 512],
                              start=True, stop=True).then_inc(b.s("pe"), 1)

        @em.on("vector")
        def _(v):
            for qt in range(4):
                v.wait_ge(b.s("pe"), knrm_pe[qt])
                for p in (0, 64):
                    v.reduce_max(SM[p:p + 1, 12:13], PS[6][p:p + 1, 0:512],
                                 axis=AX)
                v.drain()
                for p in (0, 64):
                    if qt == 0:
                        ins = v.tensor_copy(SM[p:p + 1, 9:10],
                                            SM[p:p + 1, 12:13])
                    else:
                        ins = v.tensor_tensor(SM[p:p + 1, 9:10],
                                              SM[p:p + 1, 9:10],
                                              SM[p:p + 1, 12:13], ALU.max)
                ins.then_inc(b.s("dv"), 1)

        nk_sc = b.inc("sc")
        nk_dv = b.inc("dv")

        @em.on("scalar")
        def _(scalar):
            scalar.wait_ge(b.s("dv"), kred_dv[3])
            scalar.activation(SM[0:1, 10:11], SM[0:1, 9:10], ACT.Sqrt,
                              bias=0.0, scale=1.0404)
            scalar.activation(SM[64:65, 10:11], SM[64:65, 9:10], ACT.Sqrt,
                              bias=0.0, scale=1.0404).then_inc(b.s("sc"), 1)

        @em.on("vector")
        def _(v):
            v.wait_ge(b.s("sc"), nk_sc)
            v.tensor_single_scalar(SM[0:1, 10:11], SM[0:1, 10:11], -1.0,
                                   ALU.mult)
            v.tensor_single_scalar(SM[64:65, 10:11], SM[64:65, 10:11], -1.0,
                                   ALU.mult).then_inc(b.s("dv"), 1)

        rsq_sc, rnrm_pe, rsqrt_sc = {}, {}, {}
        for qt in range(4):
            rsq_sc[qt] = b.inc("sc")
            rsqrt_sc[qt] = b.inc("sc")
        for qt in range(4):
            rnrm_pe[qt] = b.inc("pe")
        for qt in range(4):
            rrow_dv[qt] = b.inc("dv")

        @em.on("scalar")
        def _(scalar):
            for qt in range(4):
                if qt == 0:
                    scalar.wait_ge(b.s("dv"), evac_dv[0])
                if qt >= 2:
                    scalar.wait_ge(b.s("pe"), rnrm_pe[qt - 2])
                scalar.activation(
                    ZBH[:, (qt % 2) * 512:(qt % 2) * 512 + 512],
                    RBH2[:, qt * 512:qt * 512 + 512], ACT.Square,
                ).then_inc(b.s("sc"), 1)
                scalar.wait_ge(b.s("pe"), rnrm_pe[qt])
                scalar.activation(WSF[0:1, qt * 512:qt * 512 + 512],
                                  PS[6][0:1, 0:512], ACT.Sqrt,
                                  bias=0.0, scale=1.0)
                scalar.activation(WSF[64:65, qt * 512:qt * 512 + 512],
                                  PS[6][64:65, 0:512], ACT.Sqrt,
                                  bias=0.0, scale=1.0).then_inc(b.s("sc"), 1)

        @em.on("tensor")
        def _(tensor):
            for qt in range(4):
                tensor.wait_ge(b.s("sc"), rsq_sc[qt])
                if qt == 0:
                    tensor.wait_ge(b.s("dv"), kred_dv[3])
                else:
                    tensor.wait_ge(b.s("sc"), rsqrt_sc[qt - 1])
                tensor.matmul(PS[6][0:1, 0:512], SELH[:, 0:1],
                              ZBH[:, (qt % 2) * 512:(qt % 2) * 512 + 512],
                              start=True, stop=True)
                tensor.matmul(PS[6][64:65, 0:512], SELH[:, 1:2],
                              ZBH[:, (qt % 2) * 512:(qt % 2) * 512 + 512],
                              start=True, stop=True).then_inc(b.s("pe"), 1)

        @em.on("vector")
        def _(v):
            for qt in range(4):
                v.wait_ge(b.s("sc"), rsqrt_sc[qt])
                if qt == 0:
                    v.wait_ge(b.s("dv"), nk_dv)
                v.tensor_scalar(WSF[0:1, qt * 512:qt * 512 + 512],
                                WSF[0:1, qt * 512:qt * 512 + 512],
                                SM[0:1, 10:11], 1.0, ALU.mult, ALU.mult)
                v.tensor_scalar(WSF[64:65, qt * 512:qt * 512 + 512],
                                WSF[64:65, qt * 512:qt * 512 + 512],
                                SM[64:65, 10:11], 1.0, ALU.mult, ALU.mult,
                                ).then_inc(b.s("dv"), 1)
        bias_dv = None

    if not ct:
        sq_sc, nrm_pe, red_dv = {}, {}, {}
        for which in (0, 1):
            for qt in range(4):
                sq_sc[which, qt] = b.inc("sc")
        for which in (0, 1):
            for qt in range(4):
                nrm_pe[which, qt] = b.inc("pe")
        for which in (0, 1):
            for qt in range(4):
                red_dv[which, qt] = b.inc("dv")

        for which, SRC in ((0, RBH2), (1, RBH1)):
            @em.on("scalar")
            def _(scalar, which=which, SRC=SRC):
                scalar.wait_ge(b.s("dv"), evac_dv[which])
                for qt in range(4):
                    if (which, qt) > (0, 1):
                        prev = (which, qt - 2) if qt >= 2 else (which - 1, qt + 2)
                        scalar.wait_ge(b.s("pe"), nrm_pe[prev])
                    scalar.activation(
                        ZBH[:, (qt % 2) * 512:(qt % 2) * 512 + 512],
                        SRC[:, qt * 512:qt * 512 + 512], ACT.Square,
                    ).then_inc(b.s("sc"), 1)

            @em.on("tensor")
            def _(tensor, which=which):
                for qt in range(4):
                    tensor.wait_ge(b.s("sc"), sq_sc[which, qt])
                    if (which, qt) > (0, 0):
                        prev = (which, qt - 1) if qt >= 1 else (which - 1, 3)
                        tensor.wait_ge(b.s("dv"), red_dv[prev])
                    tensor.matmul(PS[6][0:2, 0:512], SELH[:, 0:2],
                                  ZBH[:, (qt % 2) * 512:(qt % 2) * 512 + 512],
                                  start=True, stop=True).then_inc(b.s("pe"), 1)

            @em.on("vector")
            def _(v, which=which):
                for qt in range(4):
                    v.wait_ge(b.s("pe"), nrm_pe[which, qt])
                    v.reduce_max(SM[0:2, 12:13], PS[6][0:2, 0:512], axis=AX)
                    v.drain()
                    if qt == 0:
                        ins = v.tensor_copy(SM[0:2, 8 + which:9 + which],
                                            SM[0:2, 12:13])
                    else:
                        ins = v.tensor_tensor(SM[0:2, 8 + which:9 + which],
                                              SM[0:2, 8 + which:9 + which],
                                              SM[0:2, 12:13], ALU.max)
                    ins.then_inc(b.s("dv"), 1)

        @em.on("vector")
        def _(v):
            v.drain()
            v.tensor_tensor(SM[0:2, 10:11], SM[0:2, 8:9], SM[0:2, 9:10],
                            ALU.mult).then_inc(b.s("dv"), 1)
        cc_dv = b.inc("dv", 1)

        @em.on("scalar")
        def _(scalar):
            scalar.wait_ge(b.s("dv"), cc_dv)
            scalar.activation(SM[0:2, 11:12], SM[0:2, 10:11], ACT.Sqrt,
                              bias=0.0, scale=1.0404 * scale * scale,
                              ).then_inc(b.s("sc"), 1)
        csqrt_sc = b.inc("sc", 1)

        @em.on("vector")
        def _(v):
            v.wait_ge(b.s("sc"), csqrt_sc)
            v.tensor_single_scalar(SM[0:2, 11:12], SM[0:2, 11:12], -1.0, ALU.mult)
            v.drain()
            v.tensor_copy(b.t["NEGCH"][0:2, 0:1],
                          SM[0:2, 11:12]).then_inc(b.s("dv"), 1)
        negc_dv = b.inc("dv", 1)

        @em.on("tensor")
        def _(tensor):
            tensor.wait_ge(b.s("dv"), negc_dv)
            tensor.matmul(PS[6][:, 0:1], PICK32[0:2, 0:128], b.t["NEGCH"][0:2, 0:1],
                          start=True, stop=True)
            mm = tensor.matmul(PS[7][:, 0:1], PICK32[0:2, 128:256],
                               b.t["NEGCH"][0:2, 0:1], start=True, stop=True)
            mm.then_inc(b.s("pe"), 1)
        bias_pe = b.inc("pe", 1)

        @em.on("vector")
        def _(v):
            v.wait_ge(b.s("pe"), bias_pe)
            v.tensor_copy(SM[:, 0:1], PS[6][:, 0:1])
            v.tensor_copy(SM[:, 1:2], PS[7][:, 0:1]).then_inc(b.s("dv"), 1)
        bias_dv = b.inc("dv", 1)

    sc_exp = {}
    pe_scores = {}
    pe_av = {}
    dv_z = {}

    def _avmm(tensor, qt, kb):
        tensor.wait_ge(b.s("sc"), sc_exp[qt, kb])
        if kb == 0:
            if qt == 0:
                tensor.wait_ge(b.s("dv"), vtr_dv[3])
            else:
                tensor.wait_ge(b.s("dv"), dv_z[qt - 1])
        for e in (0, 1):
            vt_ap = RB1[:, VT + kb * 130 + e * 65:VT + kb * 130 + e * 65 + 65]
            e_ap = RB1[:, EB + (kb % 2) * 1024 + e * 512:
                       EB + (kb % 2) * 1024 + e * 512 + 512]
            mm = tensor.matmul(PS[e][0:65, 0:512], vt_ap, e_ap,
                               start=(kb == 0), stop=(kb == 15))
        mm.then_inc(b.s("pe"), 1)

    for qt in range(4):
        for kb in range(16):
            pe_scores[qt, kb] = b.inc("pe")
            if kb >= 1:
                pe_av[qt, kb - 1] = b.inc("pe")
        pe_av[qt, 15] = b.inc("pe")
        for kb in range(16):
            sc_exp[qt, kb] = b.inc("sc")

        @em.on("tensor")
        def _(tensor, qt=qt):
            for kb in range(16):
                sb_, ob_ = (2, 3) if kb % 2 == 0 else (4, 5)
                if qt == 0 and kb == 0:
                    tensor.wait_ge(b.s("dv"), evac_dv[0])
                    tensor.wait_ge(b.s("dv"), evac_dv[1])
                if kb >= 2:
                    tensor.wait_ge(b.s("sc"), sc_exp[qt, kb - 2])
                elif qt >= 1:
                    tensor.wait_ge(b.s("sc"), sc_exp[qt - 1, 14 + kb])
                if ct and kb == 0:
                    tensor.wait_ge(b.s("dv"), rrow_dv[qt])
                for e in (0, 1):
                    bank = PS[sb_] if e == 0 else PS[ob_]
                    r0 = 64 * e
                    kh = RBH1[r0:r0 + 64, kb * 128:kb * 128 + 128]
                    kl = RBH1[r0:r0 + 64, 2048 + kb * 128:2048 + kb * 128 + 128]
                    qh = RBH2[r0:r0 + 64, qt * 512:qt * 512 + 512]
                    ql = RBH2[r0:r0 + 64, 2048 + qt * 512:2048 + qt * 512 + 512]
                    tp = (r0, 0)
                    tensor.matmul(bank[:, 0:512], kh, qh, start=True, stop=False,
                                  tile_position=tp)
                    tensor.matmul(bank[:, 0:512], kh, ql, start=False, stop=False,
                                  tile_position=tp)
                    mm = tensor.matmul(bank[:, 0:512], kl, qh, start=False,
                                       stop=not ct, tile_position=tp)
                    if ct:
                        mm = tensor.matmul(
                            bank[:, 0:512], ONESM[r0:r0 + 1, 0:128],
                            WSF[r0:r0 + 1, qt * 512:qt * 512 + 512],
                            start=False, stop=True)
                mm.then_inc(b.s("pe"), 1)
                if kb >= 1:
                    _avmm(tensor, qt, kb - 1)
            _avmm(tensor, qt, 15)

        @em.on("scalar")
        def _(scalar, qt=qt):
            for kb in range(16):
                sb_, ob_ = (2, 3) if kb % 2 == 0 else (4, 5)
                scalar.wait_ge(b.s("pe"), pe_scores[qt, kb])
                if qt == 0 and kb == 0 and not ct:
                    scalar.wait_ge(b.s("dv"), bias_dv)
                if kb >= 2:
                    scalar.wait_ge(b.s("pe"), pe_av[qt, kb - 2])
                eb0 = 0.0 if ct else SM[:, 0:1]
                eb1 = 0.0 if ct else SM[:, 1:2]
                scalar.activation(
                    RB1[:, EB + (kb % 2) * 1024:EB + (kb % 2) * 1024 + 512],
                    PS[sb_][:, 0:512], ACT.Exp, bias=eb0, scale=scale)
                scalar.activation(
                    RB1[:, EB + (kb % 2) * 1024 + 512:EB + (kb % 2) * 1024 + 1024],
                    PS[ob_][:, 0:512], ACT.Exp, bias=eb1, scale=scale,
                ).then_inc(b.s("sc"), 1)

        rc_dv = b.inc("dv")
        rp_pe = b.inc("pe")
        z_dv = b.inc("dv")
        dv_z[qt] = z_dv
        wo_pe = [b.inc("pe") for _ in range(16)]
        xacc = [b.inc("dv") for _ in range(16)]
        if qt == 3:
            wcol_consumed[3] = wo_pe[15]
            wcol_consumed[4] = wo_pe[15]
            state["prev_wo_pe"] = wo_pe[15]

        @em.on("vector")
        def _(v, qt=qt):
            v.wait_ge(b.s("pe"), pe_av[qt, 15])
            v.reciprocal(RB2[64:65, 0:512], PS[0][64:65, 0:512])
            v.reciprocal(RB2[64:65, 512:1024], PS[1][64:65, 0:512],
                         ).then_inc(b.s("dv"), 1)

        @em.on("tensor")
        def _(tensor, qt=qt, rc_dv=rc_dv):
            tensor.wait_ge(b.s("dv"), rc_dv)
            tensor.matmul(PS[6][:, 0:512], ONESM[64:65, 0:128],
                          RB2[64:65, 0:512], start=True, stop=True)
            mm = tensor.matmul(PS[7][:, 0:512], ONESM[64:65, 0:128],
                               RB2[64:65, 512:1024], start=True, stop=True)
            mm.then_inc(b.s("pe"), 1)

        @em.on("vector")
        def _(v, qt=qt, rp_pe=rp_pe):
            v.wait_ge(b.s("pe"), rp_pe)
            v.tensor_copy(RB1[:, RP:RP + 512], PS[6][:, 0:512])
            v.tensor_copy(RB1[:, RP + 512:RP + 1024], PS[7][:, 0:512])
            v.drain()
            for e in (0, 1):
                v.tensor_tensor(ZBH[0:64, 1024 * e:1024 * e + 512],
                                PS[e][0:64, 0:512],
                                RB1[0:64, RP + 512 * e:RP + 512 * e + 512],
                                ALU.mult)
            for e in (0, 1):
                v.tensor_tensor(RB2[0:64, 1024 + 512 * e:1536 + 512 * e],
                                PS[e][0:64, 0:512],
                                RB1[0:64, RP + 512 * e:RP + 512 * e + 512],
                                ALU.mult)
            v.drain()
            for e in (0, 1):
                ins = v.tensor_tensor(
                    ZBH[0:64, 1024 * e + 512:1024 * e + 1024],
                    RB2[0:64, 1024 + 512 * e:1536 + 512 * e],
                    ZBH[0:64, 1024 * e:1024 * e + 512], ALU.subtract)
            ins.then_inc(b.s("dv"), 1)

        @em.on("tensor")
        def _(tensor, qt=qt, z_dv=z_dv, xacc=xacc):
            tensor.wait_ge(b.s("dv"), z_dv)
            if qt == 0:
                for jj in (3, 4):
                    w_sem, w_val = split_dv[jj]
                    tensor.wait_ge(b.s(w_sem), w_val)
            idx = 0
            for e in (0, 1):
                slot = 1 if e == 0 else 0
                for do in range(8):
                    bank = PS[6 + (idx % 2)]
                    if idx >= 2:
                        tensor.wait_ge(b.s("dv"), xacc[idx - 2])
                    wh = WSH[0:64,
                             slot * 2048 + do * 128:slot * 2048 + do * 128 + 128]
                    wl = WSH[0:64, slot * 2048 + 1024 + do * 128:
                             slot * 2048 + 1024 + do * 128 + 128]
                    zh = ZBH[0:64, 1024 * e:1024 * e + 512]
                    zl = ZBH[0:64, 1024 * e + 512:1024 * e + 1024]
                    tensor.matmul(bank[:, 0:512], wh, zh, start=True, stop=False)
                    tensor.matmul(bank[:, 0:512], wh, zl, start=False, stop=False)
                    mm = tensor.matmul(bank[:, 0:512], wl, zh, start=False,
                                       stop=True)
                    mm.then_inc(b.s("pe"), 1)
                    idx += 1

        @em.on("vector")
        def _(v, qt=qt, wo_pe=wo_pe):
            idx = 0
            xb = cfg.get("xacc_bias")
            for e in (0, 1):
                for do in range(8):
                    bank = PS[6 + (idx % 2)]
                    v.wait_ge(b.s("pe"), wo_pe[idx])
                    sl = slice(do * T + qt * 512, do * T + qt * 512 + 512)
                    if xb is not None and e == 0 and hp == 0:
                        ins = v.scalar_tensor_tensor(
                            X[:, sl], bank[:, 0:512], xb(do), X[:, sl],
                            ALU.add, ALU.add)
                    else:
                        ins = v.tensor_tensor(X[:, sl], X[:, sl],
                                              bank[:, 0:512], ALU.add)
                    ins.then_inc(b.s("dv"), 1)
                    idx += 1

    state["prev_pair_done"] = b.cnts["dv"]


def prep_T(b, block, state, blocks, scr):
    WSF, WSH, ID = b.t["WSF"], b.t["WSH"], b.t["IDent"]
    PS = [b.t[f"PS{i}"] for i in range(8)]
    em = E(b)
    n = len(blocks)
    ld_tgt, tr_pe, mid_dv, sp_dv, st_tgt = {}, {}, {}, {}, {}
    for i in range(n):
        slot = i % 2
        ld_tgt[i] = b.inc("wsf%d" % slot, 16)
    for i in range(n):
        tr_pe[i] = b.inc("pe")
        tr_pe[i, 1] = b.inc("pe")
    for i in range(n):
        mid_dv[i] = b.inc("dv")
        sp_dv[i] = b.inc("dv")
    for i in range(n):
        slot = i % 2
        st_tgt[i] = b.inc("wsh%d" % slot, 32)

    for i in range(n):
        src_ap, col0 = blocks[i]
        slot = i % 2

        @em.on("sync")
        def _(sync, i=i, slot=slot, src_ap=src_ap):
            if i >= 2:
                sync.wait_ge(b.s("wsh%d" % slot), st_tgt[i - 2])
            elif state.get("prev_wo_pe") is not None:
                sync.wait_ge(b.s("pe"), state["prev_wo_pe"])
            sync.dma_start(
                out=WSF[:, slot * 1024:slot * 1024 + 1024],
                in_=src_ap,
            ).then_inc(b.s("wsf%d" % slot), 16)

        @em.on("tensor")
        def _(tensor, i=i, slot=slot):
            tensor.wait_ge(b.s("wsf%d" % slot), ld_tgt[i])
            if i >= 2:
                tensor.wait_ge(b.s("dv"), sp_dv[i - 2])
            bank = PS[0 + slot]
            for half in range(2):
                if half == 1:
                    tensor.wait_ge(b.s("dv"), mid_dv[i])
                for cc in range(4 * half, 4 * half + 4):
                    mm = tensor.transpose(
                        bank[:, (cc % 4) * 128:(cc % 4) * 128 + 128],
                        WSF[:, slot * 1024 + cc * 128:slot * 1024 + cc * 128 + 128],
                        ID[:, :])
                mm.then_inc(b.s("pe"), 1)

        @em.on("vector")
        def _(v, i=i, slot=slot):
            bank = PS[0 + slot]
            for half in range(2):
                v.wait_ge(b.s("pe"), tr_pe[i] if half == 0 else tr_pe[i, 1])
                for cc in range(4 * half, 4 * half + 4):
                    v.tensor_copy(
                        WSH[:, slot * 2048 + cc * 128:slot * 2048 + cc * 128 + 128],
                        bank[:, (cc % 4) * 128:(cc % 4) * 128 + 128])
                v.drain()
                for cc in range(4 * half, 4 * half + 4):
                    ins = v.tensor_tensor(
                        WSH[:, slot * 2048 + 1024 + cc * 128:
                            slot * 2048 + 1024 + cc * 128 + 128],
                        bank[:, (cc % 4) * 128:(cc % 4) * 128 + 128],
                        WSH[:, slot * 2048 + cc * 128:slot * 2048 + cc * 128 + 128],
                        ALU.subtract)
                ins.then_inc(b.s("dv"), 1)

        @em.on("gpsimd")
        def _(gp, i=i, slot=slot, col0=col0):
            gp.wait_ge(b.s("dv"), sp_dv[i])
            for h in range(2):
                src_s = WSH[:, slot * 2048 + h * 1024:
                            slot * 2048 + h * 1024 + 1024].rearrange(
                    "q (c d) -> q c d", c=8)
                dst = scr[h, :, col0:col0 + 128].rearrange(
                    "(c q) d -> q c d", q=128)
                gp.dma_start(out=dst, in_=src_s).then_inc(
                    b.s("wsh%d" % slot), 16)
    em.emit(block)
    return (b.cnts["wsh0"], b.cnts["wsh1"])


def ffn_prep(b, block, state):
    if state.get("ffn_prep_done"):
        return
    names = ["L0_ffn_W_r", "L0_ffn_W_k", "L0_ffn_W_v"]
    scr = b.t["scr_ffn"].rearrange("(h c d) -> h c d", h=2, c=1024)
    blocks = []
    for p in range(3):
        for dc in range(8):
            blocks.append((b.t[names[p]][dc * 128:dc * 128 + 128, :],
                           p * 1024 + dc * 128))
    state["ffn_prep_done"] = prep_T(b, block, state, blocks, scr)


def sub_l0_ffn(b, block, ln_dv, state):
    state["ffn_attn_wo_pe"] = state.get("prev_wo_pe")
    state.pop("ffn_prev_out_dv", None)
    ffn_prep(b, block, state)
    prep_done = state["ffn_prep_done"]
    X, RB1, RB2 = b.t["X"], b.t["RB1"], b.t["RB2"]
    XQH, XQL = b.t["XQH"], b.t["XQL"]
    RBH1, RBH2 = b.t["RBH1"], b.t["RBH2"]
    WSF, WSH, SM = b.t["WSF"], b.t["WSH"], b.t["SMALL"]
    PS = [b.t[f"PS{i}"] for i in range(8)]
    em = E(b)
    XL, T1, T2, S4 = 0, 2048, 4096, 6144

    mu_tgt = b.inc("dq", 16)
    muk_tgt = b.inc("dq", 16)
    mu_dv = b.inc("dv")

    @em.on("sync")
    def _(sync):
        sync.wait_ge(b.s("wsh0"), prep_done[0])
        sync.wait_ge(b.s("wsh1"), prep_done[1])
        sync.dma_start(out=SM[:, 32:40],
                       in_=b.t["L0_ffn_mu_r"].rearrange("(c p) -> p c", p=128),
                       allow_slow_non_contiguous=True).then_inc(b.s("dq"), 16)
        sync.dma_start(out=SM[:, 40:48],
                       in_=b.t["L0_ffn_mu_k"].rearrange("(c p) -> p c", p=128),
                       allow_slow_non_contiguous=True).then_inc(b.s("dq"), 16)

    @em.on("vector")
    def _(v):
        v.wait_ge(b.s("dq"), muk_tgt)
        v.tensor_scalar(SM[:, 48:56], SM[:, 32:40], -1.0, 1.0, ALU.mult, ALU.add)
        v.tensor_scalar(SM[:, 56:64], SM[:, 40:48], -1.0, 1.0,
                        ALU.mult, ALU.add).then_inc(b.s("dv"), 1)

    for tt in range(8):
        _ffn_tile(b, em, tt, ln_dv, mu_dv, state)
    em.emit(block)
    state["ffn_call"] = state.get("ffn_call", 0) + 1
    return b.cnts["dv"]


def _ffn_tile(b, em, tt, ln_dv, mu_dv, state):
    X, RB1, RB2 = b.t["X"], b.t["RB1"], b.t["RB2"]
    XQH, XQL = b.t["XQH"], b.t["XQL"]
    RBH1, RBH2 = b.t["RBH1"], b.t["RBH2"]
    WSF, WSH, SM = b.t["WSF"], b.t["WSH"], b.t["SMALL"]
    PS = [b.t[f"PS{i}"] for i in range(8)]
    scr = b.t["scr_ffn"].rearrange("(h c d) -> h c d", h=2, c=1024)
    XL, T1, T2, S4 = 0, 2048, 4096, 6144
    t0 = tt * 256
    bnd_w = 16 + (tt % 2) * 8
    bnd_r = 16 + ((tt + 1) % 2) * 8

    def sap(coloff, dims):
        base = RB1[:, 0:1]
        return bass.AP(base.tensor, base.offset + coloff,
                       [list(base.ap[0])] + dims)

    def smap(coloff, dims):
        base = SM[:, 0:1]
        return bass.AP(base.tensor, base.offset + coloff,
                       [list(base.ap[0])] + dims)

    mix_dv = b.inc("dv")
    prev_out = state.get("ffn_prev_out_dv")

    @em.on("vector")
    def _(v, tt=tt):
        if tt == 0:
            v.wait_ge(b.s("dv"), ln_dv)
            v.wait_ge(b.s("dv"), mu_dv)
        for c in range(NCH):
            v.tensor_tensor(RB1[:, XL + c * 256:XL + c * 256 + 256],
                            XQH[:, c * T + t0:c * T + t0 + 256],
                            XQL[:, c * T + t0:c * T + t0 + 256], ALU.add)
        v.drain()
        v.tensor_copy(smap(bnd_w, [[1, 8], [1, 1]]),
                      sap(XL + 255, [[256, 8], [1, 1]]))
        v.tensor_tensor(sap(T1 + 1, [[256, 8], [1, 255]]),
                        sap(XL + 1, [[256, 8], [1, 255]]),
                        sap(XL, [[256, 8], [1, 255]]), ALU.subtract)
        if tt == 0:
            v.tensor_copy(sap(T1, [[256, 8], [1, 1]]),
                          sap(XL, [[256, 8], [1, 1]]))
        else:
            v.tensor_tensor(sap(T1, [[256, 8], [1, 1]]),
                            sap(XL, [[256, 8], [1, 1]]),
                            smap(bnd_r, [[1, 8], [1, 1]]), ALU.subtract)
        v.drain()
        for which, mucol in ((0, 48), (1, 56)):
            mub = smap(mucol, [[1, 8], [0, 256]])
            v.tensor_tensor(sap(T2, [[256, 8], [1, 256]]),
                            sap(T1, [[256, 8], [1, 256]]), mub, ALU.mult)
            v.drain()
            v.tensor_tensor(RB1[:, S4:S4 + 2048], RB1[:, XL:XL + 2048],
                            RB1[:, T2:T2 + 2048], ALU.subtract)
            v.drain()
            H = RBH1 if which == 0 else RBH2
            v.tensor_copy(H[:, 0:2048], RB1[:, S4:S4 + 2048])
            v.drain()
            ins = v.tensor_tensor(H[:, 2048:4096], RB1[:, S4:S4 + 2048],
                                  H[:, 0:2048], ALU.subtract)
            v.drain()
        ins.then_inc(b.s("dv"), 1)

    wcons = state.setdefault("ffn_wcons", {})
    callid = state.get("ffn_call", 0)
    col_ld = {}
    for idx in range(24):
        slot = idx % 2
        col_ld[idx] = b.inc("wsf%d" % slot, 32)
    mm_pe = {}
    for p in range(3):
        mm_pe[p] = [b.inc("pe") for _ in range(8)]
    for idx in range(24):
        p, do = divmod(idx, 8)
        wcons[(callid, tt * 24 + idx)] = mm_pe[p][do]
    r_sc = b.inc("sc")
    k_sc = b.inc("sc")
    ksp_dv = b.inc("dv")
    out_dv = b.inc("dv")
    state["ffn_prev_out_dv"] = out_dv
    state["prev_wo_pe"] = mm_pe[2][7]

    @em.on("sync")
    def _(sync, tt=tt):
        for idx in range(24):
            p, do = divmod(idx, 8)
            slot = idx % 2
            gidx = tt * 24 + idx - 2
            if gidx >= 0 and (tt > 0 or idx >= 2):
                sync.wait_ge(b.s("pe"), wcons[(callid, gidx)])
            elif tt == 0 and idx < 2 and state.get("ffn_attn_wo_pe") is not None:
                sync.wait_ge(b.s("pe"), state["ffn_attn_wo_pe"])
            for h in range(2):
                sync.dma_start(
                    out=WSH[:, slot * 2048 + h * 1024:slot * 2048 + h * 1024 + 1024],
                    in_=scr[h, :, p * 1024 + do * 128:p * 1024 + do * 128 + 128
                            ].rearrange("(c q) d -> q c d", q=128),
                ).then_inc(b.s("wsf%d" % slot), 16)

    @em.on("tensor")
    def _(tensor, tt=tt, prev_out=prev_out):
        for p in range(3):
            for do in range(8):
                idx = p * 8 + do
                slot = idx % 2
                tensor.wait_ge(b.s("wsf%d" % slot), col_ld[idx])
                if idx == 0:
                    tensor.wait_ge(b.s("dv"), mix_dv)
                    if prev_out is not None:
                        tensor.wait_ge(b.s("dv"), prev_out)
                if p == 1 and do == 0:
                    tensor.wait_ge(b.s("sc"), r_sc)
                if p == 2 and do == 0:
                    tensor.wait_ge(b.s("dv"), ksp_dv)
                H = RBH1 if p == 0 else (RBH2 if p == 1 else RBH1)
                bank = PS[do]
                for c in range(NCH):
                    wh = WSH[:, slot * 2048 + c * 128:slot * 2048 + c * 128 + 128]
                    wl = WSH[:, slot * 2048 + 1024 + c * 128:
                             slot * 2048 + 1024 + c * 128 + 128]
                    tensor.matmul(bank[:, 0:256], wh, H[:, c * 256:c * 256 + 256],
                                  start=(c == 0), stop=False)
                    tensor.matmul(bank[:, 0:256], wh,
                                  H[:, 2048 + c * 256:2048 + c * 256 + 256],
                                  start=False, stop=False)
                    mm = tensor.matmul(bank[:, 0:256], wl,
                                       H[:, c * 256:c * 256 + 256],
                                       start=False, stop=(c == NCH - 1))
                mm.then_inc(b.s("pe"), 1)

    @em.on("scalar")
    def _(scalar, tt=tt):
        for do in range(8):
            scalar.wait_ge(b.s("pe"), mm_pe[0][do])
            if do == 0:
                scalar.wait_ge(b.s("dv"), mix_dv)
            ins = scalar.activation(RB1[:, XL + do * 256:XL + do * 256 + 256],
                                    PS[do][:, 0:256], ACT.Sigmoid)
        ins.then_inc(b.s("sc"), 1)
        for do in range(8):
            scalar.wait_ge(b.s("pe"), mm_pe[1][do])
            scalar.activation(RB1[:, T1 + do * 256:T1 + do * 256 + 256],
                              PS[do][:, 0:256], ACT.Relu)
        scalar.drain()
        for do in range(8):
            ins = scalar.activation(RB1[:, T2 + do * 256:T2 + do * 256 + 256],
                                    RB1[:, T1 + do * 256:T1 + do * 256 + 256],
                                    ACT.Square)
        ins.then_inc(b.s("sc"), 1)

    @em.on("vector")
    def _(v, tt=tt):
        v.wait_ge(b.s("sc"), k_sc)
        v.wait_ge(b.s("pe"), mm_pe[0][7])
        v.tensor_copy(RBH1[:, 0:2048], RB1[:, T2:T2 + 2048])
        v.drain()
        v.tensor_tensor(RBH1[:, 2048:4096], RB1[:, T2:T2 + 2048],
                        RBH1[:, 0:2048], ALU.subtract).then_inc(b.s("dv"), 1)
        for do in range(8):
            v.wait_ge(b.s("pe"), mm_pe[2][do])
            v.tensor_tensor(RB1[:, S4 + do * 256:S4 + do * 256 + 256],
                            PS[do][:, 0:256],
                            RB1[:, XL + do * 256:XL + do * 256 + 256], ALU.mult)
        v.drain()
        for do in range(8):
            sl = slice(do * T + t0, do * T + t0 + 256)
            ins = v.tensor_tensor(X[:, sl], X[:, sl],
                                  RB1[:, S4 + do * 256:S4 + do * 256 + 256],
                                  ALU.add)
        ins.then_inc(b.s("dv"), 1)


def sub_l0_attn_b(b, block, ln_dv, state):
    X, RB1, RB2 = b.t["X"], b.t["RB1"], b.t["RB2"]
    XQH, XQL = b.t["XQH"], b.t["XQL"]
    RBH1, RBH2, ZBH = b.t["RBH1"], b.t["RBH2"], b.t["ZBH"]
    WSF, WSH, SM = b.t["WSF"], b.t["WSH"], b.t["SMALL"]
    ONESM, ID = b.t["ONESM"], b.t["IDent"]
    SEL32, PICK32 = b.t["SEL32"], b.t["PICK32"]
    PS = [b.t[f"PS{i}"] for i in range(8)]
    wqkv = b.t["L0_attn_b_w_qkv"].rearrange("(c p) e -> p c e", p=128)
    wout = b.t["L0_attn_b_w_out"]
    scale = 0.17677669529663687
    VT, EB, RP, WS_ = 0, 2112, 4160, 6208
    em = E(b)
    prev_done = ln_dv
    prev_wo_pe = state.get("prev_wo_pe")
    state["ffn_attn_wo_pe"] = None

    par_tgt = b.cnts["dq"] + 48
    b.inc("dq", 48)

    @block.sync
    def _(sync):
        for i, nm in enumerate(["L0_attn_b_ln_gamma", "L0_attn_b_ln_beta",
                                "L0_attn_b_b_out"]):
            sync.dma_start(out=SM[:, 64 + 8 * i:72 + 8 * i],
                           in_=b.t[nm].rearrange("(c p) -> p c", p=128),
                           allow_slow_non_contiguous=True).then_inc(b.s("dq"), 16)

    @block.vector
    def _(v):
        v.memset(RB2[32:64, 0:1024], 0.0).then_inc(b.s("dv"), 1)
        v.memset(RB2[96:128, 0:1024], 0.0).then_inc(b.s("dv"), 1)
    b.inc("dv", 1)
    zb0_dv = b.inc("dv", 1)
    for g in range(ATTNB_QUADS):
        prev_done, prev_wo_pe = _attnb_quad(
            b, em, g, prev_done, prev_wo_pe, zb0_dv, scale,
            wqkv, wout, VT, EB, RP, WS_, state)
    em.emit(block)
    state["prev_wo_pe"] = prev_wo_pe
    if ATTNB_CUT >= 1:
        return prev_done

    scro = b.t["scr_out"]
    pe0, dv0, sc0 = b.cnts["pe"], b.cnts["dv"], b.cnts["sc"]
    acc_done0, acc_done1 = b.cnts["ac0"], b.cnts["ac1"]
    PS4, PS5, PS6, PS7 = PS[4], PS[5], PS[6], PS[7]
    ld_tgt = {}
    for tt in range(8):
        ld_tgt[tt] = b.inc("dx%d" % (tt % 2), 16)
    b.inc("dv", 32)
    b.inc("pe", 16)
    b.inc("sc", 16)

    @block.sync
    def _(sync):
        sync.wait_ge(b.s("ac0"), acc_done0)
        sync.wait_ge(b.s("ac1"), acc_done1)
        for tt in range(8):
            if tt >= 2:
                sync.wait_ge(b.s("dv"), dv0 + 4 * (tt - 2) + 4)
            sync.dma_start(
                out=RB1[:, (tt % 2) * 4096:(tt % 2) * 4096 + 2048].rearrange(
                    "p (c t) -> p c t", c=8),
                in_=scro.rearrange("(c p) t -> p c t", p=128)[
                    :, :, tt * 256:tt * 256 + 256],
            ).then_inc(b.s("dx%d" % (tt % 2)), 16)

    @block.vector
    def _(vector):
        for tt in range(8):
            a0 = (tt % 2) * 4096
            vector.wait_ge(b.s("dx%d" % (tt % 2)), ld_tgt[tt])
            if tt == 0:
                vector.wait_ge(b.s("dq"), par_tgt)
            for c in range(NCH):
                ins = vector.tensor_scalar_add(
                    RB1[:, a0 + c * 256:a0 + c * 256 + 256],
                    RB1[:, a0 + c * 256:a0 + c * 256 + 256],
                    SM[:, 80 + c:81 + c])
            ins.then_inc(b.s("dv"), 1)
            vector.wait_ge(b.s("pe"), pe0 + 2 * tt + 1)
            vector.tensor_single_scalar(
                RB2[0:1, 0:256], PS6[0:1, 0:256], 1.0 / D, ALU.mult)
            vector.drain()
            vector.tensor_tensor(
                RB2[0:1, 1536:1792], RB2[0:1, 0:256], RB2[0:1, 0:256], ALU.mult)
            vector.drain()
            vector.scalar_tensor_tensor(
                RB2[0:1, 512:768], PS7[0:1, 0:256], 1.0 / D,
                RB2[0:1, 1536:1792], ALU.mult, ALU.subtract).then_inc(b.s("dv"), 1)
            vector.wait_ge(b.s("sc"), sc0 + 2 * tt + 2)
            vector.reciprocal(
                RB2[0:1, 1024:1280], RB2[0:1, 512:768]).then_inc(b.s("dv"), 1)
            vector.wait_ge(b.s("pe"), pe0 + 2 * tt + 2)
            sls = []
            for c in range(NCH):
                sls.append((slice(c * T + tt * 256, c * T + tt * 256 + 256),
                            slice(a0 + c * 256, a0 + c * 256 + 256)))
            for sl, xl in sls:
                vector.tensor_tensor(RB1[:, xl], RB1[:, xl], PS4[:, 0:256],
                                     ALU.subtract)
            vector.drain()
            for sl, xl in sls:
                vector.tensor_tensor(RB1[:, xl], RB1[:, xl], PS5[:, 0:256],
                                     ALU.mult)
            vector.drain()
            for c in range(NCH):
                sl, xl = sls[c]
                vector.tensor_scalar(RB1[:, xl], RB1[:, xl],
                                     SM[:, 64 + c:65 + c], SM[:, 72 + c:73 + c],
                                     ALU.mult, ALU.add)
            vector.drain()
            for sl, xl in sls:
                ins = vector.tensor_tensor(X[:, sl], X[:, sl], RB1[:, xl],
                                           ALU.add)
            ins.then_inc(b.s("dv"), 1)

    @block.scalar
    def _(scalar):
        for tt in range(8):
            a0 = (tt % 2) * 4096
            scalar.wait_ge(b.s("dv"), dv0 + 4 * tt + 1)
            for c in range(NCH):
                ins = scalar.activation(
                    RB1[:, a0 + 2048 + c * 256:a0 + 2048 + c * 256 + 256],
                    RB1[:, a0 + c * 256:a0 + c * 256 + 256], ACT.Square)
            ins.then_inc(b.s("sc"), 1)
            scalar.wait_ge(b.s("dv"), dv0 + 4 * tt + 2)
            scalar.activation(
                RB2[0:1, 512:768], RB2[0:1, 512:768],
                ACT.Sqrt, bias=1e-6, scale=1.0).then_inc(b.s("sc"), 1)

    @block.tensor
    def _(tensor):
        for tt in range(8):
            a0 = (tt % 2) * 4096
            tensor.wait_ge(b.s("dv"), dv0 + 4 * tt + 1)
            if tt >= 1:
                tensor.wait_ge(b.s("dv"), dv0 + 4 * (tt - 1) + 2)
            for c in range(NCH):
                tensor.matmul(
                    PS6[0:1, 0:256], ONESM[:, 0:1],
                    RB1[:, a0 + c * 256:a0 + c * 256 + 256],
                    start=(c == 0), stop=(c == NCH - 1))
            tensor.wait_ge(b.s("sc"), sc0 + 2 * tt + 1)
            for c in range(NCH):
                mm = tensor.matmul(
                    PS7[0:1, 0:256], ONESM[:, 0:1],
                    RB1[:, a0 + 2048 + c * 256:a0 + 2048 + c * 256 + 256],
                    start=(c == 0), stop=(c == NCH - 1))
            mm.then_inc(b.s("pe"), 1)
            tensor.wait_ge(b.s("dv"), dv0 + 4 * tt + 3)
            if tt >= 1:
                tensor.wait_ge(b.s("dv"), dv0 + 4 * (tt - 1) + 4)
            tensor.matmul(PS4[:, 0:256], ONESM[0:1, 0:128], RB2[0:1, 0:256],
                          start=True, stop=True)
            mm = tensor.matmul(PS5[:, 0:256], ONESM[0:1, 0:128],
                               RB2[0:1, 1024:1280], start=True, stop=True)
            mm.then_inc(b.s("pe"), 1)
    return b.cnts["dv"]


def _attnb_quad(b, em, g, prev_done, prev_wo_pe, zb0_dv, scale, wqkv, wout,
                VT, EB, RP, WS_, state):
    X, RB1, RB2 = b.t["X"], b.t["RB1"], b.t["RB2"]
    XQH, XQL = b.t["XQH"], b.t["XQL"]
    RBH1, RBH2, ZBH = b.t["RBH1"], b.t["RBH2"], b.t["ZBH"]
    WSF, WSH, SM = b.t["WSF"], b.t["WSH"], b.t["SMALL"]
    ONESM, ID = b.t["ONESM"], b.t["IDent"]
    SEL32, PICK32 = b.t["SEL32"], b.t["PICK32"]
    PS = [b.t[f"PS{i}"] for i in range(8)]
    AVB = [0, 1, 6, 7]

    split_dv = {}
    wcol_consumed = {}
    specs = {}
    wo_ld = {}
    for j in range(5):
        slot = j % 2
        sf, sh = "wsf%d" % slot, "wsh%d" % slot
        if j < 3:
            srcs = [(wqkv[:, :, j * 1024 + 128 * g:j * 1024 + 128 * g + 128],
                     (0, 128))]
        else:
            e0 = 128 * g + (j - 3) * 64
            srcs = [(wout[e0:e0 + 32, :], (0, 32)),
                    (wout[e0 + 32:e0 + 64, :], (64, 96))]
        dq_tgt = b.inc(sf, 16 * len(srcs))
        if j < 3:
            gq_tgt = b.inc(sh, 16)
        else:
            gq_tgt = None
            wo_ld[j] = dq_tgt
        specs[j] = (slot, srcs, dq_tgt, gq_tgt, sf, sh)

        @em.on("sync")
        def _(sync, j=j, slot=slot, srcs=srcs, sf=sf):
            if j >= 2:
                sync.wait_ge(b.s("pe"), wcol_consumed[j - 2])
            elif prev_wo_pe is not None:
                sync.wait_ge(b.s("pe"), prev_wo_pe)
            for src, lanes in srcs:
                if j < 3:
                    dst = WSF[:, slot * 1024:slot * 1024 + 1024].rearrange(
                        "p (c d) -> p c d", c=8)
                else:
                    dst = WSF[lanes[0]:lanes[1], slot * 1024:slot * 1024 + 1024]
                sync.dma_start(out=dst, in_=src).then_inc(b.s(sf), 16)

        if j < 3:
            @em.on("gpsimd")
            def _(gp, j=j, slot=slot, srcs=srcs, sh=sh):
                if j >= 2:
                    gp.wait_ge(b.s("pe"), wcol_consumed[j - 2])
                elif prev_wo_pe is not None:
                    gp.wait_ge(b.s("pe"), prev_wo_pe)
                for src, lanes in srcs:
                    dst = WSH[:, slot * 2048:slot * 2048 + 1024].rearrange(
                        "p (c d) -> p c d", c=8)
                    gp.dma_start(out=dst, in_=src).then_inc(b.s(sh), 16)

    def append_split(j):
        slot, srcs, dq_tgt, gq_tgt, sf, sh = specs[j]

        @em.on("vector")
        def _(v, j=j, slot=slot, dq_tgt=dq_tgt, gq_tgt=gq_tgt, sf=sf, sh=sh):
            v.wait_ge(b.s(sf), dq_tgt)
            v.wait_ge(b.s(sh), gq_tgt)
            v.tensor_tensor(
                WSH[:, slot * 2048 + 1024:slot * 2048 + 2048],
                WSF[:, slot * 1024:slot * 1024 + 1024],
                WSH[:, slot * 2048:slot * 2048 + 1024],
                ALU.subtract).then_inc(b.s("dv"), 1)
        split_dv[j] = b.inc("dv", 1)

    append_split(0)
    append_split(1)

    evac_dv = {}

    def emit_proj(j, dst, vf32):
        @em.on("tensor")
        def _(tensor, j=j):
            tensor.wait_ge(b.s("dv"), split_dv[j])
            if j == 0:
                tensor.wait_ge(b.s("dv"), prev_done)
            else:
                tensor.wait_ge(b.s("dv"), evac_dv[j - 1])
            slot = j % 2
            for c in range(NCH):
                wh = WSH[:, slot * 2048 + c * 128:slot * 2048 + c * 128 + 128]
                wl = WSH[:, slot * 2048 + 1024 + c * 128:
                         slot * 2048 + 1024 + c * 128 + 128]
                for q in range(4):
                    xsl = slice(c * T + q * 512, c * T + q * 512 + 512)
                    tensor.matmul(PS[q][:, 0:512], wh, XQH[:, xsl],
                                  start=(c == 0), stop=False)
                    tensor.matmul(PS[q][:, 0:512], wh, XQL[:, xsl],
                                  start=False, stop=False)
                    mm = tensor.matmul(PS[q][:, 0:512], wl, XQH[:, xsl],
                                       start=False, stop=(c == NCH - 1))
            mm.then_inc(b.s("pe"), 1)
        pe_done = b.inc("pe", 1)
        wcol_consumed[j] = pe_done

        @em.on("vector")
        def _(v, j=j, pe_done=pe_done, dst=dst, vf32=vf32):
            v.wait_ge(b.s("pe"), pe_done)
            if vf32:
                for q in range(4):
                    ins = v.tensor_copy(
                        RB1[:, EB + q * 512:EB + q * 512 + 512], PS[q][:, 0:512])
            else:
                for q in range(4):
                    v.tensor_copy(dst[:, q * 512:q * 512 + 512], PS[q][:, 0:512])
                v.drain()
                for q in range(4):
                    ins = v.tensor_tensor(
                        dst[:, 2048 + q * 512:2048 + q * 512 + 512],
                        PS[q][:, 0:512], dst[:, q * 512:q * 512 + 512],
                        ALU.subtract)
            ins.then_inc(b.s("dv"), 1)
        return b.inc("dv", 1)

    evac_dv[0] = emit_proj(0, RBH2, False)
    append_split(2)
    evac_dv[1] = emit_proj(1, RBH1, False)
    evac_dv[2] = emit_proj(2, None, True)

    @em.on("vector")
    def _(v):
        v.wait_ge(b.s("dv"), prev_done)
        v.memset(RB1[:, VT:VT + 2112], 1.0).then_inc(b.s("dv"), 1)
    vt_ones = b.inc("dv", 1)

    tr_pe = [b.inc("pe") for _ in range(4)]
    vtr_dv = [b.inc("dv") for _ in range(4)]
    for rnd in range(4):
        @em.on("tensor")
        def _(tensor, rnd=rnd):
            if rnd == 0:
                tensor.wait_ge(b.s("dv"), evac_dv[2])
            if rnd >= 2:
                tensor.wait_ge(b.s("dv"), vtr_dv[rnd - 2])
            bank = PS[4 + rnd % 2]
            for kk in range(4):
                kb = rnd * 4 + kk
                mm = tensor.transpose(
                    bank[:, kk * 128:kk * 128 + 128],
                    RB1[:, EB + kb * 128:EB + kb * 128 + 128], ID[:, :])
            mm.then_inc(b.s("pe"), 1)

        @em.on("vector")
        def _(v, rnd=rnd):
            v.wait_ge(b.s("pe"), tr_pe[rnd])
            if rnd == 0:
                v.wait_ge(b.s("dv"), vt_ones)
            bank = PS[4 + rnd % 2]
            for kk in range(4):
                kb = rnd * 4 + kk
                for h in range(4):
                    off = VT + kb * 132 + h * 33
                    ins = v.tensor_copy(
                        RB1[:, off:off + 32],
                        bank[:, kk * 128 + h * 32:kk * 128 + h * 32 + 32])
            ins.then_inc(b.s("dv"), 1)

    sq_sc, nrm_pe, red_dv = {}, {}, {}
    for which in (0, 1):
        for qt in range(4):
            sq_sc[which, qt] = b.inc("sc")
    for which in (0, 1):
        for qt in range(4):
            nrm_pe[which, qt] = b.inc("pe")
    for which in (0, 1):
        for qt in range(4):
            red_dv[which, qt] = b.inc("dv")

    for which, SRC in ((0, RBH2), (1, RBH1)):
        @em.on("scalar")
        def _(scalar, which=which, SRC=SRC):
            scalar.wait_ge(b.s("dv"), evac_dv[which])
            for qt in range(4):
                if (which, qt) > (0, 1):
                    prev = (which, qt - 2) if qt >= 2 else (which - 1, qt + 2)
                    scalar.wait_ge(b.s("pe"), nrm_pe[prev])
                scalar.activation(
                    ZBH[:, (qt % 2) * 512:(qt % 2) * 512 + 512],
                    SRC[:, qt * 512:qt * 512 + 512], ACT.Square,
                ).then_inc(b.s("sc"), 1)

        @em.on("tensor")
        def _(tensor, which=which):
            for qt in range(4):
                tensor.wait_ge(b.s("sc"), sq_sc[which, qt])
                if (which, qt) > (0, 0):
                    prev = (which, qt - 1) if qt >= 1 else (which - 1, 3)
                    tensor.wait_ge(b.s("dv"), red_dv[prev])
                tensor.matmul(PS[6][0:4, 0:512], SEL32[:, 0:4],
                              ZBH[:, (qt % 2) * 512:(qt % 2) * 512 + 512],
                              start=True, stop=True).then_inc(b.s("pe"), 1)

        @em.on("vector")
        def _(v, which=which):
            for qt in range(4):
                v.wait_ge(b.s("pe"), nrm_pe[which, qt])
                v.reduce_max(SM[0:4, 12:13], PS[6][0:4, 0:512], axis=AX)
                v.drain()
                if qt == 0:
                    ins = v.tensor_copy(SM[0:4, 8 + which:9 + which],
                                        SM[0:4, 12:13])
                else:
                    ins = v.tensor_tensor(SM[0:4, 8 + which:9 + which],
                                          SM[0:4, 8 + which:9 + which],
                                          SM[0:4, 12:13], ALU.max)
                ins.then_inc(b.s("dv"), 1)

    @em.on("vector")
    def _(v):
        v.drain()
        v.tensor_tensor(SM[0:4, 10:11], SM[0:4, 8:9], SM[0:4, 9:10],
                        ALU.mult).then_inc(b.s("dv"), 1)
    cc_dv = b.inc("dv", 1)

    @em.on("scalar")
    def _(scalar):
        scalar.wait_ge(b.s("dv"), cc_dv)
        scalar.activation(SM[0:4, 11:12], SM[0:4, 10:11], ACT.Sqrt,
                          bias=0.0, scale=1.0404 * scale * scale,
                          ).then_inc(b.s("sc"), 1)
    csqrt_sc = b.inc("sc", 1)

    @em.on("vector")
    def _(v):
        v.wait_ge(b.s("sc"), csqrt_sc)
        v.tensor_single_scalar(SM[0:4, 11:12], SM[0:4, 11:12], -1.0, ALU.mult)
        v.drain()
        v.tensor_copy(b.t["NEGCH"][0:4, 0:1],
                      SM[0:4, 11:12]).then_inc(b.s("dv"), 1)
    negc_dv = b.inc("dv", 1)

    bias_pe = {}
    bias_dv = {}
    for hh in range(2):
        bias_pe[hh] = b.inc("pe")
        bias_dv[hh] = b.inc("dv")
    for hh in range(2):
        @em.on("tensor")
        def _(tensor, hh=hh):
            if hh == 0:
                tensor.wait_ge(b.s("dv"), negc_dv)
            else:
                tensor.wait_ge(b.s("dv"), bias_dv[0])
            tensor.matmul(PS[6][:, 0:1], PICK32[:, 256 * hh:256 * hh + 128],
                          b.t["NEGCH"][0:4, 0:1], start=True, stop=True)
            mm = tensor.matmul(PS[7][:, 0:1],
                               PICK32[:, 256 * hh + 128:256 * hh + 256],
                               b.t["NEGCH"][0:4, 0:1], start=True, stop=True)
            mm.then_inc(b.s("pe"), 1)

        @em.on("vector")
        def _(v, hh=hh):
            v.wait_ge(b.s("pe"), bias_pe[hh])
            v.tensor_copy(SM[:, 2 * hh:2 * hh + 1], PS[6][:, 0:1])
            v.tensor_copy(SM[:, 2 * hh + 1:2 * hh + 2],
                          PS[7][:, 0:1]).then_inc(b.s("dv"), 1)

    sc_exp, pe_scores, pe_av, dv_z = {}, {}, {}, {}
    last_wo_pe = None
    stg_last = {}
    for qt in range(4):
        for kb in range(16):
            pe_scores[qt, kb] = b.inc("pe")
            if kb >= 1:
                pe_av[qt, kb - 1] = b.inc("pe")
        pe_av[qt, 15] = b.inc("pe")
        for kb in range(16):
            sc_exp[qt, kb] = b.inc("sc")

        def _avmm_b(tensor, qt, kb):
            tensor.wait_ge(b.s("sc"), sc_exp[qt, kb])
            if kb == 0:
                if qt == 0:
                    tensor.wait_ge(b.s("dv"), vtr_dv[3])
                else:
                    tensor.wait_ge(b.s("dv"), dv_z[qt - 1])
            for h in range(4):
                vt_ap = RB1[:, VT + kb * 132 + h * 33:VT + kb * 132 + h * 33 + 33]
                e_ap = RB1[:, EB + h * 512:EB + h * 512 + 512]
                r0 = 64 * (h % 2)
                mm = tensor.matmul(PS[AVB[h]][r0:r0 + 33, 0:512],
                                   vt_ap, e_ap,
                                   start=(kb == 0), stop=(kb == 15),
                                   tile_position=(0, r0))
            mm.then_inc(b.s("pe"), 1)

        @em.on("tensor")
        def _(tensor, qt=qt, _avmm_b=_avmm_b):
            for kb in range(16):
                if qt == 0 and kb == 0:
                    tensor.wait_ge(b.s("dv"), evac_dv[0])
                    tensor.wait_ge(b.s("dv"), evac_dv[1])
                if kb >= 1:
                    tensor.wait_ge(b.s("sc"), sc_exp[qt, kb - 1])
                elif qt >= 1:
                    tensor.wait_ge(b.s("sc"), sc_exp[qt - 1, 15])
                    tensor.wait_ge(b.s("dv"), stg_last[qt - 1])
                for h in range(4):
                    bank = PS[2 + h]
                    r0 = 32 * h
                    kh = RBH1[r0:r0 + 32, kb * 128:kb * 128 + 128]
                    kl = RBH1[r0:r0 + 32, 2048 + kb * 128:2048 + kb * 128 + 128]
                    qh = RBH2[r0:r0 + 32, qt * 512:qt * 512 + 512]
                    ql = RBH2[r0:r0 + 32, 2048 + qt * 512:2048 + qt * 512 + 512]
                    tp = (r0, 0)
                    tensor.matmul(bank[:, 0:512], kh, qh, start=True, stop=False,
                                  tile_position=tp)
                    tensor.matmul(bank[:, 0:512], kh, ql, start=False,
                                  stop=False, tile_position=tp)
                    mm = tensor.matmul(bank[:, 0:512], kl, qh, start=False,
                                       stop=True, tile_position=tp)
                mm.then_inc(b.s("pe"), 1)
                if kb >= 1:
                    _avmm_b(tensor, qt, kb - 1)
            _avmm_b(tensor, qt, 15)

        @em.on("scalar")
        def _(scalar, qt=qt):
            for kb in range(16):
                scalar.wait_ge(b.s("pe"), pe_scores[qt, kb])
                if qt == 0 and kb == 0:
                    scalar.wait_ge(b.s("dv"), bias_dv[1])
                if kb >= 1:
                    scalar.wait_ge(b.s("pe"), pe_av[qt, kb - 1])
                for h in range(4):
                    ins = scalar.activation(
                        RB1[:, EB + h * 512:EB + h * 512 + 512],
                        PS[2 + h][:, 0:512], ACT.Exp, bias=SM[:, h:h + 1],
                        scale=scale)
                ins.then_inc(b.s("sc"), 1)

        rc_dv = b.inc("dv")
        mv_tgt = b.inc("sx0", 16)
        rp_pe = [b.inc("pe"), b.inc("pe")]
        rp_dv = [b.inc("dv"), b.inc("dv")]
        z_dv = b.inc("dv")
        dv_z[qt] = z_dv
        wo_pe = [b.inc("pe") for _ in range(8)]
        st_dv = [b.inc("dv") for _ in range(8)]
        stg_last[qt] = st_dv[7]
        last_wo_pe = wo_pe[7]

        @em.on("vector")
        def _(v, qt=qt):
            v.wait_ge(b.s("pe"), pe_av[qt, 15])
            v.reciprocal(RB2[32:33, 1024:1536], PS[AVB[0]][32:33, 0:512])
            v.reciprocal(RB2[96:97, 1024:1536], PS[AVB[1]][96:97, 0:512])
            v.reciprocal(RB2[32:33, 1536:2048], PS[AVB[2]][32:33, 0:512])
            v.reciprocal(RB2[96:97, 1536:2048],
                         PS[AVB[3]][96:97, 0:512]).then_inc(b.s("dv"), 1)

        @em.on("sync")
        def _(sync, qt=qt, mv_tgt=mv_tgt, rc_dv=rc_dv):
            sync.wait_ge(b.s("dv"), rc_dv)
            sync.dma_start(out=RB2[64:65, 1024:2048],
                           in_=RB2[96:97, 1024:2048]).then_inc(b.s("sx0"), 16)

        for hh in range(2):
            @em.on("tensor")
            def _(tensor, qt=qt, hh=hh, mv_tgt=mv_tgt, rc_dv=rc_dv,
                  rp_dv=rp_dv):
                if hh == 0:
                    tensor.wait_ge(b.s("dv"), rc_dv)
                    tensor.wait_ge(b.s("sx0"), mv_tgt)
                    tensor.wait_ge(b.s("sc"), sc_exp[qt, 15])
                else:
                    tensor.wait_ge(b.s("dv"), rp_dv[0])
                tensor.matmul(PS[2][:, 0:512], ONESM[32:33, 0:128],
                              RB2[32:33, 1024 + 512 * hh:1536 + 512 * hh],
                              start=True, stop=True)
                mm = tensor.matmul(PS[3][:, 0:512], ONESM[64:65, 0:128],
                                   RB2[64:65, 1024 + 512 * hh:1536 + 512 * hh],
                                   start=True, stop=True)
                mm.then_inc(b.s("pe"), 1)

            @em.on("vector")
            def _(v, qt=qt, hh=hh, rp_pe=rp_pe):
                v.wait_ge(b.s("pe"), rp_pe[hh])
                v.tensor_copy(RB1[:, RP + 1024 * hh:RP + 1024 * hh + 512],
                              PS[2][:, 0:512])
                ins = v.tensor_copy(
                    RB1[:, RP + 1024 * hh + 512:RP + 1024 * hh + 1024],
                    PS[3][:, 0:512])
                ins.then_inc(b.s("dv"), 1)

        @em.on("vector")
        def _(v, qt=qt, rp_dv=rp_dv):
            v.wait_ge(b.s("dv"), rp_dv[1])
            for h in range(4):
                hh = h // 2
                r0 = 64 * (h % 2)
                rep = RB1[r0:r0 + 32,
                          RP + 1024 * hh + 512 * (h % 2):
                          RP + 1024 * hh + 512 * (h % 2) + 512]
                ins = v.tensor_tensor(
                    RB2[r0:r0 + 32, 512 * hh:512 * hh + 512],
                    PS[AVB[h]][r0:r0 + 32, 0:512], rep, ALU.mult)
            ins.then_inc(b.s("dv"), 1)

        @em.on("tensor")
        def _(tensor, qt=qt, z_dv=z_dv, wo_pe=wo_pe, st_dv=st_dv):
            tensor.wait_ge(b.s("dv"), z_dv)
            if qt == 0:
                tensor.wait_ge(b.s("wsf1"), wo_ld[3])
                tensor.wait_ge(b.s("wsf0"), wo_ld[4])
                if g == 0:
                    tensor.wait_ge(b.s("dv"), zb0_dv)
            for do in range(8):
                bank = PS[4 + do % 2]
                if do >= 2:
                    tensor.wait_ge(b.s("dv"), st_dv[do - 2])
                for hh in range(2):
                    slot = 1 - hh
                    wo_t = WSF[:, slot * 1024 + do * 128:
                               slot * 1024 + do * 128 + 128]
                    z_t = RB2[:, 512 * hh:512 * hh + 512]
                    mm = tensor.matmul(bank[:, 0:512], wo_t, z_t,
                                       start=(hh == 0), stop=(hh == 1))
                mm.then_inc(b.s("pe"), 1)

        @em.on("vector")
        def _(v, qt=qt, wo_pe=wo_pe, g=g):
            for do in range(8):
                bank = PS[4 + do % 2]
                v.wait_ge(b.s("pe"), wo_pe[do])
                slot = do % 2
                v.wait_ge(b.s("ac%d" % slot),
                          state.get(("acst", g, qt, do), 0))
                ins = v.tensor_copy(
                    RB1[:, WS_ + slot * 512:WS_ + slot * 512 + 512],
                    bank[:, 0:512])
                ins.then_inc(b.s("dv"), 1)

        @em.on("gpsimd")
        def _(gp, qt=qt, g=g, st_dv=st_dv):
            scro = b.t["scr_out"]
            for do in range(8):
                slot = do % 2
                gp.wait_ge(b.s("dv"), st_dv[do])
                gp.dma_start(
                    out=scro[do * 128:do * 128 + 128, qt * 512:qt * 512 + 512],
                    in_=RB1[:, WS_ + slot * 512:WS_ + slot * 512 + 512],
                    accum_op=(ALU.add if g > 0 else ALU.bypass),
                ).then_inc(b.s("ac%d" % slot), 16)
        for do in range(8):
            slot = do % 2
            state[("acst_next", g, qt, do)] = b.inc("ac%d" % slot, 16)

    wcol_consumed[3] = last_wo_pe
    wcol_consumed[4] = last_wo_pe
    for qt in range(4):
        for do in range(8):
            slot = do % 2
            if do >= 2:
                pq, pd = qt, do - 2
            elif qt >= 1:
                pq, pd = qt - 1, 6 + do
            else:
                pq = pd = None
            if pq is not None:
                state[("acst", g, qt, do)] = state[("acst_next", g, pq, pd)]
            else:
                state[("acst", g, qt, do)] = state.get(
                    ("acst_next", g - 1, 3, 6 + do), 0)
    return b.cnts["dv"], last_wo_pe


def sub_l1_attn_a(b, block, ln_dv, state):
    SM = b.t["SMALL"]
    if not state.get("l1a_prep_done"):
        blocks = []
        for dc in range(24):
            blocks.append((b.t["L1_attn_a_qkv_w"][dc * 128:dc * 128 + 128, :],
                           dc * 128))
        for dc in range(8):
            blocks.append((b.t["L1_attn_a_fc_w"][dc * 128:dc * 128 + 128, :],
                           3072 + dc * 128))
        state["l1a_prep_done"] = prep_T(
            b, block, state, blocks,
            b.t["scr_l1a"].rearrange("(h c d) -> h c d", h=2, c=1024))
    prep_done = state["l1a_prep_done"]
    scr = b.t["scr_l1a"].rearrange("(h c d) -> h c d", h=2, c=1024)

    fcb_tgt = b.inc("dq", 16)

    @block.sync
    def _(sync):
        sync.wait_ge(b.s("wsh0"), prep_done[0])
        sync.wait_ge(b.s("wsh1"), prep_done[1])
        sync.dma_start(out=SM[:, 96:104],
                       in_=b.t["L1_attn_a_fc_b"].rearrange("(c p) -> p c", p=128),
                       allow_slow_non_contiguous=True).then_inc(b.s("dq"), 16)

    def col_ap(e_off, width):
        return (scr[0, :, e_off:e_off + width].rearrange("(c q) d -> q c d", q=128),
                scr[1, :, e_off:e_off + width].rearrange("(c q) d -> q c d", q=128))

    def wo_ap2(hp, e):
        hd0 = (2 * hp + e) * 64
        return (scr[0, hd0:hd0 + 64, 3072:4096],
                scr[1, hd0:hd0 + 64, 3072:4096])

    def pair_setup(em, hp):
        qb_tgt = b.inc("dq", 16 * 7)
        hm_pe = b.inc("pe")
        hm_dv = b.inc("dv")

        pd = state.get("prev_pair_done")

        @em.on("sync")
        def _(sync, hp=hp, pd=pd):
            if hp == 0:
                sync.wait_ge(b.s("dq"), fcb_tgt)
            if pd is not None and hp > 0:
                sync.wait_ge(b.s("dv"), pd)
            for j in range(3):
                for e in (0, 1):
                    off = (2 * hp + e) * 192 + j * 64
                    sync.dma_start(
                        out=SM[e * 64:e * 64 + 64, 89 + j:90 + j],
                        in_=b.t["L1_attn_a_qkv_b"][off:off + 64].rearrange(
                            "(k o) -> k o", o=1),
                    ).then_inc(b.s("dq"), 16)
            sync.dma_start(
                out=SM[0:2, 92:93],
                in_=b.t["L1_attn_a_head_mask"][2 * hp:2 * hp + 2].rearrange(
                    "(k o) -> k o", o=1),
            ).then_inc(b.s("dq"), 16)

        @em.on("tensor")
        def _(tensor, hp=hp, prev_hm=state.get("l1a_prev_hm")):
            tensor.wait_ge(b.s("dq"), qb_tgt)
            if prev_hm is not None:
                tensor.wait_ge(b.s("dv"), prev_hm)
            tensor.matmul(b.t["PS6"][:, 0:1], b.t["SELH2"][:, 0:128],
                          SM[0:2, 92:93], start=True,
                          stop=True).then_inc(b.s("pe"), 1)

        @em.on("vector")
        def _(v, hp=hp):
            v.wait_ge(b.s("pe"), hm_pe)
            v.tensor_copy(SM[:, 93:94], b.t["PS6"][:, 0:1]).then_inc(b.s("dv"), 1)
        state["l1a_prev_hm"] = hm_dv
        return [("dq", qb_tgt), ("dv", hm_dv)]

    RB2 = b.t["RB2"]

    def evac_mod(v, hp, j, q, src):
        dstm = RB2[:, q * 512:q * 512 + 512]
        v.tensor_scalar(dstm, src, SM[:, 89 + j:90 + j], SM[:, 93:94],
                        ALU.add, ALU.mult)
        return dstm

    cfg = {
        "wq_ap": lambda hp, e: col_ap((2 * hp + e) * 192 + 0, 64),
        "wk_ap": lambda hp, e: col_ap((2 * hp + e) * 192 + 64, 64),
        "wv_ap": lambda hp, e: col_ap((2 * hp + e) * 192 + 128, 64),
        "wo_ap": wo_ap2,
        "scale": 0.125,
        "internal_residual": False,
        "ln_dv": ln_dv,
        "presplit": True,
        "ct_rows": True,
        "pair_setup": pair_setup,
        "evac_mod": evac_mod,
        "xacc_bias": lambda do: SM[:, 96 + do:97 + do],
    }
    return attn64(b, block, cfg, state)


def _round5(v, w, scr, DORA):
    v.tensor_scalar(scr, w, C23, C23, ALU.add, ALU.subtract)
    v.tensor_scalar(w, w, C52, C52, ALU.add, ALU.subtract)
    v.drain()
    v.tensor_tensor(w, w, scr, ALU.subtract)
    v.drain()
    v.scalar_tensor_tensor(w, w, DORA[:, 0:1], scr, ALU.mult, ALU.add)
    v.drain()


def prep_bitnet(b, block, state, ln_dv):
    WSF, WSH, ID, SM = b.t["WSF"], b.t["WSH"], b.t["IDent"], b.t["SMALL"]
    ONESM, DORA, RB2, ZBH = (b.t["ONESM"], b.t["DORA"], b.t["RB2"],
                             b.t["ZBH"])
    PS = [b.t[f"PS{i}"] for i in range(8)]
    gT = b.t["scr_gT"].rearrange("(r c) -> r c", c=4096)
    uT = b.t["scr_uT"].rearrange("(r c) -> r c", c=4096)
    dT = b.t["scr_dT"].rearrange("(r c) -> r c", c=1024)
    em = E(b)
    blocks = []
    for dc in range(32):
        blocks.append((b.t["L1_ffn_w_gate"][dc * 128:dc * 128 + 128, :],
                       gT[:, 128 * dc:128 * dc + 128], 0))
    for dc in range(32):
        blocks.append((b.t["L1_ffn_w_up"][dc * 128:dc * 128 + 128, :],
                       uT[:, 128 * dc:128 * dc + 128], 1))
    for ob in range(8):
        for isub in range(4):
            blocks.append((
                b.t["L1_ffn_w_down"][ob * 128:ob * 128 + 128,
                                     isub * 1024:isub * 1024 + 1024],
                dT[1024 * isub:1024 * isub + 1024, 128 * ob:128 * ob + 128],
                2))
    n = len(blocks)
    MATN = {0: 4096.0 * 1024, 1: 4096.0 * 1024, 2: 1024.0 * 4096}
    prev_done = state.get("prev_pair_done")
    pw = state.get("prev_wo_pe")

    ld1, abs_dv, sum_pe = {}, {}, {}
    for i in range(n):
        ld1[i] = b.inc("wsf%d" % (i % 2), 16)
    for i in range(n):
        abs_dv[i] = b.inc("dv")
    for i in range(n):
        sum_pe[i] = b.inc("pe")
    mstat_sc, mstat_dv, mrep_pe, mrep_dv = {}, {}, {}, {}
    for m in range(3):
        mstat_sc[m] = b.inc("sc")
    for m in range(3):
        mstat_dv[m] = b.inc("dv")
    for m in range(3):
        mrep_pe[m] = b.inc("pe")
    for m in range(3):
        mrep_dv[m] = b.inc("dv")

    @em.on("sync")
    def _(sync):
        for i in range(n):
            slot = i % 2
            if i >= 2:
                sync.wait_ge(b.s("dv"), abs_dv[i - 2])
            else:
                if pw is not None:
                    sync.wait_ge(b.s("pe"), pw)
                if prev_done is not None:
                    sync.wait_ge(b.s("dv"), prev_done)
            sync.dma_start(out=WSF[:, slot * 1024:slot * 1024 + 1024],
                           in_=blocks[i][0]).then_inc(b.s("wsf%d" % slot), 16)

    @em.on("vector")
    def _(v):
        v.wait_ge(b.s("dv"), ln_dv)
        for i in range(n):
            slot = i % 2
            v.wait_ge(b.s("wsf%d" % slot), ld1[i])
            if i >= 1:
                v.wait_ge(b.s("pe"), sum_pe[i - 1])
            v.scalar_tensor_tensor(
                RB2[:, 0:1024], WSF[:, slot * 1024:slot * 1024 + 1024], -1.0,
                WSF[:, slot * 1024:slot * 1024 + 1024],
                ALU.mult, ALU.max).then_inc(b.s("dv"), 1)
        for m in range(3):
            v.wait_ge(b.s("sc"), mstat_sc[m])
            v.tensor_scalar(SM[0:1, 104 + m:105 + m],
                            SM[0:1, 104 + m:105 + m],
                            1.0 / MATN[m], 1e-5, ALU.mult, ALU.max)
            v.drain()
            v.reciprocal(SM[0:1, 108 + m:109 + m],
                         SM[0:1, 104 + m:105 + m]).then_inc(b.s("dv"), 1)
        for m in range(3):
            v.wait_ge(b.s("pe"), mrep_pe[m])
            v.tensor_copy(SM[:, 108 + m:109 + m], PS[7][:, 0:1])
            v.tensor_copy(SM[:, 111 + m:112 + m],
                          PS[7][:, 1:2]).then_inc(b.s("dv"), 1)

    @em.on("tensor")
    def _(tensor):
        for i in range(n):
            m = blocks[i][2]
            first = (i == 0 or blocks[i - 1][2] != m)
            last = (i == n - 1 or blocks[i + 1][2] != m)
            tensor.wait_ge(b.s("dv"), abs_dv[i])
            if first and m > 0:
                tensor.wait_ge(b.s("sc"), mstat_sc[m - 1])
            for cg in range(2):
                mm = tensor.matmul(
                    PS[6][0:1, 0:512], ONESM[:, 0:1],
                    RB2[:, cg * 512:cg * 512 + 512],
                    start=(first and cg == 0), stop=(last and cg == 1))
            mm.then_inc(b.s("pe"), 1)
        for m in range(3):
            tensor.wait_ge(b.s("dv"), mstat_dv[m])
            if m >= 1:
                tensor.wait_ge(b.s("dv"), mrep_dv[m - 1])
            tensor.matmul(PS[7][:, 0:1], ONESM[0:1, 0:128],
                          SM[0:1, 108 + m:109 + m], start=True, stop=True)
            tensor.matmul(PS[7][:, 1:2], ONESM[0:1, 0:128],
                          SM[0:1, 104 + m:105 + m],
                          start=True, stop=True).then_inc(b.s("pe"), 1)

    @em.on("scalar")
    def _(scalar):
        for m in range(3):
            scalar.wait_ge(b.s("pe"), sum_pe[{0: 31, 1: 63, 2: 95}[m]])
            scalar.activation(ZBH[0:1, 0:512], PS[6][0:1, 0:512],
                              ACT.Identity,
                              accum_out=SM[0:1, 104 + m:105 + m],
                              ).then_inc(b.s("sc"), 1)

    ld2, q_dv, tr_pe, cp_dv, st_tgt = {}, {}, {}, {}, {}
    for i in range(n):
        ld2[i] = b.inc("wsf%d" % (i % 2), 16)
    for i in range(n):
        q_dv[i] = b.inc("dv")
        cp_dv[i] = b.inc("dv")
        cp_dv[i, 1] = b.inc("dv")
    for i in range(n):
        tr_pe[i] = b.inc("pe")
        tr_pe[i, 1] = b.inc("pe")
    for i in range(n):
        st_tgt[i] = b.inc("wsh%d" % (i % 2), 16)

    @em.on("sync")
    def _(sync):
        for i in range(n):
            slot = i % 2
            if i >= 2:
                sync.wait_ge(b.s("pe"), tr_pe[i - 2, 1])
            else:
                sync.wait_ge(b.s("dv"), mrep_dv[2])
            sync.dma_start(out=WSF[:, slot * 1024:slot * 1024 + 1024],
                           in_=blocks[i][0]).then_inc(b.s("wsf%d" % slot), 16)


    @em.on("tensor")
    def _(tensor):
        for i in range(n):
            slot = i % 2
            bank = PS[slot]
            tensor.wait_ge(b.s("dv"), q_dv[i])
            if i >= 2:
                tensor.wait_ge(b.s("dv"), cp_dv[i - 2, 1])
            for half in range(2):
                if half == 1:
                    tensor.wait_ge(b.s("dv"), cp_dv[i])
                for cc in range(4 * half, 4 * half + 4):
                    mm = tensor.transpose(
                        bank[:, (cc % 4) * 128:(cc % 4) * 128 + 128],
                        WSF[:, slot * 1024 + cc * 128:
                            slot * 1024 + cc * 128 + 128],
                        ID[:, :])
                mm.then_inc(b.s("pe"), 1)

    @em.on("vector")
    def _(v):
        for i in range(n):
            slot = i % 2
            m = blocks[i][2]
            bank = PS[slot]
            w = WSF[:, slot * 1024:slot * 1024 + 1024]
            v.wait_ge(b.s("wsf%d" % slot), ld2[i])
            if i == 0:
                v.wait_ge(b.s("pe"), sum_pe[n - 1])
            if i >= 2:
                v.wait_ge(b.s("wsh%d" % slot), st_tgt[i - 2])
            v.tensor_scalar(w, w, SM[:, 108 + m:109 + m], 1.0,
                            ALU.mult, ALU.mult)
            v.drain()
            _round5(v, w, RB2[:, 0:1024], DORA)
            v.tensor_scalar(w, w, 1.0, -1.0, ALU.min,
                            ALU.max).then_inc(b.s("dv"), 1)
            for half in range(2):
                v.wait_ge(b.s("pe"), tr_pe[i] if half == 0 else tr_pe[i, 1])
                for cc in range(4 * half, 4 * half + 4):
                    ins = v.tensor_copy(
                        WSH[:, slot * 2048 + cc * 128:
                            slot * 2048 + cc * 128 + 128],
                        bank[:, (cc % 4) * 128:(cc % 4) * 128 + 128])
                ins.then_inc(b.s("dv"), 1)

    @em.on("gpsimd")
    def _(gp):
        for i in range(n):
            slot = i % 2
            gp.wait_ge(b.s("dv"), cp_dv[i, 1])
            src_s = WSH[:, slot * 2048:slot * 2048 + 1024].rearrange(
                "q (c d) -> q c d", c=8)
            dst = blocks[i][1].rearrange("(c q) d -> q c d", q=128)
            gp.dma_start(out=dst, in_=src_s).then_inc(b.s("wsh%d" % slot), 16)
    em.emit(block)

    fin_dv = b.inc("dv")

    @block.vector
    def _(v):
        v.wait_ge(b.s("dv"), mrep_dv[2])
        v.tensor_tensor(SM[:, 114:115], SM[:, 111:112], SM[:, 112:113],
                        ALU.mult).then_inc(b.s("dv"), 1)
    state["bit_prep"] = (st_tgt[n - 2], st_tgt[n - 1], fin_dv,
                         tr_pe[n - 1, 1])


def sub_l1_ffn(b, block, ln_dv, state):
    X, RB1, RB2, ZBH = b.t["X"], b.t["RB1"], b.t["RB2"], b.t["ZBH"]
    XQH, XQL = b.t["XQH"], b.t["XQL"]
    WSH, SM, ONESM, DORA = b.t["WSH"], b.t["SMALL"], b.t["ONESM"], b.t["DORA"]
    PICK32 = b.t["PICK32"]
    PS = [b.t[f"PS{i}"] for i in range(8)]
    gT = b.t["scr_gT"].rearrange("(r c) -> r c", c=4096)
    uT = b.t["scr_uT"].rearrange("(r c) -> r c", c=4096)
    dT = b.t["scr_dT"].rearrange("(r c) -> r c", c=1024)
    if not state.get("bit_prep"):
        prep_bitnet(b, block, state, ln_dv)
    prep0, prep1, fin_dv, prep_tr = state["bit_prep"]
    em = E(b)
    AX_C = mybir.AxisListType.C
    ABSC = 7168

    q1_add_dv, q1_abs_dv, q1_red_gp, q1_row_dv, q1_rep_pe, q1_u_dv = \
        {}, {}, {}, {}, {}, {}
    q1_mg_dv = {}
    for tc in range(4):
        for c in range(8):
            q1_add_dv[tc, c] = b.inc("dv")
            q1_abs_dv[tc, c] = b.inc("dv")
            if c >= 1:
                q1_mg_dv[tc, c] = b.inc("dv")
        q1_row_dv[tc] = b.inc("dv")
        for c in range(8):
            q1_u_dv[tc, c] = b.inc("dv")
    for tc in range(4):
        q1_red_gp[tc] = b.inc("gp")
    for tc in range(4):
        q1_rep_pe[tc] = b.inc("pe")

    @em.on("vector")
    def _(v):
        for tc in range(4):
            if tc == 0:
                v.wait_ge(b.s("dv"), ln_dv)
                v.wait_ge(b.s("pe"), prep_tr)
            for c in range(8):
                xsl = slice(c * T + tc * 512, c * T + tc * 512 + 512)
                v.tensor_tensor(RB1[:, c * 512:c * 512 + 512],
                                XQH[:, xsl], XQL[:, xsl],
                                ALU.add).then_inc(b.s("dv"), 1)
                v.drain()
                dst_a = (RB1[:, 6144:6656] if c == 0
                         else RB1[:, ABSC:ABSC + 512])
                v.scalar_tensor_tensor(
                    dst_a, RB1[:, c * 512:c * 512 + 512],
                    -1.0, RB1[:, c * 512:c * 512 + 512],
                    ALU.mult, ALU.max).then_inc(b.s("dv"), 1)
                if c >= 1:
                    v.drain()
                    v.tensor_tensor(RB1[:, 6144:6656],
                                    RB1[:, 6144:6656],
                                    RB1[:, ABSC:ABSC + 512],
                                    ALU.max).then_inc(b.s("dv"), 1)
                    v.drain()
            v.wait_ge(b.s("gp"), q1_red_gp[tc])
            v.tensor_single_scalar(RB1[0:1, 4096:4608], RB1[0:1, 4096:4608],
                                   1e-5, ALU.max)
            v.drain()
            v.tensor_single_scalar(RB2[0:1, tc * 512:tc * 512 + 512],
                                   RB1[0:1, 4096:4608], 1.0 / 127.0, ALU.mult)
            v.reciprocal(RB1[0:1, 4608:5120], RB1[0:1, 4096:4608])
            v.drain()
            v.tensor_single_scalar(RB1[0:1, 4608:5120], RB1[0:1, 4608:5120],
                                   127.0, ALU.mult).then_inc(b.s("dv"), 1)
            v.wait_ge(b.s("pe"), q1_rep_pe[tc])
            for c in range(8):
                xf = RB1[:, c * 512:c * 512 + 512]
                xsl = slice(c * T + tc * 512, c * T + tc * 512 + 512)
                v.tensor_tensor(xf, xf, PS[5][:, 0:512], ALU.mult)
                v.drain()
                _round5(v, xf, RB1[:, ABSC:ABSC + 512], DORA)
                v.tensor_scalar(xf, xf, 127.0, -128.0, ALU.min, ALU.max)
                v.drain()
                v.tensor_copy(XQH[:, xsl], xf).then_inc(b.s("dv"), 1)

    @em.on("gpsimd")
    def _(gp):
        for tc in range(4):
            gp.wait_ge(b.s("dv"), q1_mg_dv[tc, 7])
            gp.reduce_max(RB1[0:1, 4096:4608], RB1[:, 6144:6656],
                          axis=AX_C).then_inc(b.s("gp"), 1)

    @em.on("tensor")
    def _(tensor):
        for tc in range(4):
            tensor.wait_ge(b.s("dv"), q1_row_dv[tc])
            if tc >= 1:
                tensor.wait_ge(b.s("dv"), q1_u_dv[tc - 1, 7])
            tensor.matmul(PS[5][:, 0:512], ONESM[0:1, 0:128],
                          RB1[0:1, 4608:5120], start=True,
                          stop=True).then_inc(b.s("pe"), 1)

    rep1_pe, guld, gu_pe = {}, {}, {}
    gd_dv, gu_dv, q2_abs_dv, q2_red_gp, q2_row_dv, q2_rep_pe = \
        {}, {}, {}, {}, {}, {}
    u2_dv, dld, d_pe, d_dv, sg_sc = {}, {}, {}, {}, {}
    dq2_rep_pe, q2_mg_dv = {}, {}
    for tt in range(4):
        rep1_pe[tt] = b.inc("pe")
        for oc in range(32):
            guld[tt, oc] = b.inc("wsf%d" % (oc % 2), 32)
        for oc in range(32):
            gu_pe[tt, oc] = b.inc("pe")
        q2_rep_pe[tt] = b.inc("pe")
        dq2_rep_pe[tt] = b.inc("pe")
        for do in range(8):
            d_pe[tt, do] = b.inc("pe")
        for oc in range(32):
            gd_dv[tt, oc] = b.inc("dv")
            gu_dv[tt, oc] = b.inc("dv")
            q2_abs_dv[tt, oc] = b.inc("dv")
            if oc >= 1:
                q2_mg_dv[tt, oc] = b.inc("dv")
        q2_row_dv[tt] = b.inc("dv")
        for oc in range(32):
            u2_dv[tt, oc] = b.inc("dv")
        for do in range(8):
            d_dv[tt, do] = b.inc("dv")
        q2_red_gp[tt] = b.inc("gp")
        for oc in range(32):
            sg_sc[tt, oc] = b.inc("sc")
        for do in range(8):
            dld[tt, do, 0] = b.inc("wsf0", 16)
            dld[tt, do, 1] = b.inc("wsf1", 16)

    @em.on("sync")
    def _(sync):
        for tt in range(4):
            for oc in range(32):
                slot = oc % 2
                if tt == 0 and oc < 2:
                    sync.wait_ge(b.s("wsh0"), prep0)
                    sync.wait_ge(b.s("wsh1"), prep1)
                    if state.get("prev_wo_pe") is not None:
                        sync.wait_ge(b.s("pe"), state["prev_wo_pe"])
                elif oc < 2:
                    sync.wait_ge(b.s("pe"), d_pe[tt - 1, 7])
                else:
                    sync.wait_ge(b.s("pe"), gu_pe[tt, oc - 2])
                sync.dma_start(
                    out=WSH[:, slot * 2048:slot * 2048 + 1024].rearrange(
                        "p (c d) -> p c d", c=8),
                    in_=gT[:, oc * 128:oc * 128 + 128].rearrange(
                        "(c p) d -> p c d", p=128),
                ).then_inc(b.s("wsf%d" % slot), 16)
                sync.dma_start(
                    out=WSH[:, slot * 2048 + 1024:slot * 2048 + 2048
                            ].rearrange("p (c d) -> p c d", c=8),
                    in_=uT[:, oc * 128:oc * 128 + 128].rearrange(
                        "(c p) d -> p c d", p=128),
                ).then_inc(b.s("wsf%d" % slot), 16)
            for do in range(8):
                if do == 0:
                    sync.wait_ge(b.s("pe"), gu_pe[tt, 31])
                else:
                    sync.wait_ge(b.s("pe"), d_pe[tt, do - 1])
                for half in range(2):
                    sync.dma_start(
                        out=WSH[:, half * 2048:half * 2048 + 2048].rearrange(
                            "p (c d) -> p c d", c=16),
                        in_=dT[half * 2048:half * 2048 + 2048,
                               do * 128:do * 128 + 128].rearrange(
                            "(c p) d -> p c d", p=128),
                    ).then_inc(b.s("wsf%d" % half), 16)

    @em.on("tensor")
    def _(tensor):
        for tt in range(4):
            tensor.wait_ge(b.s("dv"), q1_row_dv[tt])
            if tt >= 1:
                tensor.wait_ge(b.s("dv"), d_dv[tt - 1, 7])
            tensor.matmul(PS[2][:, 0:512], ONESM[0:1, 0:128],
                          RB2[0:1, tt * 512:tt * 512 + 512], start=True,
                          stop=True).then_inc(b.s("pe"), 1)
            for oc in range(32):
                slot = oc % 2
                tensor.wait_ge(b.s("wsf%d" % slot), guld[tt, oc])
                if oc == 0:
                    tensor.wait_ge(b.s("dv"), q1_u_dv[tt, 7])
                    tensor.wait_ge(b.s("dv"), fin_dv)
                prev = (tt, oc - 1) if oc >= 1 else (tt - 1, 31)
                if prev[0] >= 0:
                    tensor.wait_ge(b.s("dv"), gu_dv[prev])
                for kc in range(8):
                    u_ap = XQH[:, kc * T + tt * 512:kc * T + tt * 512 + 512]
                    tensor.matmul(
                        PS[0][:, 0:512],
                        WSH[:, slot * 2048 + kc * 128:
                            slot * 2048 + kc * 128 + 128],
                        u_ap, start=(kc == 0), stop=(kc == 7))
                for kc in range(8):
                    u_ap = XQH[:, kc * T + tt * 512:kc * T + tt * 512 + 512]
                    mm = tensor.matmul(
                        PS[1][:, 0:512],
                        WSH[:, slot * 2048 + 1024 + kc * 128:
                            slot * 2048 + 1024 + kc * 128 + 128],
                        u_ap, start=(kc == 0), stop=(kc == 7))
                mm.then_inc(b.s("pe"), 1)
            tensor.wait_ge(b.s("dv"), q2_row_dv[tt])
            tensor.matmul(PS[2][:, 0:512], ONESM[0:1, 0:128],
                          RB1[0:1, 4608:5120], start=True,
                          stop=True).then_inc(b.s("pe"), 1)
            tensor.wait_ge(b.s("dv"), u2_dv[tt, 31])
            tensor.matmul(PS[2][:, 0:512], ONESM[0:1, 0:128],
                          RB1[0:1, 5120 + tt * 512:5632 + tt * 512],
                          start=True, stop=True).then_inc(b.s("pe"), 1)
            for do in range(8):
                bank = PS[6 + do % 2]
                tensor.wait_ge(b.s("wsf0"), dld[tt, do, 0])
                tensor.wait_ge(b.s("wsf1"), dld[tt, do, 1])
                if do == 0:
                    tensor.wait_ge(b.s("dv"), u2_dv[tt, 31])
                if do >= 2:
                    tensor.wait_ge(b.s("dv"), d_dv[tt, do - 2])
                for kc in range(32):
                    mm = tensor.matmul(
                        bank[:, 0:512],
                        WSH[:, kc * 128:kc * 128 + 128],
                        XQL[:, kc * 512:kc * 512 + 512],
                        start=(kc == 0), stop=(kc == 31))
                mm.then_inc(b.s("pe"), 1)

    @em.on("scalar")
    def _(scalar):
        for tt in range(4):
            for oc in range(32):
                scalar.wait_ge(b.s("dv"), gd_dv[tt, oc])
                scalar.activation(RB1[:, 1024:1536], RB1[:, 2048:2560],
                                  ACT.Sigmoid).then_inc(b.s("sc"), 1)

    @em.on("gpsimd")
    def _(gp):
        for tt in range(4):
            gp.wait_ge(b.s("dv"), q2_mg_dv[tt, 31])
            gp.reduce_max(ZBH[0:1, 0:512], RB1[:, 3072:3584],
                          axis=AX_C).then_inc(b.s("gp"), 1)

    @em.on("vector")
    def _(v):
        for tt in range(4):
            for oc in range(32):
                v.wait_ge(b.s("pe"), gu_pe[tt, oc])
                if oc == 0:
                    v.wait_ge(b.s("pe"), rep1_pe[tt])
                    v.drain()
                    v.tensor_copy(RB1[:, 1536:2048], PS[2][:, 0:512])
                    v.drain()
                gucol = XQL[:, oc * 512:oc * 512 + 512]
                rep_sb = RB1[:, 1536:2048]
                ga, ua = RB1[:, 0:512], RB1[:, 512:1024]
                gd, pp = RB1[:, 2048:2560], RB1[:, 2560:3072]
                v.tensor_copy(ga, PS[0][:, 0:512])
                v.tensor_copy(ua, PS[1][:, 0:512])
                v.drain()
                v.tensor_tensor(gd, ga, rep_sb, ALU.mult)
                v.drain()
                v.tensor_scalar(gd, gd, SM[:, 111:112], 1.0,
                                ALU.mult, ALU.mult).then_inc(b.s("dv"), 1)
                v.tensor_tensor(pp, ga, ua, ALU.mult)
                v.drain()
                v.tensor_tensor(pp, pp, rep_sb, ALU.mult)
                v.drain()
                v.tensor_tensor(pp, pp, rep_sb, ALU.mult)
                v.drain()
                v.tensor_scalar(pp, pp, SM[:, 114:115], 1.0,
                                ALU.mult, ALU.mult)
                v.wait_ge(b.s("sc"), sg_sc[tt, oc])
                v.drain()
                v.tensor_tensor(gucol, pp, RB1[:, 1024:1536],
                                ALU.mult).then_inc(b.s("dv"), 1)
                v.drain()
                dst_a = (RB1[:, 3072:3584] if oc == 0
                         else RB1[:, 7168:7680])
                v.scalar_tensor_tensor(
                    dst_a, gucol, -1.0, gucol,
                    ALU.mult, ALU.max).then_inc(b.s("dv"), 1)
                if oc >= 1:
                    v.drain()
                    v.tensor_tensor(RB1[:, 3072:3584],
                                    RB1[:, 3072:3584],
                                    RB1[:, 7168:7680],
                                    ALU.max).then_inc(b.s("dv"), 1)
                    v.drain()
            v.wait_ge(b.s("gp"), q2_red_gp[tt])
            v.tensor_single_scalar(RB1[0:1, 4096:4608], ZBH[0:1, 0:512],
                                   1e-5, ALU.max)
            v.drain()
            v.tensor_single_scalar(
                RB1[0:1, 5120 + tt * 512:5632 + tt * 512],
                RB1[0:1, 4096:4608], 1.0 / 127.0, ALU.mult)
            v.reciprocal(RB1[0:1, 4608:5120], RB1[0:1, 4096:4608])
            v.drain()
            v.tensor_single_scalar(RB1[0:1, 4608:5120], RB1[0:1, 4608:5120],
                                   127.0, ALU.mult).then_inc(b.s("dv"), 1)
            v.wait_ge(b.s("pe"), q2_rep_pe[tt])
            for oc in range(32):
                gucol = XQL[:, oc * 512:oc * 512 + 512]
                v.tensor_tensor(gucol, gucol, PS[2][:, 0:512], ALU.mult)
                v.drain()
                _round5(v, gucol, RB1[:, ABSC:ABSC + 512], DORA)
                v.tensor_scalar(gucol, gucol, 127.0, -128.0, ALU.min,
                                ALU.max).then_inc(b.s("dv"), 1)
            for do in range(8):
                bank = PS[6 + do % 2]
                v.wait_ge(b.s("pe"), d_pe[tt, do])
                if do == 0:
                    v.drain()
                    v.tensor_copy(RB1[:, 1536:2048], PS[2][:, 0:512])
                v.drain()
                v.tensor_tensor(RB1[:, 2048:2560], bank[:, 0:512],
                                RB1[:, 1536:2048], ALU.mult)
                v.drain()
                v.tensor_scalar(RB1[:, 2048:2560], RB1[:, 2048:2560],
                                SM[:, 113:114], 1.0, ALU.mult, ALU.mult)
                v.drain()
                xsl = slice(do * T + tt * 512, do * T + tt * 512 + 512)
                v.tensor_tensor(X[:, xsl], X[:, xsl], RB1[:, 2048:2560],
                                ALU.add).then_inc(b.s("dv"), 1)
    em.emit(block)
    state["prev_wo_pe"] = d_pe[3, 7]
    state["prev_pair_done"] = b.cnts["dv"]
    return b.cnts["dv"]


def sub_l1_attn_b(b, block, ln_dv, state):
    X, RB1, RB2, ZBH = b.t["X"], b.t["RB1"], b.t["RB2"], b.t["ZBH"]
    XQH, XQL = b.t["XQH"], b.t["XQL"]
    WSH, SM = b.t["WSH"], b.t["SMALL"]
    ONESM, ID, TRIU = b.t["ONESM"], b.t["IDent"], b.t["TRIU"]
    PS = [b.t[f"PS{i}"] for i in range(8)]
    scr = b.t["scr_l1a"].rearrange("(h c d) -> h c d", h=2, c=1024)
    scro = b.t["scr_out"]
    em = E(b)
    blocks = []
    for j, nm in enumerate(["L1_attn_b_Wq", "L1_attn_b_Wk", "L1_attn_b_Wv",
                            "L1_attn_b_Wo"]):
        for dc in range(8):
            blocks.append((b.t[nm][dc * 128:dc * 128 + 128, :],
                           j * 1024 + dc * 128))
    prep_done = prep_T(b, block, state, blocks, scr)

    nw_tgt = b.inc("dq", 16)

    @block.sync
    def _(sync):
        sync.wait_ge(b.s("wsh0"), prep_done[0])
        sync.wait_ge(b.s("wsh1"), prep_done[1])
        sync.dma_start(out=SM[:, 118:119],
                       in_=b.t["L1_attn_b_norm_w"].rearrange(
                           "(c p) -> p c", p=128),
                       allow_slow_non_contiguous=True).then_inc(b.s("dq"), 16)

    QO, KO, VO = 0, 2048, 4096
    ac_base = b.cnts["sx0"]
    wld, proj_pe, evac_dv = {}, {}, {}
    tr_pe, trc_dv, sc_pe, msk_dv, io_pe = {}, {}, {}, {}, {}
    cs_pe, zc_dv, dn_pe, dnr_dv, rp_pe = {}, {}, {}, {}, {}
    o2_dv, sq_sc, var_pe, vs_sc, vr_dv, rs_pe = {}, {}, {}, {}, {}, {}
    o3_dv, sadd_pe, supd_dv, sz_dv, ost = {}, {}, {}, {}, {}
    for h in range(8):
        for j in range(3):
            wld[h, j] = b.inc("wsf%d" % (j % 2), 32)
    for h in range(8):
        for j in range(3):
            proj_pe[h, j] = b.inc("pe")
        for c in range(32):
            tr_pe[h, c] = b.inc("pe")
            sc_pe[h, c] = b.inc("pe")
            io_pe[h, c] = b.inc("pe")
            cs_pe[h, c] = b.inc("pe")
            dn_pe[h, c] = b.inc("pe")
            rp_pe[h, c] = b.inc("pe")
            var_pe[h, c] = b.inc("pe")
            rs_pe[h, c] = b.inc("pe")
            sadd_pe[h, c] = b.inc("pe")
    for h in range(8):
        for j in range(3):
            evac_dv[h, j] = b.inc("dv")
        sz_dv[h] = b.inc("dv", 2)
        for c in range(32):
            trc_dv[h, c] = b.inc("dv")
            msk_dv[h, c] = b.inc("dv")
            zc_dv[h, c] = b.inc("dv")
            dnr_dv[h, c] = b.inc("dv")
            o2_dv[h, c] = b.inc("dv")
            vr_dv[h, c] = b.inc("dv")
            o3_dv[h, c] = b.inc("dv")
            supd_dv[h, c] = b.inc("dv")
    for h in range(8):
        for c in range(32):
            sq_sc[h, c] = b.inc("sc")
            vs_sc[h, c] = b.inc("sc")
    for h in range(8):
        for c in range(32):
            ost[h, c] = b.inc("sx0", 16)

    @em.on("sync")
    def _(sync):
        for h in range(8):
            for j in range(3):
                slot = j % 2
                if h == 0 and j < 2:
                    sync.wait_ge(b.s("wsh0"), prep_done[0])
                    sync.wait_ge(b.s("wsh1"), prep_done[1])
                    if state.get("prev_wo_pe") is not None:
                        sync.wait_ge(b.s("pe"), state["prev_wo_pe"])
                else:
                    pj = (h, j - 2) if j >= 2 else (h - 1, j + 1)
                    sync.wait_ge(b.s("pe"), proj_pe[pj])
                for e in (0, 1):
                    sync.dma_start(
                        out=WSH[:, slot * 2048 + e * 1024:
                                slot * 2048 + e * 1024 + 1024].rearrange(
                            "p (c d) -> p c d", c=8),
                        in_=scr[e, :, j * 1024 + h * 128:
                                j * 1024 + h * 128 + 128].rearrange(
                            "(c p) d -> p c d", p=128),
                    ).then_inc(b.s("wsf%d" % slot), 16)
            for c in range(32):
                sync.wait_ge(b.s("dv"), o3_dv[h, c])
                if (h, c) > (0, 0):
                    ph, pc = (h, c - 1) if c >= 1 else (h - 1, 31)
                    sync.wait_ge(b.s("sx0"), ost[ph, pc])
                sync.dma_start(
                    out=scro[h * 128:h * 128 + 128, c * 64:c * 64 + 64],
                    in_=RB2[:, 640:704]).then_inc(b.s("sx0"), 16)

    @em.on("tensor")
    def _(tensor):
        for h in range(8):
            for j in range(3):
                slot = j % 2
                tensor.wait_ge(b.s("wsf%d" % slot), wld[h, j])
                if h == 0 and j == 0:
                    tensor.wait_ge(b.s("dv"), ln_dv)
                if (h, j) > (0, 0):
                    pj = (h, j - 1) if j >= 1 else (h - 1, 2)
                    tensor.wait_ge(b.s("dv"), evac_dv[pj])
                for q in range(4):
                    for c in range(NCH):
                        wh = WSH[:, slot * 2048 + c * 128:
                                 slot * 2048 + c * 128 + 128]
                        wl = WSH[:, slot * 2048 + 1024 + c * 128:
                                 slot * 2048 + 1024 + c * 128 + 128]
                        xsl = slice(c * T + q * 512, c * T + q * 512 + 512)
                        tensor.matmul(PS[q][:, 0:512], wh, XQH[:, xsl],
                                      start=(c == 0), stop=False)
                        tensor.matmul(PS[q][:, 0:512], wh, XQL[:, xsl],
                                      start=False, stop=False)
                        mm = tensor.matmul(PS[q][:, 0:512], wl, XQH[:, xsl],
                                           start=False, stop=(c == NCH - 1))
                mm.then_inc(b.s("pe"), 1)
            for c in range(32):
                if c == 0:
                    tensor.wait_ge(b.s("dv"), evac_dv[h, 2])
                    tensor.wait_ge(b.s("dv"), sz_dv[h])
                if c >= 1:
                    tensor.wait_ge(b.s("dv"), trc_dv[h, c - 1])
                    tensor.wait_ge(b.s("dv"), dnr_dv[h, c - 1])
                tensor.transpose(PS[4][0:64, 0:128],
                                 RB1[:, KO + c * 64:KO + c * 64 + 64],
                                 ID[:, :])
                tensor.transpose(PS[4][0:64, 128:256],
                                 RB1[:, VO + c * 64:VO + c * 64 + 64],
                                 ID[:, :]).then_inc(b.s("pe"), 1)
                if c >= 1:
                    tensor.wait_ge(b.s("dv"), msk_dv[h, c - 1])
                tensor.matmul(PS[0][0:64, 0:64],
                              RB1[:, KO + c * 64:KO + c * 64 + 64],
                              RB1[:, QO + c * 64:QO + c * 64 + 64],
                              start=True, stop=True).then_inc(b.s("pe"), 1)
                tensor.wait_ge(b.s("dv"), msk_dv[h, c])
                tensor.wait_ge(b.s("dv"), trc_dv[h, c])
                if c >= 1:
                    tensor.wait_ge(b.s("dv"), o2_dv[h, c - 1])
                    tensor.wait_ge(b.s("dv"), supd_dv[h, c - 1])
                tensor.matmul(PS[1][:, 0:64], RB2[0:64, 128:256],
                              RB2[0:64, 384:448], start=True, stop=True)
                tensor.matmul(PS[2][:, 0:64], RB2[:, 256:384],
                              RB1[:, QO + c * 64:QO + c * 64 + 64],
                              start=True, stop=True).then_inc(b.s("pe"), 1)
                if c >= 1:
                    tensor.wait_ge(b.s("dv"), zc_dv[h, c - 1])
                tensor.matmul(PS[3][:, 0:64], RB2[0:64, 0:128],
                              TRIU[:, :], start=True,
                              stop=True).then_inc(b.s("pe"), 1)
                tensor.wait_ge(b.s("dv"), zc_dv[h, c])
                tensor.matmul(PS[7][0:1, 0:64], ONESM[:, 0:1],
                              RB2[:, 448:512], start=True,
                              stop=True).then_inc(b.s("pe"), 1)
                tensor.wait_ge(b.s("dv"), dnr_dv[h, c])
                tensor.matmul(PS[5][:, 0:64], ONESM[0:1, 0:128],
                              RB2[0:1, 704:768], start=True,
                              stop=True).then_inc(b.s("pe"), 1)
                tensor.wait_ge(b.s("sc"), sq_sc[h, c])
                if c >= 1:
                    tensor.wait_ge(b.s("sc"), vs_sc[h, c - 1])
                tensor.matmul(PS[7][0:1, 64:128], ONESM[:, 0:1],
                              RB2[:, 832:896], start=True,
                              stop=True).then_inc(b.s("pe"), 1)
                tensor.wait_ge(b.s("dv"), vr_dv[h, c])
                tensor.matmul(PS[5][:, 64:128], ONESM[0:1, 0:128],
                              RB2[0:1, 768:832], start=True,
                              stop=True).then_inc(b.s("pe"), 1)
                tensor.wait_ge(b.s("dv"), o3_dv[h, c])
                tensor.matmul(PS[6][:, 0:128], RB2[0:64, 0:128],
                              RB2[0:64, 128:256], start=True,
                              stop=True).then_inc(b.s("pe"), 1)


    @em.on("vector")
    def _(v):
        for h in range(8):
            for j in range(3):
                v.wait_ge(b.s("pe"), proj_pe[h, j])
                base = (QO, KO, VO)[j]
                for q in range(4):
                    ins = v.tensor_copy(
                        RB1[:, base + q * 512:base + q * 512 + 512],
                        PS[q][:, 0:512])
                ins.then_inc(b.s("dv"), 1)
            if h == 0:
                v.wait_ge(b.s("dq"), nw_tgt)
            v.memset(RB2[:, 256:384], 0.0).then_inc(b.s("dv"), 1)
            v.memset(RB2[:, 512:513], 0.0).then_inc(b.s("dv"), 1)
            for c in range(32):
                v.wait_ge(b.s("pe"), tr_pe[h, c])
                v.tensor_copy(RB2[0:64, 0:128], PS[4][0:64, 0:128])
                ins = v.tensor_copy(RB2[0:64, 128:256], PS[4][0:64, 128:256])
                ins.then_inc(b.s("dv"), 1)
                v.wait_ge(b.s("pe"), sc_pe[h, c])
                v.tensor_tensor(RB2[0:64, 384:448], PS[0][0:64, 0:64],
                                TRIU[:, :], ALU.mult).then_inc(b.s("dv"), 1)
                v.wait_ge(b.s("pe"), cs_pe[h, c])
                v.tensor_scalar(RB2[:, 448:512], PS[3][:, 0:64],
                                RB2[:, 512:513], 0.0, ALU.add, ALU.add)
                v.drain()
                v.tensor_copy(RB2[:, 512:513], RB2[:, 511:512])
                v.drain()
                v.tensor_tensor(RB2[:, 448:512], RB2[:, 448:512],
                                RB1[:, QO + c * 64:QO + c * 64 + 64],
                                ALU.mult).then_inc(b.s("dv"), 1)
                v.wait_ge(b.s("pe"), dn_pe[h, c])
                v.tensor_copy(RB2[0:1, 960:1024], PS[7][0:1, 0:64])
                v.drain()
                v.scalar_tensor_tensor(RB2[0:1, 704:768],
                                       RB2[0:1, 960:1024], -1.0,
                                       RB2[0:1, 960:1024],
                                       ALU.mult, ALU.max)
                v.drain()
                v.tensor_single_scalar(RB2[0:1, 704:768], RB2[0:1, 704:768],
                                       1.0, ALU.max)
                v.drain()
                v.reciprocal(RB2[0:1, 704:768],
                             RB2[0:1, 704:768]).then_inc(b.s("dv"), 1)
                v.wait_ge(b.s("pe"), io_pe[h, c])
                v.wait_ge(b.s("pe"), rp_pe[h, c])
                v.tensor_copy(RB2[:, 896:960], PS[2][:, 0:64])
                v.drain()
                v.tensor_tensor(RB2[:, 576:640], PS[1][:, 0:64],
                                RB2[:, 896:960], ALU.add)
                v.drain()
                v.tensor_tensor(RB2[:, 576:640], RB2[:, 576:640],
                                PS[5][:, 0:64], ALU.mult,
                                ).then_inc(b.s("dv"), 1)
                v.wait_ge(b.s("sc"), vs_sc[h, c])
                v.reciprocal(RB2[0:1, 768:832],
                             RB2[0:1, 768:832]).then_inc(b.s("dv"), 1)
                v.wait_ge(b.s("pe"), rs_pe[h, c])
                if c >= 1 or h >= 1:
                    ph, pc = (h, c - 1) if c >= 1 else (h - 1, 31)
                    v.wait_ge(b.s("sx0"), ost[ph, pc])
                v.tensor_tensor(RB2[:, 640:704], RB2[:, 576:640],
                                PS[5][:, 64:128], ALU.mult)
                v.drain()
                v.tensor_scalar(RB2[:, 640:704], RB2[:, 640:704],
                                SM[:, 118:119], 1.0, ALU.mult, ALU.mult,
                                ).then_inc(b.s("dv"), 1)
                v.wait_ge(b.s("pe"), sadd_pe[h, c])
                v.tensor_tensor(RB2[:, 256:384], RB2[:, 256:384],
                                PS[6][:, 0:128],
                                ALU.add).then_inc(b.s("dv"), 1)

    @em.on("scalar")
    def _(scalar):
        for h in range(8):
            for c in range(32):
                scalar.wait_ge(b.s("dv"), o2_dv[h, c])
                scalar.activation(RB2[:, 832:896], RB2[:, 576:640],
                                  ACT.Square).then_inc(b.s("sc"), 1)
                scalar.wait_ge(b.s("pe"), var_pe[h, c])
                scalar.activation(RB2[0:1, 768:832], PS[7][0:1, 64:128],
                                  ACT.Sqrt, bias=1e-5,
                                  scale=1.0 / 128.0).then_inc(b.s("sc"), 1)
    em.emit(block)

    em2 = E(b)
    ofin = b.cnts["sx0"]
    old_, osp_dv, wo_ld, woec_pe, wo_dv = {}, {}, {}, {}, {}
    for tcol in range(4):
        for ec in range(8):
            old_[tcol, ec] = b.inc("dx%d" % (ec % 2), 16)
    for tcol in range(4):
        for ec in range(8):
            wo_ld[tcol, ec] = b.inc("wsf%d" % (ec % 2), 32)
    for tcol in range(4):
        for ec in range(8):
            woec_pe[tcol, ec] = b.inc("pe")
    for tcol in range(4):
        for ec in range(8):
            osp_dv[tcol, ec] = b.inc("dv")
        for do in range(8):
            wo_dv[tcol, do] = b.inc("dv")

    @em2.on("sync")
    def _(sync):
        for tcol in range(4):
            for ec in range(8):
                if tcol == 0 and ec == 0:
                    sync.wait_ge(b.s("sx0"), ofin)
                if (tcol, ec) > (0, 0):
                    pt = (tcol, ec - 1) if ec else (tcol - 1, 7)
                    sync.wait_ge(b.s("dv"), osp_dv[pt])
                sync.dma_start(
                    out=RB1[:, 0:512],
                    in_=scro[ec * 128:ec * 128 + 128,
                             tcol * 512:tcol * 512 + 512],
                ).then_inc(b.s("dx%d" % (ec % 2)), 16)
                slot = ec % 2
                pw = (tcol, ec - 2) if ec >= 2 else (tcol - 1, 6 + ec)
                if pw[0] >= 0:
                    sync.wait_ge(b.s("pe"), woec_pe[pw])
                sync.dma_start(
                    out=WSH[:, slot * 2048:slot * 2048 + 1024],
                    in_=scr[0, ec * 128:ec * 128 + 128, 3072:4096],
                ).then_inc(b.s("wsf%d" % slot), 16)
                sync.dma_start(
                    out=WSH[:, slot * 2048 + 1024:slot * 2048 + 2048],
                    in_=scr[1, ec * 128:ec * 128 + 128, 3072:4096],
                ).then_inc(b.s("wsf%d" % slot), 16)

    @em2.on("vector")
    def _(v):
        for tcol in range(4):
            for ec in range(8):
                v.wait_ge(b.s("dx%d" % (ec % 2)), old_[tcol, ec])
                if (tcol, ec) > (0, 0):
                    pt = (tcol, ec - 1) if ec else (tcol - 1, 7)
                    v.wait_ge(b.s("pe"), woec_pe[pt])
                v.tensor_copy(ZBH[:, 0:512], RB1[:, 0:512])
                v.drain()
                ins = v.tensor_tensor(ZBH[:, 512:1024], RB1[:, 0:512],
                                      ZBH[:, 0:512], ALU.subtract)
                ins.then_inc(b.s("dv"), 1)
            for do in range(8):
                v.wait_ge(b.s("pe"), woec_pe[tcol, 7])
                v.drain()
                v.tensor_copy(RB2[:, 0:512], PS[do][:, 0:512])
                v.drain()
                xsl = slice(do * T + tcol * 512, do * T + tcol * 512 + 512)
                v.tensor_tensor(X[:, xsl], X[:, xsl], RB2[:, 0:512],
                                ALU.add).then_inc(b.s("dv"), 1)

    @em2.on("tensor")
    def _(tensor):
        for tcol in range(4):
            for ec in range(8):
                slot = ec % 2
                tensor.wait_ge(b.s("dv"), osp_dv[tcol, ec])
                tensor.wait_ge(b.s("wsf%d" % slot), wo_ld[tcol, ec])
                if ec == 0 and tcol >= 1:
                    tensor.wait_ge(b.s("dv"), wo_dv[tcol - 1, 7])
                for do in range(8):
                    woh = WSH[:, slot * 2048 + do * 128:
                              slot * 2048 + do * 128 + 128]
                    wol = WSH[:, slot * 2048 + 1024 + do * 128:
                              slot * 2048 + 1024 + do * 128 + 128]
                    tensor.matmul(PS[do][:, 0:512], woh, ZBH[:, 0:512],
                                  start=(ec == 0), stop=False)
                    tensor.matmul(PS[do][:, 0:512], woh, ZBH[:, 512:1024],
                                  start=False, stop=False)
                    mm = tensor.matmul(PS[do][:, 0:512], wol, ZBH[:, 0:512],
                                       start=False, stop=(ec == 7))
                mm.then_inc(b.s("pe"), 1)
    em2.emit(block)
    state["prev_wo_pe"] = woec_pe[3, 7]
    state["prev_pair_done"] = b.cnts["dv"]
    return b.cnts["dv"]


WSHAPES = {
    "L0_attn_a_w_q": [16, 1024, 64], "L0_attn_a_w_k": [16, 1024, 64],
    "L0_attn_a_w_v": [16, 1024, 64], "L0_attn_a_w_o": [1024, 1024],
    "L0_attn_b_w_qkv": [1024, 3072], "L0_attn_b_w_out": [1024, 1024],
    "L0_attn_b_b_out": [1024], "L0_attn_b_ln_gamma": [1024],
    "L0_attn_b_ln_beta": [1024],
    "L0_ffn_W_r": [1024, 1024], "L0_ffn_W_k": [1024, 1024],
    "L0_ffn_W_v": [1024, 1024], "L0_ffn_mu_r": [1024], "L0_ffn_mu_k": [1024],
    "L1_attn_a_head_mask": [16], "L1_attn_a_qkv_w": [3072, 1024],
    "L1_attn_a_qkv_b": [3072], "L1_attn_a_fc_w": [1024, 1024],
    "L1_attn_a_fc_b": [1024],
    "L1_attn_b_Wq": [1024, 1024], "L1_attn_b_Wk": [1024, 1024],
    "L1_attn_b_Wv": [1024, 1024], "L1_attn_b_Wo": [1024, 1024],
    "L1_attn_b_norm_w": [128],
    "L1_ffn_w_gate": [4096, 1024], "L1_ffn_w_up": [4096, 1024],
    "L1_ffn_w_down": [1024, 4096],
}


def sub_l0_attn_a(b, block, ln_dv, state):
    wq = b.t["L0_attn_a_w_q"].rearrange("h (c p) k -> h p c k", p=128)
    wk = b.t["L0_attn_a_w_k"].rearrange("h (c p) k -> h p c k", p=128)
    wv = b.t["L0_attn_a_w_v"].rearrange("h (c p) k -> h p c k", p=128)
    wo = b.t["L0_attn_a_w_o"]
    cfg = {
        "wq_ap": lambda hp, e: wq[2 * hp + e],
        "wk_ap": lambda hp, e: wk[2 * hp + e],
        "wv_ap": lambda hp, e: wv[2 * hp + e],
        "wo_ap": lambda hp, e: wo[(2 * hp + e) * 64:(2 * hp + e) * 64 + 64, :],
        "scale": 0.125,
        "internal_residual": True,
        "ln_dv": ln_dv,
    }
    return attn64(b, block, cfg, state)


def build_program(n_sub=None):
    if n_sub is None:
        n_sub = N_SUB
    b = B()
    b.dram("x_in", [T, D])
    b.dram("out", [T, D], kind="ExternalOutput")
    b.dram("x_ident", [128, 128])
    b.dram("x_onesm", [128, 128])
    b.dram("x_triu", [64, 64])
    b.dram("x_selh", [128, 2], BF16)
    b.dram("x_fconst", [1, 8])
    b.dram("x_sel32", [128, 4], BF16)
    b.dram("x_pick32", [4, 512], BF16)
    b.dram("x_selh2", [2, 128])
    for k, v in WSHAPES.items():
        b.dram(k, v)
    b.dram("scr_gT", [1024 * 4096], BF16, kind="Internal")
    b.dram("scr_uT", [1024 * 4096], BF16, kind="Internal")
    b.dram("scr_dT", [4096 * 1024], BF16, kind="Internal")
    b.dram("scr_kv", [1 << 21], F32, kind="Internal")
    b.dram("scr_ffn", [3 * 2 * 1024 * 1024], BF16, kind="Internal")
    b.dram("scr_out", [1024, 2048], F32, kind="Internal")
    b.dram("scr_l1a", [2 * 1024 * 4096], BF16, kind="Internal")

    b.sb("X", [128, NCH * T], F32)
    b.sb("XQH", [128, NCH * T], BF16)
    b.sb("XQL", [128, NCH * T], BF16)
    b.sb("RB1", [128, 8192], F32)
    b.sb("RB2", [128, 2048], F32)
    b.sb("RBH1", [128, 4096], BF16)
    b.sb("RBH2", [128, 4096], BF16)
    b.sb("WSF", [128, 2048], F32)
    b.sb("WSH", [128, 4096], BF16)
    b.sb("ZBH", [128, 2048], BF16)
    b.sb("SMALL", [128, 120], F32)
    b.sb("IDent", [128, 128], F32)
    b.sb("ONESM", [128, 128], F32)
    b.sb("TRIU", [64, 64], F32)
    b.sb("SELH", [128, 2], BF16)
    b.sb("SEL32", [128, 4], BF16)
    b.sb("PICK32", [4, 512], BF16)
    b.sb("NEGCH", [4, 1], BF16)
    b.sb("SELH2", [2, 128], F32)
    b.sb("DORA", [128, 1], F32)
    b.sb("FCONST", [1, 8], F32)
    b.nc.const_aps.register(F32, 1e-5, b.t["FCONST"][0:1, 0:1])
    b.nc.const_aps.register(F32, 1e-6, b.t["FCONST"][0:1, 1:2])

    for i in range(8):
        b.ps_(f"PS{i}", [128, 512])

    for s in ["dq", "gq", "pe", "dv", "sc", "gp", "st", "wf",
              "dx0", "dx1", "wsf0", "wsf1", "wsh0", "wsh1", "sx0", "sx1",
              "ac0", "ac1"]:
        b.new_sem(s)

    with b.nc.Block() as block:
        gstate = {}
        phase_load_x(b, block)
        if n_sub >= 1:
            ln_dv = phase_ln(b, block)
            sub_l0_attn_a(b, block, ln_dv, gstate)
        if n_sub >= 2:
            ln_dv = phase_ln(b, block)
            sub_l0_ffn(b, block, ln_dv, gstate)
        if n_sub >= 3:
            ln_dv = phase_ln(b, block)
            sub_l0_attn_b(b, block, ln_dv, gstate)
        if n_sub >= 4:
            ln_dv = phase_ln(b, block)
            sub_l0_ffn(b, block, ln_dv, gstate)
        if n_sub >= 5:
            ln_dv = phase_ln(b, block)
            sub_l1_attn_a(b, block, ln_dv, gstate)
        if n_sub >= 6:
            ln_dv = phase_ln(b, block)
            sub_l1_ffn(b, block, ln_dv, gstate)
        if n_sub >= 7:
            ln_dv = phase_ln(b, block)
            sub_l1_attn_b(b, block, ln_dv, gstate)
        if n_sub >= 8:
            ln_dv = phase_ln(b, block)
            sub_l1_ffn(b, block, ln_dv, gstate)
        phase_store_x(b, block)
    return b.nc


_CONSTS = None


def kernel(**inputs):
    global _CONSTS
    if _CONSTS is None:
        _CONSTS = make_consts()
    x = np.ascontiguousarray(np.asarray(inputs["L0_attn_a_x"], dtype=np.float32))
    nc = build_program()
    base = dict(_CONSTS)
    for k in WSHAPES:
        base[k] = np.ascontiguousarray(np.asarray(inputs[k], dtype=np.float32))
    in_maps = []
    for i in range(8):
        m = dict(base)
        m["x_in"] = x[i]
        in_maps.append(m)
    res = run_bass_kernel_spmd(nc, in_maps, core_ids=list(range(8)))
    return np.stack([r["out"] for r in res.results], axis=0)
